# version 02
# baseline (speedup 1.0000x reference)
import math
import contextlib
import numpy as np
import ml_dtypes
import concourse.bass as bass
import concourse.mybir as mybir
from concourse.bass_utils import run_bass_kernel_spmd

F32 = mybir.dt.float32
BF16 = mybir.dt.bfloat16
AF = mybir.ActivationFunctionType
ALU = mybir.AluOpType
AX = mybir.AxisListType
BF = ml_dtypes.bfloat16

NCORES = 8
D = 2048
DC = D // 128
T = 2048
TT = 512
NTT = T // TT


def tile_w(w):
    K, M = w.shape
    assert K % 128 == 0
    KC = K // 128
    MC = (M + 127) // 128
    if M % 128:
        w = np.concatenate([w, np.zeros((K, MC * 128 - M), w.dtype)], axis=1)
    return np.ascontiguousarray(
        w.reshape(KC, 128, MC, 128).transpose(2, 1, 0, 3).reshape(MC, 128, KC * 128)
    )


def fm_chunks(x):
    t, d = x.shape
    return np.ascontiguousarray(x.T.reshape(d // 128, 128, t))


def fm_to_tokmajor(xc):
    dc, p, t = xc.shape
    return np.ascontiguousarray(xc.reshape(dc * p, t).T)


def hadamard(n):
    h = np.array([[1.0]], np.float32)
    while h.shape[0] < n:
        h = np.block([[h, h], [h, -h]])
    return h



_UNIQ = [0]


def uq(name):
    _UNIQ[0] += 1
    return f"{name}_{_UNIQ[0]}"


class S_:

    def __init__(self, nc, es, name):
        self.h = es.enter_context(nc.semaphore(name))
        self.n = 0

    def inc(self, instr, k=1):
        instr.then_inc(self.h, k)
        self.n += k
        return instr

    def wait(self, eng, n=None):
        eng.wait_ge(self.h, self.n if n is None else n)


def emit_prep(nc, es, sm, w2t, h128, h64, w2ps, t1d):
    h128b = es.enter_context(nc.sbuf_tensor(uq("h128b"), [128, 128], BF16))
    h64b = es.enter_context(nc.sbuf_tensor(uq("h64b"), [64, 64], BF16))
    blk = es.enter_context(nc.sbuf_tensor(uq("blk"), [128, 2048], BF16))
    ev1 = es.enter_context(nc.sbuf_tensor(uq("ev1"), [128, 2048], F32))
    rhs2 = es.enter_context(nc.sbuf_tensor(uq("rhs2"), [64, 512], BF16))
    ev2 = es.enter_context(nc.sbuf_tensor(uq("ev2"), [64, 512], BF16))
    ps1 = es.enter_context(nc.psum_tensor(uq("ps1"), [128, 512], F32))
    ps2 = es.enter_context(nc.psum_tensor(uq("ps2"), [64, 512], F32))
    if True:
        gp, pe, ac, sy = sm["gp"], sm["pe"], sm["ac"], sm["sy"]

        nc.gpsimd.dma_start(out=h128b[:, :], in_=h128[:, :]).then_inc(gp.h, 16)
        gp.n += 16
        nc.gpsimd.dma_start(out=h64b[:, :], in_=h64[:, :]).then_inc(gp.h, 16)
        gp.n += 16

        for i1 in range(64):
            gp.inc(nc.gpsimd.dma_start(
                out=blk[:, :], in_=w2t[i1 * 128:(i1 + 1) * 128, :]), 16)
            nc.tensor.wait_ge(gp.h, gp.n)
            for dt in range(4):
                nc.tensor.wait_ge(ac.h, ac.n)
                pe.inc(nc.tensor.matmul(
                    ps1[:, :], h128b[:, :], blk[:, dt * 512:(dt + 1) * 512],
                    start=True, stop=True))
                nc.scalar.wait_ge(pe.h, pe.n)
                ac.inc(nc.scalar.copy(ev1[:, dt * 512:(dt + 1) * 512], ps1[:, :]))
            nc.sync.wait_ge(ac.h, ac.n)
            sy.inc(nc.sync.dma_start(out=t1d[i1, :, :], in_=ev1[:, :]), 16)
            nc.gpsimd.wait_ge(pe.h, pe.n)
            nc.scalar.wait_ge(sy.h, sy.n)

        scale = 1.0 / math.sqrt(8192.0)
        nc.gpsimd.wait_ge(sy.h, sy.n)
        for j2 in range(128):
            for dt in range(4):
                gp.inc(nc.gpsimd.dma_start(
                    out=rhs2[:, :],
                    in_=t1d[:, j2, dt * 512:(dt + 1) * 512]), 16)
                nc.tensor.wait_ge(gp.h, gp.n)
                nc.tensor.wait_ge(ac.h, ac.n)
                pe.inc(nc.tensor.matmul(ps2[:, :], h64b[:, :], rhs2[:, :],
                                        start=True, stop=True))
                nc.scalar.wait_ge(pe.h, pe.n)
                ac.inc(nc.scalar.activation(ev2[:, :], ps2[:, :], AF.Copy,
                                            scale=scale))
                nc.sync.wait_ge(ac.h, ac.n)
                dst = w2ps[dt * 4:(dt + 1) * 4, j2, :].rearrange(
                    "dcc (j1 q) -> j1 dcc q", q=128)
                src = ev2[:, :].rearrange("j1 (dcc q) -> j1 dcc q", q=128)
                sy.inc(nc.sync.dma_start(out=dst, in_=src), 16)
                nc.gpsimd.wait_ge(pe.h, pe.n)
                nc.scalar.wait_ge(sy.h, sy.n)
        nc.sync.wait_ge(sy.h, sy.n)
        nc.vector.wait_ge(sy.h, sy.n)


def make_sems(nc, es):
    sm = {}
    for name in ["sy", "gp", "pe", "ac", "ve", "gv", "st", "vst", "qst", "qld",
                 "cld", "bnc"]:
        sm[name] = S_(nc, es, name)
    for name in ["wsl", "w2sl", "stsl", "dsl", "osl", "vsl", "xsl", "psl", "gsl"]:
        sm[name] = [S_(nc, es, f"{name}{i}") for i in range(2)]
    return sm


def build_prep():
    nc = bass.Bass("TRN2", target_bir_lowering=False, debug=False)
    w2t = nc.dram_tensor("w2t", [8192, 2048], F32, kind="ExternalInput").ap()
    h128 = nc.dram_tensor("h128", [128, 128], F32, kind="ExternalInput").ap()
    h64 = nc.dram_tensor("h64", [64, 64], F32, kind="ExternalInput").ap()
    w2ps = nc.dram_tensor("w2ps", [16, 128, 8192], BF16, kind="ExternalOutput").ap()
    t1d = nc.dram_tensor("t1d", [64, 128, 2048], F32).ap()
    es = contextlib.ExitStack()
    with es:
        sm = make_sems(nc, es)
        emit_prep(nc, es, sm, w2t, h128, h64, w2ps, t1d)
    return nc


def emit_ffn(nc, es, sm, xfm, w1s, w3s, w2ps, onesc, xo):
    ones = es.enter_context(nc.sbuf_tensor(uq("ones"), [128, 128], BF16))
    xt = es.enter_context(nc.sbuf_tensor(uq("xt"), [128, DC * TT], F32))
    xsq = es.enter_context(nc.sbuf_tensor(uq("xsq"), [128, DC * TT], BF16))
    xln = es.enter_context(nc.sbuf_tensor(uq("xln"), [128, DC * TT], BF16))
    w1b = [es.enter_context(nc.sbuf_tensor(uq(f"w1b{i}"), [128, 2048], BF16)) for i in range(2)]
    w3b = [es.enter_context(nc.sbuf_tensor(uq(f"w3b{i}"), [128, 2048], BF16)) for i in range(2)]
    w2b = [es.enter_context(nc.sbuf_tensor(uq(f"w2b{i}"), [128, 8192], BF16)) for i in range(2)]
    hid = es.enter_context(nc.sbuf_tensor(uq("hid"), [128, 64 * TT], BF16))
    gtmp = es.enter_context(nc.sbuf_tensor(uq("gtmp"), [128, TT], BF16))
    rows = es.enter_context(nc.sbuf_tensor(uq("rows"), [1, TT * 5], F32))
    rowsb = es.enter_context(nc.sbuf_tensor(uq("rowsb"), [1, TT * 2], BF16))
    mb = es.enter_context(nc.sbuf_tensor(uq("mb"), [128, TT], F32))
    rb = es.enter_context(nc.sbuf_tensor(uq("rb"), [128, TT], F32))
    t32 = [es.enter_context(nc.sbuf_tensor(uq(f"t32_{i}"), [128, TT], F32)) for i in range(2)]
    xot = [es.enter_context(nc.sbuf_tensor(uq(f"xot{i}"), [128, TT], F32)) for i in range(2)]

    psG = es.enter_context(nc.psum_tensor(uq("psG"), [128, TT], F32))
    psU = es.enter_context(nc.psum_tensor(uq("psU"), [128, TT], F32))
    psO = es.enter_context(nc.psum_tensor(uq("psO"), [128, TT], F32))
    psR1 = es.enter_context(nc.psum_tensor(uq("psR1"), [1, TT], F32))
    psR2 = es.enter_context(nc.psum_tensor(uq("psR2"), [1, TT], F32))

    if True:
        sy, gp, pe, ac, ve, gv = (sm["sy"], sm["gp"], sm["pe"], sm["ac"],
                                  sm["ve"], sm["gv"])
        wsl, w2sl, stsl = sm["wsl"], sm["w2sl"], sm["stsl"]

        gp.inc(nc.gpsimd.dma_start(out=ones[:, :], in_=onesc[:, :]), 16)

        for tt in range(NTT):
            if tt > 0:
                nc.sync.wait_ge(ve.h, ve.n)
            for dc in range(DC):
                sy.inc(nc.sync.dma_start(
                    out=xt[:, dc * TT:(dc + 1) * TT],
                    in_=xfm[dc, :, tt * TT:(tt + 1) * TT]), 16)
            nc.vector.wait_ge(sy.h, sy.n)
            ve.inc(nc.vector.tensor_copy(xln[:, :], xt[:, :]))
            nc.scalar.wait_ge(sy.h, sy.n)
            ac.inc(nc.scalar.activation(xsq[:, :], xt[:, :], AF.Square))
            nc.tensor.wait_ge(ve.h, ve.n)
            nc.tensor.wait_ge(ac.h, ac.n)
            nc.tensor.wait_ge(gp.h, 16)
            for dc in range(DC):
                nc.tensor.matmul(psR1[:, :], ones[:, 0:1],
                                 xln[:, dc * TT:(dc + 1) * TT],
                                 start=(dc == 0), stop=(dc == DC - 1))
            for dc in range(DC):
                ins = nc.tensor.matmul(psR2[:, :], ones[:, 0:1],
                                       xsq[:, dc * TT:(dc + 1) * TT],
                                       start=(dc == 0), stop=(dc == DC - 1))
            pe.inc(ins)
            nc.vector.wait_ge(pe.h, pe.n)
            r_mean = rows[:, 0:TT]
            r_msq = rows[:, TT:2 * TT]
            r_var = rows[:, 2 * TT:3 * TT]
            r_std = rows[:, 3 * TT:4 * TT]
            r_rstd = rows[:, 4 * TT:5 * TT]
            r_mbr = rows[:, 3 * TT:4 * TT]

            def vch(ins):
                ve.inc(ins)
                nc.vector.wait_ge(ve.h, ve.n)
            vch(nc.vector.tensor_scalar_mul(r_mean, psR1[:, :], 1.0 / D))
            vch(nc.vector.tensor_scalar_mul(r_msq, psR2[:, :], 1.0 / D))
            vch(nc.vector.tensor_mul(r_var, r_mean, r_mean))
            vch(nc.vector.tensor_sub(r_var, r_msq, r_var))
            ve.inc(nc.vector.tensor_scalar_add(r_var, r_var, 1e-5))
            nc.scalar.wait_ge(ve.h, ve.n)
            ac.inc(nc.scalar.activation(r_std, r_var, AF.Sqrt))
            nc.vector.wait_ge(ac.h, ac.n)
            vch(nc.vector.reciprocal(r_rstd, r_std))
            vch(nc.vector.tensor_copy(rowsb[:, 0:TT], r_rstd))
            vch(nc.vector.tensor_mul(r_mbr, r_mean, r_rstd))
            ve.inc(nc.vector.tensor_copy(rowsb[:, TT:2 * TT], r_mbr))
            nc.tensor.wait_ge(ve.h, ve.n)
            pe_snap = pe.n
            pe.inc(nc.tensor.matmul(psG[:, :], ones[0:1, :], rowsb[:, 0:TT],
                                    start=True, stop=True))
            pe.inc(nc.tensor.matmul(psU[:, :], ones[0:1, :], rowsb[:, TT:2 * TT],
                                    start=True, stop=True))
            nc.scalar.wait_ge(pe.h, pe_snap + 1)
            ac.inc(nc.scalar.copy(rb[:, :], psG[:, :]))
            nc.scalar.wait_ge(pe.h, pe_snap + 2)
            ac.inc(nc.scalar.copy(mb[:, :], psU[:, :]))
            nc.vector.wait_ge(ac.h, ac.n)
            nc.gpsimd.wait_ge(ac.h, ac.n)
            gv_base = gv.n
            ve_ap_base = ve.n
            for dc in range(DC):
                if dc >= 2:
                    nc.vector.wait_ge(gv.h, gv_base + dc - 1)
                ve.inc(nc.vector.tensor_mul(t32[dc % 2][:, :],
                                            xt[:, dc * TT:(dc + 1) * TT], rb[:, :]))
                nc.gpsimd.wait_ge(ve.h, ve_ap_base + dc + 1)
                gv.inc(nc.gpsimd.tensor_sub(xln[:, dc * TT:(dc + 1) * TT],
                                            t32[dc % 2][:, :], mb[:, :]))
            gv_xln = gv.n

            pe_gu_base = pe.n
            wsl_base = [wsl[0].n, wsl[1].n]
            for hc in range(64):
                if hc >= 2:
                    nc.gpsimd.wait_ge(pe.h, pe_gu_base + 2 * hc - 2)
                wsl[hc % 2].inc(nc.gpsimd.dma_start(out=w1b[hc % 2][:, :], in_=w1s[hc, :, :]), 16)
                wsl[hc % 2].inc(nc.gpsimd.dma_start(out=w3b[hc % 2][:, :], in_=w3s[hc, :, :]), 16)
            ac_base = ac.n
            ve_base = ve.n
            for hc in range(64):
                nc.tensor.wait_ge(wsl[hc % 2].h, wsl_base[hc % 2] + 32 * (hc // 2 + 1))
                if hc == 0:
                    nc.tensor.wait_ge(gv.h, gv_xln)
                else:
                    nc.tensor.wait_ge(ac.h, ac_base + hc)
                    nc.tensor.wait_ge(ve.h, ve_base + hc)
                for dc in range(DC):
                    ins = nc.tensor.matmul(psG[:, :], w1b[hc % 2][:, dc * 128:(dc + 1) * 128],
                                           xln[:, dc * TT:(dc + 1) * TT],
                                           start=(dc == 0), stop=(dc == DC - 1))
                pe.inc(ins)
                for dc in range(DC):
                    ins = nc.tensor.matmul(psU[:, :], w3b[hc % 2][:, dc * 128:(dc + 1) * 128],
                                           xln[:, dc * TT:(dc + 1) * TT],
                                           start=(dc == 0), stop=(dc == DC - 1))
                pe.inc(ins)
                nc.scalar.wait_ge(pe.h, pe_gu_base + 2 * hc + 1)
                nc.scalar.wait_ge(ve.h, ve_base + hc)
                ac.inc(nc.scalar.activation(gtmp[:, :], psG[:, :], AF.Silu))
                nc.vector.wait_ge(pe.h, pe_gu_base + 2 * hc + 2)
                nc.vector.wait_ge(ac.h, ac_base + hc + 1)
                ve.inc(nc.vector.tensor_tensor(hid[:, hc * TT:(hc + 1) * TT],
                                               psU[:, :], gtmp[:, :], ALU.mult))
            ve_hid = ve.n

            pe_w2_base = pe.n
            w2sl_base = [w2sl[0].n, w2sl[1].n]
            for dc in range(DC):
                if dc >= 2:
                    nc.sync.wait_ge(pe.h, pe_w2_base + dc - 1)
                w2sl[dc % 2].inc(nc.sync.dma_start(out=w2b[dc % 2][:, :], in_=w2ps[dc, :, :]), 16)
            ve_w2_base = ve.n
            stsl_base = [stsl[0].n, stsl[1].n]
            for dc in range(DC):
                nc.tensor.wait_ge(w2sl[dc % 2].h, w2sl_base[dc % 2] + 16 * (dc // 2 + 1))
                if dc == 0:
                    nc.tensor.wait_ge(ve.h, ve_hid)
                else:
                    nc.tensor.wait_ge(ve.h, ve_w2_base + dc)
                for hc in range(64):
                    ins = nc.tensor.matmul(psO[:, :], w2b[dc % 2][:, hc * 128:(hc + 1) * 128],
                                           hid[:, hc * TT:(hc + 1) * TT],
                                           start=(hc == 0), stop=(hc == 63))
                pe.inc(ins)
                nc.vector.wait_ge(pe.h, pe_w2_base + dc + 1)
                if dc >= 2:
                    nc.vector.wait_ge(stsl[dc % 2].h, stsl_base[dc % 2] + 16 * (dc // 2))
                else:
                    nc.vector.wait_ge(stsl[dc % 2].h, stsl_base[dc % 2])
                ve.inc(nc.vector.scalar_tensor_tensor(
                    xot[dc % 2][:, :], psO[:, :], 0.5, xt[:, dc * TT:(dc + 1) * TT],
                    ALU.mult, ALU.add))
                nc.scalar.wait_ge(ve.h, ve.n)
                stsl[dc % 2].inc(nc.scalar.dma_start(
                    out=xo[dc, :, tt * TT:(tt + 1) * TT], in_=xot[dc % 2][:, :]), 16)

        nc.sync.wait_ge(stsl[0].h, stsl[0].n)
        nc.sync.wait_ge(stsl[1].h, stsl[1].n)
        nc.vector.wait_ge(stsl[0].h, stsl[0].n)
        nc.vector.wait_ge(stsl[1].h, stsl[1].n)


def build_ffn():
    nc = bass.Bass("TRN2", target_bir_lowering=False, debug=False)
    xfm = nc.dram_tensor("xfm", [DC, 128, T], F32, kind="ExternalInput").ap()
    w1s = nc.dram_tensor("w1s", [64, 128, 2048], F32, kind="ExternalInput").ap()
    w3s = nc.dram_tensor("w3s", [64, 128, 2048], F32, kind="ExternalInput").ap()
    w2ps = nc.dram_tensor("w2ps", [16, 128, 8192], BF16, kind="ExternalInput").ap()
    onesc = nc.dram_tensor("onesc", [128, 128], BF16, kind="ExternalInput").ap()
    xo = nc.dram_tensor("xo", [DC, 128, T], F32, kind="ExternalOutput").ap()
    es = contextlib.ExitStack()
    with es:
        sm = make_sems(nc, es)
        emit_ffn(nc, es, sm, xfm, w1s, w3s, w2ps, onesc, xo)
    return nc


_cache = {}


def _prep_host(inputs):
    h = {}
    h["w1s"] = tile_w(np.ascontiguousarray(inputs["L0_ffn_W1"].T))
    h["w3s"] = tile_w(np.ascontiguousarray(inputs["L0_ffn_W3"].T))
    h["w2t"] = np.ascontiguousarray(inputs["L0_ffn_W2"].T)
    h["h128"] = hadamard(128)
    h["h64"] = hadamard(64)
    h["onesc"] = np.ones((128, 128), BF)
    h["identf"] = np.eye(128, dtype=np.float32)
    sel = np.zeros((8, 8 * 128), np.float32)
    for e in range(8):
        sel[e, e * 128:(e + 1) * 128] = 1.0
    h["selc"] = sel.astype(BF)
    h["rts"] = tile_w(np.ascontiguousarray(inputs["L1_ffn_router_w"].T))
    h["wgs"] = np.stack([tile_w(inputs["L1_ffn_wg"][e]) for e in range(8)])
    h["wus"] = np.stack([tile_w(inputs["L1_ffn_wu"][e]) for e in range(8)])
    h["wds"] = np.stack([tile_w(inputs["L1_ffn_wd"][e]) for e in range(8)])
    h["sgs"] = tile_w(inputs["L1_ffn_sg"])
    h["sus"] = tile_w(inputs["L1_ffn_su"])
    h["sds"] = tile_w(inputs["L1_ffn_sd"])
    h["wdqs"] = tile_w(inputs["L1_attn_w_dq"])
    h["wdkvs"] = tile_w(inputs["L1_attn_w_dkv"])
    wuq = inputs["L1_attn_w_uq"]
    wuqr = np.empty((128, 2048), np.float32)
    for hh in range(16):
        wuqr[:, hh * 128:hh * 128 + 96] = wuq[:, hh * 96:(hh + 1) * 96]
        wuqr[:, hh * 128 + 96:(hh + 1) * 128] = wuq[:, 1536 + hh * 32:1536 + (hh + 1) * 32]
    h["wuqr"] = wuqr
    h["wukv"] = np.ascontiguousarray(inputs["L1_attn_w_ukv"])
    h["wos"] = tile_w(inputs["L1_attn_w_o"])
    h["gq"] = np.ascontiguousarray(inputs["L1_attn_gamma_q"].reshape(128, 1))
    h["gkv"] = np.ascontiguousarray(inputs["L1_attn_gamma_kv"].reshape(128, 1))

    def rope_tables(nd, pos):
        nf = nd // 2
        inv = 10000.0 ** (-np.arange(0, nd, 2, dtype=np.float64) / nd)
        ang = pos[None, :] * inv[:, None]
        A = np.empty((nd, len(pos)), np.float32)
        B = np.empty((nd, len(pos)), np.float32)
        A[0::2] = np.cos(ang); A[1::2] = np.cos(ang)
        B[0::2] = -np.sin(ang); B[1::2] = np.sin(ang)
        return A.astype(BF), B.astype(BF)

    for half in range(2):
        loc = np.arange(T, dtype=np.float64) + half * T
        rem = np.arange(T, dtype=np.float64) + (1 - half) * T
        qa, qb = rope_tables(32, loc)
        qA = np.zeros((128, T), np.float32); qB = np.zeros((128, T), np.float32)
        qA = qA.astype(BF); qB = qB.astype(BF)
        qA[:96] = BF(1.0); qA[96:] = qa; qB[96:] = qb
        h[f"qA{half}"], h[f"qB{half}"] = np.ascontiguousarray(qA), np.ascontiguousarray(qB)
        ka, kb = rope_tables(64, np.concatenate([loc, rem]))
        h[f"kA{half}"], h[f"kB{half}"] = np.ascontiguousarray(ka), np.ascontiguousarray(kb)
        drop = np.zeros((1, 32), np.float32)
        if half == 0:
            drop[0, 16:] = 1.0
        h[f"drop{half}"] = drop
    dm = np.zeros((4, 128, 512), np.float32)
    for rr in range(4):
        for k in range(128):
            dm[rr, k, rr * 128 + k:] = 1.0
    h["dmaskc"] = dm.astype(BF)
    win = np.zeros((8576, 2048), np.float32)
    win[:8512] = inputs["L0_attn_W_in"]
    h["wins"] = tile_w(np.ascontiguousarray(win.T))
    h["wouts"] = tile_w(np.ascontiguousarray(inputs["L0_attn_W_out"].T))
    cw = inputs["L0_attn_conv_weight"]
    h["convw"] = np.ascontiguousarray(cw.reshape(34, 128, 4).transpose(1, 0, 2).reshape(128, 34 * 4))
    h["convbc"] = np.ascontiguousarray(inputs["L0_attn_conv_bias"].reshape(34, 128).T)
    h["nrmw"] = np.ascontiguousarray(inputs["L0_attn_norm_weight"].reshape(32, 128).T)
    dex = np.repeat(inputs["L0_attn_D"], 64)
    h["dexp"] = np.ascontiguousarray(dex.reshape(32, 128).T)
    h["abias"] = np.ascontiguousarray(
        np.stack([inputs["L0_attn_dt_bias"], inputs["L0_attn_A_log"]], axis=1))
    h["identb"] = np.eye(128, dtype=np.float32).astype(BF)
    h["onesf"] = np.ones((128, 128), np.float32)
    mk = np.zeros((128, 128), np.float32)
    for u in range(128):
        mk[u, u:] = 1.0
    h["maskc"] = mk.astype(BF)
    return h


def _run(nc_key, build_fn, in_maps):
    if nc_key not in _cache:
        _cache[nc_key] = build_fn()
    nc = _cache[nc_key]
    return run_bass_kernel_spmd(nc, in_maps, core_ids=list(range(NCORES)))


def shard_x(x):
    out = []
    for c in range(NCORES):
        b, hh = c >> 1, c & 1
        out.append(fm_chunks(np.ascontiguousarray(x[b, hh * T:(hh + 1) * T, :])))
    return out


def unshard_x(shards):
    x = np.empty((4, 4096, 2048), np.float32)
    for c in range(NCORES):
        b, hh = c >> 1, c & 1
        x[b, hh * T:(hh + 1) * T, :] = fm_to_tokmajor(shards[c])
    return x


def run_prep(h):
    im = [{"w2t": h["w2t"], "h128": h["h128"], "h64": h["h64"]}] * NCORES
    res = _run("prep", build_prep, im)
    return res.results[0]["w2ps"]


def run_ffn(h, xs, w2ps):
    im = [{"xfm": xs[c], "w1s": h["w1s"], "w3s": h["w3s"], "w2ps": w2ps,
           "onesc": h["onesc"]} for c in range(NCORES)]
    res = _run("ffn", build_ffn, im)
    return [res.results[c]["xo"] for c in range(NCORES)]


def _kernel_unfused_body(**inputs):
    h = _prep_host(inputs)
    xs = shard_x(np.asarray(inputs["L0_attn_hidden_states"], np.float32))
    w2ps = run_prep(h)
    xs = run_ffn(h, xs, w2ps)
    xs = run_mamba(h, xs)
    xs = run_ffn(h, xs, w2ps)
    xs = run_moe(h, xs)
    xs = run_attn(h, xs)
    xs = run_moe(h, xs)
    return unshard_x(xs)


class LNEmitter:

    def __init__(self, nc, es, psA, psB, psr1=None, psr2=None, xsq_buf=None,
                 rows_buf=None, rowsb_buf=None, mb_buf=None, rb_buf=None,
                 t32_bufs=None):
        self.nc = nc
        self.psA, self.psB = psA, psB
        self.ext = rows_buf is not None
        self.xt = es.enter_context(nc.sbuf_tensor(uq("ln_xt"), [128, DC * TT], F32))
        if xsq_buf is None:
            self.xsq = es.enter_context(nc.sbuf_tensor(uq("ln_xsq"), [128, DC * TT], BF16))
        else:
            self.xsq = xsq_buf
        if self.ext:
            self.rows, self.rowsb = rows_buf, rowsb_buf
            self.mb, self.rb = mb_buf, rb_buf
            self.t32 = t32_bufs
        if not self.ext:
            self.rows = es.enter_context(nc.sbuf_tensor(uq("ln_rows"), [1, TT * 5], F32))
            self.rowsb = es.enter_context(nc.sbuf_tensor(uq("ln_rowsb"), [1, TT * 2], BF16))
            self.mb = es.enter_context(nc.sbuf_tensor(uq("ln_mb"), [128, TT], F32))
            self.rb = es.enter_context(nc.sbuf_tensor(uq("ln_rb"), [128, TT], F32))
            self.t32 = [es.enter_context(nc.sbuf_tensor(uq(f"ln_t32_{i}"), [128, TT], F32))
                        for i in range(2)]
        if psr1 is None:
            self.psR1 = es.enter_context(nc.psum_tensor(uq("ln_psR1"), [1, TT], F32))[:, :]
            self.psR2 = es.enter_context(nc.psum_tensor(uq("ln_psR2"), [1, TT], F32))[:, :]
        else:
            self.psR1, self.psR2 = psr1, psr2

    def emit(self, sm, xfm, tt, dest_chunk, ones):
        nc = self.nc
        sy, pe, ac, ve, gv = sm["sy"], sm["pe"], sm["ac"], sm["ve"], sm["gv"]
        for dc in range(DC):
            sy.inc(nc.sync.dma_start(
                out=self.xt[:, dc * TT:(dc + 1) * TT],
                in_=xfm[dc, :, tt * TT:(tt + 1) * TT]), 16)
        nc.vector.wait_ge(sy.h, sy.n)
        for dc in range(DC):
            ins = nc.vector.tensor_copy(dest_chunk(dc), self.xt[:, dc * TT:(dc + 1) * TT])
        ve.inc(ins)
        nc.scalar.wait_ge(sy.h, sy.n)
        ac.inc(nc.scalar.activation(self.xsq[:, :], self.xt[:, :], AF.Square))
        nc.tensor.wait_ge(ve.h, ve.n)
        nc.tensor.wait_ge(ac.h, ac.n)
        for dc in range(DC):
            nc.tensor.matmul(self.psR1, ones[:, 0:1], dest_chunk(dc),
                             start=(dc == 0), stop=(dc == DC - 1))
        for dc in range(DC):
            ins = nc.tensor.matmul(self.psR2, ones[:, 0:1],
                                   self.xsq[:, dc * TT:(dc + 1) * TT],
                                   start=(dc == 0), stop=(dc == DC - 1))
        pe.inc(ins)
        nc.vector.wait_ge(pe.h, pe.n)
        rows, rowsb = self.rows, self.rowsb
        r_mean, r_msq = rows[:, 0:TT], rows[:, TT:2 * TT]
        r_var, r_std, r_rstd = rows[:, 2 * TT:3 * TT], rows[:, 3 * TT:4 * TT], rows[:, 4 * TT:5 * TT]

        def vch(ins):
            ve.inc(ins)
            nc.vector.wait_ge(ve.h, ve.n)
        vch(nc.vector.tensor_scalar_mul(r_mean, self.psR1, 1.0 / D))
        vch(nc.vector.tensor_scalar_mul(r_msq, self.psR2, 1.0 / D))
        vch(nc.vector.tensor_mul(r_var, r_mean, r_mean))
        vch(nc.vector.tensor_sub(r_var, r_msq, r_var))
        ve.inc(nc.vector.tensor_scalar_add(r_var, r_var, 1e-5))
        nc.scalar.wait_ge(ve.h, ve.n)
        ac.inc(nc.scalar.activation(r_std, r_var, AF.Sqrt))
        nc.vector.wait_ge(ac.h, ac.n)
        vch(nc.vector.reciprocal(r_rstd, r_std))
        vch(nc.vector.tensor_copy(rowsb[:, 0:TT], r_rstd))
        vch(nc.vector.tensor_mul(r_std, r_mean, r_rstd))
        ve.inc(nc.vector.tensor_copy(rowsb[:, TT:2 * TT], r_std))
        nc.tensor.wait_ge(ve.h, ve.n)
        pe_snap = pe.n
        pe.inc(nc.tensor.matmul(self.psA[:, 0:TT], ones[0:1, :], rowsb[:, 0:TT],
                                start=True, stop=True))
        pe.inc(nc.tensor.matmul(self.psB[:, 0:TT], ones[0:1, :], rowsb[:, TT:2 * TT],
                                start=True, stop=True))
        nc.scalar.wait_ge(pe.h, pe_snap + 1)
        ac.inc(nc.scalar.copy(self.rb[:, :], self.psA[:, 0:TT]))
        nc.scalar.wait_ge(pe.h, pe_snap + 2)
        ac.inc(nc.scalar.copy(self.mb[:, :], self.psB[:, 0:TT]))
        nc.vector.wait_ge(ac.h, ac.n)
        nc.gpsimd.wait_ge(ac.h, ac.n)
        gv_base = gv.n
        ve_ap = ve.n
        for dc in range(DC):
            if dc >= 2:
                nc.vector.wait_ge(gv.h, gv_base + dc - 1)
            ve.inc(nc.vector.tensor_mul(self.t32[dc % 2][:, :],
                                        self.xt[:, dc * TT:(dc + 1) * TT], self.rb[:, :]))
            nc.gpsimd.wait_ge(ve.h, ve_ap + dc + 1)
            gv.inc(nc.gpsimd.tensor_sub(dest_chunk(dc), self.t32[dc % 2][:, :],
                                        self.mb[:, :]))


NE = 8
HC_E = 11
HC_S = 22


def emit_moe(nc, es, sm, xfm, rts, wgs, wus, wds, sgs, sus, sds, onesc, identf,
             selc, xo):
    ones = es.enter_context(nc.sbuf_tensor(uq("ones"), [128, 128], BF16))
    idf = es.enter_context(nc.sbuf_tensor(uq("idf"), [128, 128], F32))
    sel = es.enter_context(nc.sbuf_tensor(uq("sel"), [8, 8 * 128], BF16))
    rtsb = es.enter_context(nc.sbuf_tensor(uq("rtsb"), [128, 2048], BF16))
    xf = es.enter_context(nc.sbuf_tensor(uq("xf"), [128, DC * T], BF16))
    hid = es.enter_context(nc.sbuf_tensor(uq("hid"), [128, HC_E * T], BF16))
    wgb = [es.enter_context(nc.sbuf_tensor(uq(f"wgb{i}"), [128, 2048], BF16)) for i in range(2)]
    wub = [es.enter_context(nc.sbuf_tensor(uq(f"wub{i}"), [128, 2048], BF16)) for i in range(2)]
    wdb = [es.enter_context(nc.sbuf_tensor(uq(f"wdb{i}"), [128, 1408], BF16)) for i in range(2)]
    sgt = es.enter_context(nc.sbuf_tensor(uq("sgt"), [128, 1024], BF16))
    ut = es.enter_context(nc.sbuf_tensor(uq("ut"), [128, 1024], BF16))
    geb = es.enter_context(nc.sbuf_tensor(uq("geb"), [128, T], BF16))
    gw = es.enter_context(nc.sbuf_tensor(uq("gw"), [128, 16 * 8], F32))
    grow = es.enter_context(nc.sbuf_tensor(uq("grow"), [8, T], BF16))
    gsc = es.enter_context(nc.sbuf_tensor(uq("gsc"), [128, 64], F32))
    ot = [es.enter_context(nc.sbuf_tensor(uq(f"ot{i}"), [128, TT], F32)) for i in range(2)]

    psG = es.enter_context(nc.psum_tensor(uq("psG"), [128, 1024], F32))
    psU = es.enter_context(nc.psum_tensor(uq("psU"), [128, 1024], F32))
    psO = es.enter_context(nc.psum_tensor(uq("psO"), [128, TT], F32))
    psR = es.enter_context(nc.psum_tensor(uq("psR"), [128, 8], F32))
    psT = es.enter_context(nc.psum_tensor(uq("psT"), [8, 128], F32))
    psB = es.enter_context(nc.psum_tensor(uq("psB"), [128, TT], F32))

    if True:
        sy, gp, pe, ac, ve, gv = (sm["sy"], sm["gp"], sm["pe"], sm["ac"],
                                  sm["ve"], sm["gv"])
        wsl, dsl, osl = sm["wsl"], sm["dsl"], sm["osl"]
        ln = LNEmitter(nc, es, psG, psU, psr1=psO[0:1, 0:TT], psr2=psB[0:1, 0:TT], xsq_buf=hid[:, 0:DC * TT])
        sm = {"sy": sy, "pe": pe, "ac": ac, "ve": ve, "gv": gv}

        gp.inc(nc.gpsimd.dma_start(out=ones[:, :], in_=onesc[:, :]), 16)
        gp.inc(nc.gpsimd.dma_start(out=idf[:, :], in_=identf[:, :]), 16)
        gp.inc(nc.gpsimd.dma_start(out=sel[:, :], in_=selc[:, :]), 16)
        gp.inc(nc.gpsimd.dma_start(out=rtsb[:, :], in_=rts[0, :, :]), 16)
        for dc in range(DC):
            sy.inc(nc.sync.dma_start(out=xo[dc, :, :], in_=xfm[dc, :, :]), 16)
        nc.tensor.wait_ge(gp.h, gp.n)

        for tt in range(NTT):
            if tt > 0:
                nc.sync.wait_ge(gv.h, gv.n)
            ln.emit(sm, xfm, tt,
                    lambda dc, _t=tt: xf[:, dc * T + _t * TT: dc * T + (_t + 1) * TT],
                    ones)

        nc.tensor.wait_ge(gv.h, gv.n)

        def vch(ins):
            ve.inc(ins)
            nc.vector.wait_ge(ve.h, ve.n)
        for ts in range(16):
            nc.tensor.wait_ge(ve.h, ve.n)
            for dc in range(DC):
                ins = nc.tensor.matmul(psR[:, :],
                                       xf[:, dc * T + ts * 128: dc * T + (ts + 1) * 128],
                                       rtsb[:, dc * 128: dc * 128 + 8],
                                       start=(dc == 0), stop=(dc == DC - 1))
            pe.inc(ins)
            nc.vector.wait_ge(pe.h, pe.n)
            g0 = gsc[:, 0:8]
            nm = gsc[:, 8:9]
            ex = gsc[:, 9:17]
            s1 = gsc[:, 17:18]
            rs = gsc[:, 18:19]
            gg = gsc[:, 19:27]
            m1 = gsc[:, 27:28]
            vch(nc.vector.tensor_copy(g0, psR[:, :]))
            vch(nc.vector.reduce_max(nm, g0, axis=AX.X, negate=True))
            nc.scalar.wait_ge(ve.h, ve.n)
            ac.inc(nc.scalar.activation(ex, g0, AF.Exp, bias=nm, accum_out=s1))
            nc.vector.wait_ge(ac.h, ac.n)
            vch(nc.vector.reciprocal(rs, s1))
            vch(nc.vector.tensor_scalar(gg, ex, rs, None, ALU.mult))
            vch(nc.vector.reduce_max(m1, gg, axis=AX.X))
            eq = gsc[:, 28:36]
            vch(nc.vector.tensor_scalar(eq, gg, m1, None, ALU.is_ge))
            tt2 = gsc[:, 36:44]
            vch(nc.vector.scalar_tensor_tensor(tt2, eq, -1e30, gg, ALU.mult, ALU.add))
            m2 = gsc[:, 44:45]
            vch(nc.vector.reduce_max(m2, tt2, axis=AX.X))
            keep = gsc[:, 45:53]
            vch(nc.vector.tensor_scalar(keep, gg, m2, None, ALU.is_ge))
            den = gsc[:, 53:54]
            vch(nc.vector.tensor_add(den, m1, m2))
            vch(nc.vector.tensor_scalar(den, den, 1e-20, None, ALU.add))
            rd = gsc[:, 54:55]
            vch(nc.vector.reciprocal(rd, den))
            vch(nc.vector.scalar_tensor_tensor(gw[:, ts * 8:(ts + 1) * 8],
                                               gg, rd, keep, ALU.mult, ALU.mult))
        nc.tensor.wait_ge(ve.h, ve.n)
        for ts in range(16):
            nc.tensor.wait_ge(ac.h, ac.n)
            pe.inc(nc.tensor.transpose(psT[:, :], gw[:, ts * 8:(ts + 1) * 8], idf[:, :]))
            nc.scalar.wait_ge(pe.h, pe.n)
            ac.inc(nc.scalar.copy(grow[:, ts * 128:(ts + 1) * 128], psT[:, 0:128]))

        nc.gpsimd.wait_ge(sy.h, sy.n)
        for e in range(NE + 2):
            shared = e >= NE
            hh = e - NE
            if not shared:
                nc.tensor.wait_ge(ac.h, ac.n)
                nc.tensor.wait_ge(gv.h, gv.n)
                for q in range(4):
                    nc.tensor.wait_ge(ac.h, ac.n)
                    pe.inc(nc.tensor.matmul(psB[:, :], sel[:, e * 128:(e + 1) * 128],
                                            grow[:, q * TT:(q + 1) * TT],
                                            start=True, stop=True))
                    nc.scalar.wait_ge(pe.h, pe.n)
                    ac.inc(nc.scalar.copy(geb[:, q * TT:(q + 1) * TT], psB[:, :]))
            pe_gu = pe.n
            wsl_b = [wsl[0].n, wsl[1].n]

            def gu_load(hc):
                src_g = sgs[hh * HC_E + hc, :, :] if shared else wgs[e, hc, :, :]
                src_u = sus[hh * HC_E + hc, :, :] if shared else wus[e, hc, :, :]
                if hc >= 2:
                    nc.gpsimd.wait_ge(pe.h, pe_gu + 4 * hc - 4)
                wsl[hc % 2].inc(nc.gpsimd.dma_start(out=wgb[hc % 2][:, :], in_=src_g), 16)
                wsl[hc % 2].inc(nc.gpsimd.dma_start(out=wub[hc % 2][:, :], in_=src_u), 16)
            gu_load(0)
            gu_load(1)
            ac_gu = ac.n
            ve_gu = ve.n
            gv_gu = gv.n
            for hc in range(HC_E):
                nc.tensor.wait_ge(wsl[hc % 2].h, wsl_b[hc % 2] + 32 * (hc // 2 + 1))
                for th in range(2):
                    k = 2 * hc + th
                    if k >= 1:
                        nc.tensor.wait_ge(ac.h, ac_gu + k)
                        nc.tensor.wait_ge(ve.h, ve_gu + k)
                    for dc in range(DC):
                        for tq in range(2):
                            ins = nc.tensor.matmul(
                                psG[:, tq * TT:(tq + 1) * TT],
                                wgb[hc % 2][:, dc * 128:(dc + 1) * 128],
                                xf[:, dc * T + th * 1024 + tq * TT: dc * T + th * 1024 + (tq + 1) * TT],
                                start=(dc == 0), stop=(dc == DC - 1))
                    pe.inc(ins)
                    for dc in range(DC):
                        for tq in range(2):
                            ins = nc.tensor.matmul(
                                psU[:, tq * TT:(tq + 1) * TT],
                                wub[hc % 2][:, dc * 128:(dc + 1) * 128],
                                xf[:, dc * T + th * 1024 + tq * TT: dc * T + th * 1024 + (tq + 1) * TT],
                                start=(dc == 0), stop=(dc == DC - 1))
                    pe.inc(ins)
                    nc.scalar.wait_ge(pe.h, pe_gu + 2 * k + 1)
                    if k >= 1:
                        nc.scalar.wait_ge(gv.h, gv_gu + k)
                    ac.inc(nc.scalar.activation(sgt[:, :], psG[:, :], AF.Silu))
                    nc.vector.wait_ge(pe.h, pe_gu + 2 * k + 2)
                    if k >= 1:
                        nc.vector.wait_ge(gv.h, gv_gu + k)
                    if shared:
                        ve.inc(nc.vector.tensor_copy(ut[:, :], psU[:, :]))
                    else:
                        ve.inc(nc.vector.tensor_tensor(
                            ut[:, :], psU[:, :],
                            geb[:, th * 1024:(th + 1) * 1024], ALU.mult))
                    nc.gpsimd.wait_ge(ac.h, ac_gu + k + 1)
                    nc.gpsimd.wait_ge(ve.h, ve_gu + k + 1)
                    gv.inc(nc.gpsimd.tensor_mul(
                        hid[:, hc * T + th * 1024: hc * T + (th + 1) * 1024],
                        sgt[:, :], ut[:, :]))
                if hc + 2 < HC_E:
                    gu_load(hc + 2)
            gv_hid = gv.n
            dsl_b = [dsl[0].n, dsl[1].n]
            pe_wd = pe.n
            ac_wd = ac.n
            osl_b = [osl[0].n, osl[1].n]

            def wd_load(dc):
                src_d = sds[dc, :, hh * 1408:(hh + 1) * 1408] if shared else wds[e, dc, :, :]
                if dc >= 2:
                    nc.gpsimd.wait_ge(pe.h, pe_wd + 4 * (dc - 1))
                dsl[dc % 2].inc(nc.gpsimd.dma_start(out=wdb[dc % 2][:, :], in_=src_d), 16)
            wd_load(0)
            wd_load(1)
            for dc in range(DC):
                nc.tensor.wait_ge(dsl[dc % 2].h, dsl_b[dc % 2] + 16 * (dc // 2 + 1))
                if dc == 0:
                    nc.tensor.wait_ge(gv.h, gv_hid)
                for tq in range(4):
                    k = 4 * dc + tq
                    if k >= 1:
                        nc.tensor.wait_ge(ac.h, ac_wd + k)
                    for hc in range(HC_E):
                        ins = nc.tensor.matmul(
                            psO[:, :], wdb[dc % 2][:, hc * 128:(hc + 1) * 128],
                            hid[:, hc * T + tq * TT: hc * T + (tq + 1) * TT],
                            start=(hc == 0), stop=(hc == HC_E - 1))
                    pe.inc(ins)
                    nc.scalar.wait_ge(pe.h, pe_wd + k + 1)
                    if k >= 2:
                        nc.scalar.wait_ge(osl[k % 2].h, osl_b[k % 2] + 16 * (k // 2))
                    ac.inc(nc.scalar.activation(ot[k % 2][:, :], psO[:, :], AF.Copy,
                                                scale=0.5))
                    nc.gpsimd.wait_ge(ac.h, ac.n)
                    osl[k % 2].inc(nc.gpsimd.dma_start(
                        out=xo[dc, :, tq * TT:(tq + 1) * TT], in_=ot[k % 2][:, :],
                        accum_op=ALU.add), 16)
                if dc + 2 < DC:
                    wd_load(dc + 2)
            nc.gpsimd.wait_ge(osl[0].h, osl[0].n)
            nc.gpsimd.wait_ge(osl[1].h, osl[1].n)
        nc.sync.wait_ge(osl[0].h, osl[0].n)
        nc.sync.wait_ge(osl[1].h, osl[1].n)
        nc.vector.wait_ge(osl[0].h, osl[0].n)
        nc.vector.wait_ge(osl[1].h, osl[1].n)


def build_moe():
    nc = bass.Bass("TRN2", target_bir_lowering=False, debug=False)
    xfm = nc.dram_tensor("xfm", [DC, 128, T], F32, kind="ExternalInput").ap()
    rts = nc.dram_tensor("rts", [1, 128, 2048], F32, kind="ExternalInput").ap()
    wgs = nc.dram_tensor("wgs", [NE, HC_E, 128, 2048], F32, kind="ExternalInput").ap()
    wus = nc.dram_tensor("wus", [NE, HC_E, 128, 2048], F32, kind="ExternalInput").ap()
    wds = nc.dram_tensor("wds", [NE, DC, 128, 1408], F32, kind="ExternalInput").ap()
    sgs = nc.dram_tensor("sgs", [HC_S, 128, 2048], F32, kind="ExternalInput").ap()
    sus = nc.dram_tensor("sus", [HC_S, 128, 2048], F32, kind="ExternalInput").ap()
    sds = nc.dram_tensor("sds", [DC, 128, 2816], F32, kind="ExternalInput").ap()
    onesc = nc.dram_tensor("onesc", [128, 128], BF16, kind="ExternalInput").ap()
    identf = nc.dram_tensor("identf", [128, 128], F32, kind="ExternalInput").ap()
    selc = nc.dram_tensor("selc", [8, 8 * 128], BF16, kind="ExternalInput").ap()
    xo = nc.dram_tensor("xo", [DC, 128, T], F32, kind="ExternalOutput").ap()
    es = contextlib.ExitStack()
    with es:
        sm = make_sems(nc, es)
        emit_moe(nc, es, sm, xfm, rts, wgs, wus, wds, sgs, sus, sds, onesc,
                 identf, selc, xo)
    return nc


def run_moe(h, xs):
    im = [{"xfm": xs[c], "rts": h["rts"], "wgs": h["wgs"], "wus": h["wus"],
           "wds": h["wds"], "sgs": h["sgs"], "sus": h["sus"], "sds": h["sds"],
           "onesc": h["onesc"], "identf": h["identf"], "selc": h["selc"]}
          for c in range(NCORES)]
    res = _run("moe", build_moe, im)
    return [res.results[c]["xo"] for c in range(NCORES)]


NKT = 32
QTN = 4


def emit_attn(nc, es, sm, xloc, xrem, wdqs, wdkvs, wuqr, wukv, wos, gq, gkv,
              qA, qB, kA, kB, dmaskc, droprc, onesc, xo, qd, vd):
    ones = es.enter_context(nc.sbuf_tensor(uq("ones"), [128, 128], BF16))
    wdqb = es.enter_context(nc.sbuf_tensor(uq("wdqb"), [128, 2048], BF16))
    wdkvb = es.enter_context(nc.sbuf_tensor(uq("wdkvb"), [128, 2 * 2048], BF16))
    wuqb = es.enter_context(nc.sbuf_tensor(uq("wuqb"), [128, 2048], BF16))
    wukvb = es.enter_context(nc.sbuf_tensor(uq("wukvb"), [128, 1280], BF16))
    gqb = es.enter_context(nc.sbuf_tensor(uq("gqb"), [128, 1], F32))
    gkvb = es.enter_context(nc.sbuf_tensor(uq("gkvb"), [128, 1], F32))
    qAb = es.enter_context(nc.sbuf_tensor(uq("qAb"), [128, T], BF16))
    qBb = es.enter_context(nc.sbuf_tensor(uq("qBb"), [128, T], BF16))
    kAb = es.enter_context(nc.sbuf_tensor(uq("kAb"), [64, 2 * T], BF16))
    kBb = es.enter_context(nc.sbuf_tensor(uq("kBb"), [64, 2 * T], BF16))
    dmask = es.enter_context(nc.sbuf_tensor(uq("dmask"), [128, 4 * TT], BF16))
    dropr = es.enter_context(nc.sbuf_tensor(uq("dropr"), [1, NKT], F32))
    cqnall = es.enter_context(nc.sbuf_tensor(uq("cqnall"), [128, T + TT], BF16))
    ckvnall = es.enter_context(nc.sbuf_tensor(uq("ckvnall"), [128, 2 * T], BF16))
    krall = es.enter_context(nc.sbuf_tensor(uq("krall"), [64, TT], BF16))
    kall = [es.enter_context(nc.sbuf_tensor(uq(f"kall{g}"), [128, 2 * T], BF16))
            for g in range(4)]
    sc1 = es.enter_context(nc.sbuf_tensor(uq("sc1"), [128, 1024], BF16))
    sc2 = es.enter_context(nc.sbuf_tensor(uq("sc2"), [128, 1024], BF16))
    sf1 = es.enter_context(nc.sbuf_tensor(uq("sf1"), [128, TT], F32))
    sf2 = es.enter_context(nc.sbuf_tensor(uq("sf2"), [128, TT], F32))
    rows = es.enter_context(nc.sbuf_tensor(uq("rows"), [1, 5 * TT + 64], F32))
    rowsb = es.enter_context(nc.sbuf_tensor(uq("rowsb"), [1, 1024], BF16))
    lnrb = es.enter_context(nc.sbuf_tensor(uq("lnrb"), [128, TT], F32))
    kmax = es.enter_context(nc.sbuf_tensor(uq("kmax"), [1, 8], F32))
    qpr = es.enter_context(nc.sbuf_tensor(uq("qpr"), [128, 2 * TT], BF16))
    vtb = [es.enter_context(nc.sbuf_tensor(uq(f"vtb{i}"), [128, 1024], BF16)) for i in range(2)]
    sc3 = vtb[0]
    pexp = [es.enter_context(nc.sbuf_tensor(uq(f"pexp{i}"), [128, TT], BF16)) for i in range(2)]
    biasT = es.enter_context(nc.sbuf_tensor(uq("biasT"), [128, 2 * NKT], F32))
    oa = es.enter_context(nc.sbuf_tensor(uq("oa"), [128, 32 * TT], BF16))
    rlb = es.enter_context(nc.sbuf_tensor(uq("rlb"), [128, TT], F32))
    wob = es.enter_context(nc.sbuf_tensor(uq("wob"), [128, 4096], BF16))
    xot = [es.enter_context(nc.sbuf_tensor(uq(f"xot{i}"), [128, TT], F32)) for i in range(2)]

    psS = es.enter_context(nc.psum_tensor(uq("psS"), [128, TT], F32))
    psB = es.enter_context(nc.psum_tensor(uq("psB"), [128, TT], F32))
    psL = es.enter_context(nc.psum_tensor(uq("psL"), [1, 1024], F32))
    psO = [es.enter_context(nc.psum_tensor(uq(f"psO{i}"), [128, TT], F32)) for i in range(4)]

    if True:
        sy, gp, pe, ac, ve, gv = (sm["sy"], sm["gp"], sm["pe"], sm["ac"],
                                  sm["ve"], sm["gv"])
        vsl, wsl, stsl, xsl = sm["vsl"], sm["wsl"], sm["stsl"], sm["xsl"]
        vst, qst, qld = sm["vst"], sm["qst"], sm["qld"]
        ln = LNEmitter(nc, es, psO[0], psO[1], psr1=psS[0:1, :], psr2=psB[0:1, :],
                       xsq_buf=oa[:, 0:DC * TT], rows_buf=rows, rowsb_buf=rowsb,
                       mb_buf=rlb, rb_buf=lnrb, t32_bufs=[xot[0], xot[1]])
        xlnb = oa[:, DC * TT:2 * DC * TT]
        sm = {"sy": sy, "pe": pe, "ac": ac, "ve": ve, "gv": gv}

        def vch(ins):
            ve.inc(ins)
            nc.vector.wait_ge(ve.h, ve.n)

        for t, d in [(ones, onesc), (wdqb, wdqs[0]), (wuqb, wuqr), (wukvb, wukv),
                     (gqb, gq), (gkvb, gkv), (qAb, qA), (qBb, qB), (kAb, kA),
                     (kBb, kB), (dropr, droprc)]:
            gp.inc(nc.gpsimd.dma_start(out=t[:, :] if t is not dmask else t, in_=d), 16)
        gp.inc(nc.gpsimd.dma_start(out=wdkvb[:, :].rearrange("p (m q) -> p m q", m=2), in_=wdkvs[:, :, :].transpose([1, 0, 2])), 16)
        gp.inc(nc.gpsimd.dma_start(out=dmask[:, :].rearrange("p (m q) -> p m q", m=4), in_=dmaskc[:, :, :].transpose([1, 0, 2])), 16)
        nc.tensor.wait_ge(gp.h, gp.n)
        nc.vector.wait_ge(gp.h, gp.n)
        nc.scalar.wait_ge(gp.h, gp.n)

        for tt in range(2 * NTT):
            xsrc = xloc if tt < NTT else xrem
            tl = tt if tt < NTT else tt - NTT
            if tt > 0:
                nc.sync.wait_ge(gv.h, gv.n)
            ln.emit(sm, xsrc, tl,
                    lambda dc: xlnb[:, dc * TT:(dc + 1) * TT], ones)
            nc.tensor.wait_ge(gv.h, gv.n)
            col = slice(tt * TT, (tt + 1) * TT)

            def t5ln(ps_in, gcol, dest, nrows=128):
                nc.scalar.wait_ge(pe.h, pe.n)
                nc.scalar.wait_ge(ve.h, ve.n)
                nc.scalar.wait_ge(gv.h, gv.n)
                ac.inc(nc.scalar.copy(sc1[:, 0:TT], ps_in))
                ac.inc(nc.scalar.activation(sc2[:, 0:TT], ps_in, AF.Square))
                nc.tensor.wait_ge(ac.h, ac.n)
                pe.inc(nc.tensor.matmul(psS[0:1, :], ones[:, 0:1], sc1[:, 0:TT],
                                        start=True, stop=True))
                pe.inc(nc.tensor.matmul(psB[0:1, :], ones[:, 0:1], sc2[:, 0:TT],
                                        start=True, stop=True))
                nc.vector.wait_ge(pe.h, pe.n)
                r_m, r_q, r_v, r_s, r_r = (rows[:, 0:TT], rows[:, TT:2 * TT],
                                           rows[:, 2 * TT:3 * TT], rows[:, 3 * TT:4 * TT],
                                           rows[:, 4 * TT:5 * TT])
                vch(nc.vector.tensor_scalar_mul(r_m, psS[0:1, :], 1.0 / nrows))
                vch(nc.vector.tensor_scalar_mul(r_q, psB[0:1, :], 1.0 / nrows))
                vch(nc.vector.tensor_mul(r_v, r_m, r_m))
                vch(nc.vector.tensor_sub(r_v, r_q, r_v))
                ve.inc(nc.vector.tensor_scalar_add(r_v, r_v, 1e-5))
                nc.scalar.wait_ge(ve.h, ve.n)
                ac.inc(nc.scalar.activation(r_s, r_v, AF.Sqrt))
                nc.vector.wait_ge(ac.h, ac.n)
                vch(nc.vector.reciprocal(r_r, r_s))
                vch(nc.vector.tensor_copy(rowsb[:, 0:TT], r_r))
                vch(nc.vector.tensor_mul(r_s, r_m, r_r))
                ve.inc(nc.vector.tensor_copy(rowsb[:, TT:2 * TT], r_s))
                nc.tensor.wait_ge(ve.h, ve.n)
                p_s = pe.n
                pe.inc(nc.tensor.matmul(psO[0][:, :], ones[0:1, :], rowsb[:, 0:TT],
                                        start=True, stop=True))
                pe.inc(nc.tensor.matmul(psO[1][:, :], ones[0:1, :], rowsb[:, TT:2 * TT],
                                        start=True, stop=True))
                nc.scalar.wait_ge(pe.h, p_s + 1)
                ac.inc(nc.scalar.copy(sf1[:, :], psO[0][:, :]))
                nc.scalar.wait_ge(pe.h, p_s + 2)
                ac.inc(nc.scalar.copy(sf2[:, :], psO[1][:, :]))
                nc.vector.wait_ge(ac.h, ac.n)
                ve.inc(nc.vector.tensor_mul(sf1[:, :], sc1[:, 0:TT], sf1[:, :]))
                nc.gpsimd.wait_ge(ve.h, ve.n)
                gv.inc(nc.gpsimd.tensor_sub(sc3[:, 0:TT], sf1[:, :], sf2[:, :]))
                nc.vector.wait_ge(gv.h, gv.n)
                ve.inc(nc.vector.tensor_scalar(dest, sc3[:, 0:TT], gcol, None, ALU.mult))
                nc.vector.wait_ge(ve.h, ve.n)

            nc.tensor.wait_ge(ac.h, ac.n)
            for dc in range(DC):
                ins = nc.tensor.matmul(psO[2][:, :], wdqb[:, dc * 128:(dc + 1) * 128],
                                       xlnb[:, dc * TT:(dc + 1) * TT],
                                       start=(dc == 0), stop=(dc == DC - 1))
            pe.inc(ins)
            cq_col = slice(tt * TT, (tt + 1) * TT) if tt < NTT else slice(T, T + TT)
            t5ln(psO[2][:, :], gqb[:, :], cqnall[:, cq_col])
            for dc in range(DC):
                ins = nc.tensor.matmul(psO[2][:, :], wdkvb[:, dc * 128:(dc + 1) * 128],
                                       xlnb[:, dc * TT:(dc + 1) * TT],
                                       start=(dc == 0), stop=(dc == DC - 1))
            pe.inc(ins)
            for dc in range(DC):
                ins = nc.tensor.matmul(psO[3][:, :], wdkvb[:, 2048 + dc * 128:2048 + (dc + 1) * 128],
                                       xlnb[:, dc * TT:(dc + 1) * TT],
                                       start=(dc == 0), stop=(dc == DC - 1))
            pe.inc(ins)
            t5ln(psO[2][:, :], gkvb[:, :], ckvnall[:, col])
            nc.scalar.wait_ge(pe.h, pe.n)
            nc.scalar.wait_ge(ve.h, ve.n)
            nc.scalar.wait_ge(gv.h, gv.n)
            ac.inc(nc.scalar.copy(sc1[0:64, 0:TT], psO[3][0:64, :]))
            nc.vector.wait_ge(ac.h, ac.n)
            mask32 = [(i // 2 * 2) + (1 - i % 2) for i in range(32)]
            vch(nc.vector.stream_shuffle(sc2[0:64, 0:TT], sc1[0:64, 0:TT], mask32))
            vch(nc.vector.tensor_mul(sf1[0:64, :], sc1[0:64, 0:TT], kAb[:, col]))
            nc.gpsimd.wait_ge(ve.h, ve.n)
            gv.inc(nc.gpsimd.tensor_mul(sf2[0:64, :], sc2[0:64, 0:TT], kBb[:, col]))
            nc.vector.wait_ge(gv.h, gv.n)
            vch(nc.vector.tensor_add(krall[:, :], sf1[0:64, :], sf2[0:64, :]))
            nc.tensor.wait_ge(ve.h, ve.n)
            pe_k = pe.n
            pe.inc(nc.tensor.matmul(psO[2][:, :], wukvb[:, 0:128], ckvnall[:, col],
                                    start=True, stop=True))
            pe.inc(nc.tensor.matmul(psO[3][:, :], wukvb[:, 128:256], ckvnall[:, col],
                                    start=True, stop=True))
            nc.scalar.wait_ge(pe.h, pe.n)
            ac.inc(nc.scalar.copy(kall[0][0:64, tt * TT:(tt + 1) * TT], psO[2][0:64, :]))
            ac.inc(nc.scalar.copy(kall[1][0:64, tt * TT:(tt + 1) * TT], psO[2][64:128, :]))
            ac.inc(nc.scalar.copy(kall[2][0:64, tt * TT:(tt + 1) * TT], psO[3][0:64, :]))
            ac.inc(nc.scalar.copy(kall[3][0:64, tt * TT:(tt + 1) * TT], psO[3][64:128, :]))
            nc.vector.wait_ge(ve.h, ve.n)
            for g in range(4):
                ins = nc.vector.tensor_copy(kall[g][64:128, tt * TT:(tt + 1) * TT],
                                            krall[:, :])
            ve.inc(ins)
            for sub in range(4):
                nc.tensor.wait_ge(ac.h, ac.n)
                lh = ckvnall[:, tt * TT + sub * 128: tt * TT + (sub + 1) * 128]
                pe.inc(nc.tensor.matmul(psO[2][:, :], lh, wukvb[:, 256:768],
                                        start=True, stop=True))
                pe.inc(nc.tensor.matmul(psO[3][:, :], lh, wukvb[:, 768:1280],
                                        start=True, stop=True))
                nc.scalar.wait_ge(pe.h, pe.n)
                ac.inc(nc.scalar.copy(sc3[:, 0:512], psO[2][:, :]))
                ac.inc(nc.scalar.copy(sc3[:, 512:1024], psO[3][:, :]))
                nc.sync.wait_ge(ac.h, ac.n)
                vst.inc(nc.sync.dma_start(out=vd[tt * 4 + sub, :, :], in_=sc3[:, 0:1024]), 16)
                nc.scalar.wait_ge(vst.h, vst.n)

        for qt in range(QTN):
            for blk in range(16):
                nc.tensor.wait_ge(ac.h, ac.n)
                for _ in [0]:
                    ins = nc.tensor.matmul(psO[2][:, :], wuqb[:, blk * 128:(blk + 1) * 128],
                                           cqnall[:, qt * TT:(qt + 1) * TT],
                                           start=True, stop=True)
                pe.inc(ins)
                nc.scalar.wait_ge(pe.h, pe.n)
                nc.scalar.wait_ge(ve.h, ve.n)
                nc.scalar.wait_ge(gv.h, gv.n)
                ac.inc(nc.scalar.copy(sc1[:, 0:TT], psO[2][:, :]))
                nc.vector.wait_ge(ac.h, ac.n)
                qcol = slice(qt * TT, (qt + 1) * TT)
                mask32 = [(i // 2 * 2) + (1 - i % 2) for i in range(32)]
                vch(nc.vector.stream_shuffle(sc2[:, 0:TT], sc1[:, 0:TT], mask32))
                vch(nc.vector.tensor_mul(sf1[:, :], sc1[:, 0:TT], qAb[:, qcol]))
                nc.gpsimd.wait_ge(ve.h, ve.n)
                gv.inc(nc.gpsimd.tensor_mul(sf2[:, :], sc2[:, 0:TT], qBb[:, qcol]))
                nc.vector.wait_ge(gv.h, gv.n)
                vch(nc.vector.tensor_add(sc3[:, 0:TT], sf1[:, :], sf2[:, :]))
                nc.sync.wait_ge(ve.h, ve.n)
                qst.inc(nc.sync.dma_start(out=qd[blk, :, qcol], in_=sc3[:, 0:TT]), 16)
                nc.vector.wait_ge(qst.h, qst.n)

        nc.vector.wait_ge(qst.h, qst.n)
        ve.inc(nc.vector.memset(kmax[:, :], 0.0))
        for g in range(4):
            for piece in range(8):
                nc.scalar.wait_ge(pe.h, pe.n)
                nc.scalar.wait_ge(ve.h, ve.n)
                ac.inc(nc.scalar.activation(sc1[:, 0:TT],
                                            kall[g][:, piece * TT:(piece + 1) * TT],
                                            AF.Square))
                nc.tensor.wait_ge(ac.h, ac.n)
                nc.tensor.wait_ge(ve.h, ve.n)
                pe.inc(nc.tensor.matmul(psS[0:1, :], ones[:, 0:1], sc1[:, 0:TT],
                                        start=True, stop=True))
                nc.scalar.wait_ge(pe.h, pe.n)
                ac.inc(nc.scalar.activation(rows[:, 0:TT], psS[0:1, :], AF.Sqrt))
                nc.vector.wait_ge(ac.h, ac.n)
                vch(nc.vector.reduce_max(rows[:, TT:TT + 1], rows[:, 0:TT], axis=AX.X))
                vch(nc.vector.tensor_max(kmax[:, g:g + 1], kmax[:, g:g + 1],
                                         rows[:, TT:TT + 1]))

        scale = 1.0 / math.sqrt(128.0)
        mask32 = [(i // 2 * 2) + (1 - i % 2) for i in range(32)]
        for qt in range(QTN):
            kts = list(range(4 * qt + 4)) + list(range(16, NKT))
            nkt = len(kts)
            for hp in range(8):
                nc.sync.wait_ge(pe.h, pe.n)
                for hi in range(2):
                    qld.inc(nc.sync.dma_start(out=qpr[:, hi * TT:(hi + 1) * TT],
                                              in_=qd[2 * hp + hi, :, qt * TT:(qt + 1) * TT]), 16)
                for hi, h in enumerate((2 * hp, 2 * hp + 1)):
                    g = h // 4
                    nc.scalar.wait_ge(qld.h, qld.n)
                    nc.scalar.wait_ge(pe.h, pe.n)
                    nc.scalar.wait_ge(ve.h, ve.n)
                    ac.inc(nc.scalar.activation(sc1[:, 0:TT], qpr[:, hi * TT:(hi + 1) * TT],
                                                AF.Square))
                    nc.tensor.wait_ge(ac.h, ac.n)
                    nc.tensor.wait_ge(ve.h, ve.n)
                    pe.inc(nc.tensor.matmul(psS[0:1, :], ones[:, 0:1], sc1[:, 0:TT],
                                            start=True, stop=True))
                    nc.scalar.wait_ge(pe.h, pe.n)
                    ac.inc(nc.scalar.activation(rows[:, 0:TT], psS[0:1, :], AF.Sqrt))
                    nc.vector.wait_ge(ac.h, ac.n)
                    vch(nc.vector.reduce_max(rows[:, TT:TT + 1], rows[:, 0:TT], axis=AX.X))
                    vch(nc.vector.tensor_mul(rows[:, TT:TT + 1], rows[:, TT:TT + 1],
                                             kmax[:, g:g + 1]))
                    vch(nc.vector.tensor_scalar_mul(rows[:, TT:TT + 1],
                                                    rows[:, TT:TT + 1], scale))
                    vch(nc.vector.tensor_scalar(rows[:, 2 * TT:2 * TT + NKT], dropr[:, :],
                                                -1e30, rows[:, TT:TT + 1],
                                                ALU.mult, ALU.subtract))
                    ve.inc(nc.vector.tensor_copy(rowsb[:, 0:NKT], rows[:, 2 * TT:2 * TT + NKT]))
                    nc.tensor.wait_ge(ve.h, ve.n)
                    pe.inc(nc.tensor.matmul(psB[:, 0:NKT], ones[0:1, :], rowsb[:, 0:NKT],
                                            start=True, stop=True))
                    nc.scalar.wait_ge(pe.h, pe.n)
                    ac.inc(nc.scalar.copy(biasT[:, hi * NKT:(hi + 1) * NKT], psB[:, 0:NKT]))

                vs_b = [vsl[0].n, vsl[1].n]
                pe_kt = pe.n

                def v_load(ki):
                    if ki >= 2:
                        nc.sync.wait_ge(pe.h, pe_kt + 8 * (ki - 1))
                    vsl[ki % 2].inc(nc.sync.dma_start(out=vtb[ki % 2][:, :],
                                                      in_=vd[kts[ki], :, :]), 16)
                v_load(0)
                if nkt > 1:
                    v_load(1)
                ac_kt = ac.n
                ve_kt = ve.n
                for ki, kt in enumerate(kts):
                    diag = kt < 16 and kt >= 4 * qt
                    nc.tensor.wait_ge(vsl[ki % 2].h, vs_b[ki % 2] + 16 * (ki // 2 + 1))
                    for hi, h in enumerate((2 * hp, 2 * hp + 1)):
                        g = h // 4
                        if ki > 0 or hi > 0:
                            nc.tensor.wait_ge(ac.h, ac_kt + 2 * ki + hi)
                        else:
                            nc.tensor.wait_ge(ac.h, ac_kt)
                        ins = nc.tensor.matmul(psS[:, :],
                                               kall[g][:, kt * 128:(kt + 1) * 128],
                                               qpr[:, hi * TT:(hi + 1) * TT],
                                               start=True, stop=True)
                        pe.inc(ins)
                        nc.scalar.wait_ge(pe.h, pe.n)
                        if ki >= 1:
                            nc.scalar.wait_ge(pe.h, pe_kt + 8 * ki)
                        ac.inc(nc.scalar.activation(
                            pexp[hi][:, :], psS[:, :], AF.Exp, scale=scale,
                            bias=biasT[:, hi * NKT + kt: hi * NKT + kt + 1]))
                        use_sem, use_n = ac, ac.n
                        if diag:
                            nc.vector.wait_ge(ac.h, ac.n)
                            ve.inc(nc.vector.tensor_tensor(
                                pexp[hi][:, :], pexp[hi][:, :],
                                dmask[:, (kt - 4 * qt) * TT:(kt - 4 * qt + 1) * TT], ALU.mult))
                            use_sem, use_n = ve, ve.n
                        nc.tensor.wait_ge(use_sem.h, use_n)
                        first, last = (ki == 0), (ki == nkt - 1)
                        pe.inc(nc.tensor.matmul(psO[2 * hi][:, :],
                                                vtb[ki % 2][:, g * 256:g * 256 + 128],
                                                pexp[hi][:, :], start=first, stop=last))
                        pe.inc(nc.tensor.matmul(psO[2 * hi + 1][:, :],
                                                vtb[ki % 2][:, g * 256 + 128:g * 256 + 256],
                                                pexp[hi][:, :], start=first, stop=last))
                        pe.inc(nc.tensor.matmul(psL[0:1, hi * TT:(hi + 1) * TT],
                                                ones[:, 0:1], pexp[hi][:, :],
                                                start=first, stop=last))
                    if ki + 2 < nkt:
                        v_load(ki + 2)
                for hi, h in enumerate((2 * hp, 2 * hp + 1)):
                    nc.vector.wait_ge(pe.h, pe.n)
                    vch(nc.vector.reciprocal(rows[:, 0:TT], psL[0:1, hi * TT:(hi + 1) * TT]))
                    ve.inc(nc.vector.tensor_copy(rowsb[:, 0:TT], rows[:, 0:TT]))
                    nc.tensor.wait_ge(ve.h, ve.n)
                    nc.tensor.wait_ge(ac.h, ac.n)
                    pe.inc(nc.tensor.matmul(psS[:, :], ones[0:1, :], rowsb[:, 0:TT],
                                            start=True, stop=True))
                    nc.scalar.wait_ge(pe.h, pe.n)
                    ac.inc(nc.scalar.copy(rlb[:, :], psS[:, :]))
                    nc.vector.wait_ge(ac.h, ac.n)
                    nc.gpsimd.wait_ge(ac.h, ac.n)
                    ve.inc(nc.vector.tensor_tensor(
                        oa[:, (h * 2) * TT:(h * 2 + 1) * TT],
                        psO[2 * hi][:, :], rlb[:, :], ALU.mult))
                    ve.inc(nc.vector.tensor_tensor(
                        oa[:, (h * 2 + 1) * TT:(h * 2 + 2) * TT],
                        psO[2 * hi + 1][:, :], rlb[:, :], ALU.mult))
                nc.tensor.wait_ge(ve.h, ve.n)

            ws_b = wsl[0].n
            pe_wo = pe.n

            def wo_load(dc):
                nc.gpsimd.wait_ge(pe.h, pe_wo + dc)
                wsl[0].inc(nc.gpsimd.dma_start(out=wob[:, :], in_=wos[dc, :, :]), 16)
            st_b = [stsl[0].n, stsl[1].n]
            xs_b = [xsl[0].n, xsl[1].n]
            for dc in range(DC):
                wo_load(dc)
                nc.tensor.wait_ge(wsl[0].h, ws_b + 16 * (dc + 1))
                if dc >= 1:
                    nc.tensor.wait_ge(ve.h, ve.n)
                if dc >= 2:
                    nc.sync.wait_ge(stsl[dc % 2].h, st_b[dc % 2] + 16 * (dc // 2))
                xsl[dc % 2].inc(nc.sync.dma_start(out=sf1[:, :] if dc % 2 == 0 else sf2[:, :],
                                                  in_=xloc[dc, :, qt * TT:(qt + 1) * TT]), 16)
                for hv in range(32):
                    ins = nc.tensor.matmul(psS[:, :], wob[:, hv * 128:(hv + 1) * 128],
                                           oa[:, hv * TT:(hv + 1) * TT],
                                           start=(hv == 0), stop=(hv == 31))
                pe.inc(ins)
                nc.vector.wait_ge(pe.h, pe.n)
                nc.vector.wait_ge(xsl[dc % 2].h, xs_b[dc % 2] + 16 * (dc // 2 + 1))
                if dc >= 2:
                    nc.vector.wait_ge(stsl[dc % 2].h, st_b[dc % 2] + 16 * (dc // 2))
                else:
                    nc.vector.wait_ge(stsl[dc % 2].h, st_b[dc % 2])
                ve.inc(nc.vector.tensor_add(xot[dc % 2][:, :], psS[:, :],
                                            sf1[:, :] if dc % 2 == 0 else sf2[:, :]))
                nc.scalar.wait_ge(ve.h, ve.n)
                stsl[dc % 2].inc(nc.scalar.dma_start(
                    out=xo[dc, :, qt * TT:(qt + 1) * TT], in_=xot[dc % 2][:, :]), 16)

        nc.sync.wait_ge(stsl[0].h, stsl[0].n)
        nc.sync.wait_ge(stsl[1].h, stsl[1].n)
        nc.vector.wait_ge(stsl[0].h, stsl[0].n)
        nc.vector.wait_ge(stsl[1].h, stsl[1].n)


ATTN_DRAM = dict(
    xloc=([DC, 128, T], "F32", "ExternalInput"),
    xrem=([DC, 128, T], "F32", "ExternalInput"),
    wdqs=([1, 128, 2048], "F32", "ExternalInput"),
    wdkvs=([2, 128, 2048], "F32", "ExternalInput"),
    wuqr=([128, 2048], "F32", "ExternalInput"),
    wukv=([128, 1280], "F32", "ExternalInput"),
    wos=([16, 128, 4096], "F32", "ExternalInput"),
    gq=([128, 1], "F32", "ExternalInput"),
    gkv=([128, 1], "F32", "ExternalInput"),
    qA=([128, T], "BF16", "ExternalInput"),
    qB=([128, T], "BF16", "ExternalInput"),
    kA=([64, 2 * T], "BF16", "ExternalInput"),
    kB=([64, 2 * T], "BF16", "ExternalInput"),
    dmaskc=([4, 128, TT], "BF16", "ExternalInput"),
    droprc=([1, 32], "F32", "ExternalInput"),
    onesc=([128, 128], "BF16", "ExternalInput"),
    xo=([DC, 128, T], "F32", "ExternalOutput"),
    qd=([16, 128, T], "BF16", None),
    vd=([32, 128, 1024], "BF16", None),
)


def declare_drams(nc, spec, prefix=""):
    aps = {}
    for nm, (shape, dtname, kind) in spec.items():
        dt_ = F32 if dtname == "F32" else BF16
        full = prefix + nm
        aps[nm] = (nc.dram_tensor(full, shape, dt_, kind=kind) if kind
                   else nc.dram_tensor(full, shape, dt_)).ap()
    return aps


def build_attn():
    nc = bass.Bass("TRN2", target_bir_lowering=False, debug=False)
    aps = declare_drams(nc, ATTN_DRAM)
    es = contextlib.ExitStack()
    with es:
        sm = make_sems(nc, es)
        emit_attn(nc, es, sm, **aps)
    return nc


def run_attn(h, xs):
    im = []
    for c in range(NCORES):
        half = c & 1
        im.append({"xloc": xs[c], "xrem": xs[c ^ 1], "wdqs": h["wdqs"],
                   "wdkvs": h["wdkvs"], "wuqr": h["wuqr"], "wukv": h["wukv"],
                   "wos": h["wos"], "gq": h["gq"], "gkv": h["gkv"],
                   "qA": h[f"qA{half}"], "qB": h[f"qB{half}"],
                   "kA": h[f"kA{half}"], "kB": h[f"kB{half}"],
                   "dmaskc": h["dmaskc"], "droprc": h[f"drop{half}"],
                   "onesc": h["onesc"]})
    res = _run("attn", build_attn, im)
    return [res.results[c]["xo"] for c in range(NCORES)]


PROJC = 67
CONVC = 34


def mamba_dram_spec(TM):
    return dict(
        xfm=([DC, 128, TM], "F32", "ExternalInput"),
        wins=([PROJC, 128, 2048], "F32", "ExternalInput"),
        wouts=([DC, 128, 4096], "F32", "ExternalInput"),
        convw=([128, CONVC * 4], "F32", "ExternalInput"),
        convbc=([128, CONVC], "F32", "ExternalInput"),
        nrmw=([128, 32], "F32", "ExternalInput"),
        dexp=([128, 32], "F32", "ExternalInput"),
        abias=([64, 2], "F32", "ExternalInput"),
        identb=([128, 128], "BF16", "ExternalInput"),
        identf=([128, 128], "F32", "ExternalInput"),
        onesc=([128, 128], "BF16", "ExternalInput"),
        onesf=([128, 128], "F32", "ExternalInput"),
        maskc=([128, 128], "BF16", "ExternalInput"),
        xo=([DC, 128, TM], "F32", "ExternalOutput"),
        projD=([PROJC, 128, TM], "BF16", None),
        convD=([CONVC, 128, TM], "BF16", None),
        ynD=([32, 128, TM], "BF16", None),
        crowD=([64, 128], "F32", None),
    )


def emit_mamba(nc, es, sm, TM, xfm, wins, wouts, convw, convbc, nrmw, dexp,
               abias, identb, identf, onesc, onesf, maskc, xo, projD, convD,
               ynD, crowD):
    NCH = TM // 128
    NT = TM // TT
    ar1 = es.enter_context(nc.sbuf_tensor(uq("ar1"), [128, 16384], BF16))
    ones = es.enter_context(nc.sbuf_tensor(uq("ones"), [128, 128], BF16))
    onf = es.enter_context(nc.sbuf_tensor(uq("onf"), [128, 128], F32))
    idb = es.enter_context(nc.sbuf_tensor(uq("idb"), [128, 128], BF16))
    idf = es.enter_context(nc.sbuf_tensor(uq("idf"), [128, 128], F32))
    mskb = es.enter_context(nc.sbuf_tensor(uq("mskb"), [128, 128], BF16))
    cwb = es.enter_context(nc.sbuf_tensor(uq("cwb"), [128, CONVC * 4], F32))
    cbb = es.enter_context(nc.sbuf_tensor(uq("cbb"), [128, CONVC], F32))
    nwb = es.enter_context(nc.sbuf_tensor(uq("nwb"), [128, 32], F32))
    dxb = es.enter_context(nc.sbuf_tensor(uq("dxb"), [128, 32], F32))
    abb = es.enter_context(nc.sbuf_tensor(uq("abb"), [64, 2], F32))
    winb = [es.enter_context(nc.sbuf_tensor(uq(f"winb{i}"), [128, 2048], BF16)) for i in range(2)]
    ptile = [es.enter_context(nc.sbuf_tensor(uq(f"ptile{i}"), [128, TT], BF16)) for i in range(2)]
    dgt = es.enter_context(nc.sbuf_tensor(uq("dgt"), [128, 4 * 128], BF16))
    cvt = es.enter_context(nc.sbuf_tensor(uq("cvt"), [128, TT + 3], BF16))
    S = es.enter_context(nc.sbuf_tensor(uq("Sst"), [128, 4096], F32))
    Sb = es.enter_context(nc.sbuf_tensor(uq("Sbst"), [128, 4096], BF16))
    d64 = es.enter_context(nc.sbuf_tensor(uq("d64"), [64, 128 * 8], F32))
    negA = es.enter_context(nc.sbuf_tensor(uq("negA"), [64, 1], F32))
    dtraw = es.enter_context(nc.sbuf_tensor(uq("dtraw"), [64, 128], BF16))
    dtpT = es.enter_context(nc.sbuf_tensor(uq("dtpT"), [128, 64], BF16))
    fT = es.enter_context(nc.sbuf_tensor(uq("fT"), [128, 64], BF16))
    cT = es.enter_context(nc.sbuf_tensor(uq("cT"), [128, 64], F32))
    crow = es.enter_context(nc.sbuf_tensor(uq("crow"), [1, 512], F32))
    crowE = es.enter_context(nc.sbuf_tensor(uq("crowE"), [1, 512], F32))
    grow = es.enter_context(nc.sbuf_tensor(uq("grow"), [1, 64], F32))
    epsT = es.enter_context(nc.sbuf_tensor(uq("epsT"), [1, 1], F32))
    Btile = es.enter_context(nc.sbuf_tensor(uq("Btile"), [128, 128], BF16))
    Ctile = es.enter_context(nc.sbuf_tensor(uq("Ctile"), [128, 128], BF16))
    Btb = es.enter_context(nc.sbuf_tensor(uq("Btb"), [128, 128], BF16))
    CBm = es.enter_context(nc.sbuf_tensor(uq("CBm"), [128, 128], BF16))
    dsb = es.enter_context(nc.sbuf_tensor(uq("dsb"), [128, 512], F32))
    cpbs = es.enter_context(nc.sbuf_tensor(uq("cpbs"), [128, 128], F32))
    tf1 = es.enter_context(nc.sbuf_tensor(uq("tf1"), [128, 128], F32))
    tf2 = es.enter_context(nc.sbuf_tensor(uq("tf2"), [128, 128], F32))
    gtl = [es.enter_context(nc.sbuf_tensor(uq(f"gtl{i}"), [128, 128], BF16)) for i in range(2)]
    sgt = es.enter_context(nc.sbuf_tensor(uq("sgt"), [128, 128], BF16))
    rrow = es.enter_context(nc.sbuf_tensor(uq("rrow"), [1, 128], F32))
    woutb = [es.enter_context(nc.sbuf_tensor(uq(f"woutb{i}"), [128, 4096], BF16)) for i in range(2)]
    xrt = [es.enter_context(nc.sbuf_tensor(uq(f"xrt{i}"), [128, TT], F32)) for i in range(2)]
    xot = [es.enter_context(nc.sbuf_tensor(uq(f"xot{i}"), [128, TT], F32)) for i in range(2)]

    pA = es.enter_context(nc.psum_tensor(uq("pA"), [128, TT], F32))
    pB = es.enter_context(nc.psum_tensor(uq("pB"), [128, TT], F32))
    pC = es.enter_context(nc.psum_tensor(uq("pC"), [128, TT], F32))
    pD = es.enter_context(nc.psum_tensor(uq("pD"), [128, TT], F32))
    pTb = es.enter_context(nc.psum_tensor(uq("pTb"), [128, TT], BF16))

    if True:
        sy, gp, pe, ac, ve, gv = (sm["sy"], sm["gp"], sm["pe"], sm["ac"],
                                  sm["ve"], sm["gv"])
        wsl, psl, gsl, xsl, stsl = (sm["wsl"], sm["psl"], sm["gsl"], sm["xsl"],
                                    sm["stsl"])
        cld, bnc = sm["cld"], sm["bnc"]
        ln = LNEmitter(nc, es, pA, pB, psr1=pC[0:1, :], psr2=pD[0:1, :],
                       xsq_buf=ar1[:, 0:DC * TT])
        xlnb = ar1[:, DC * TT:2 * DC * TT]
        sm = {"sy": sy, "pe": pe, "ac": ac, "ve": ve, "gv": gv}

        def vch(ins):
            ve.inc(ins)
            nc.vector.wait_ge(ve.h, ve.n)

        for t, d in [(ones, onesc), (onf, onesf), (idb, identb), (idf, identf),
                     (mskb, maskc), (cwb, convw), (cbb, convbc), (nwb, nrmw),
                     (dxb, dexp), (abb, abias)]:
            gp.inc(nc.gpsimd.dma_start(out=t[:, :], in_=d), 16)
        nc.tensor.wait_ge(gp.h, gp.n)
        nc.vector.wait_ge(gp.h, gp.n)
        nc.scalar.wait_ge(gp.h, gp.n)
        gv.inc(nc.gpsimd.memset(epsT[:, :], 1e-6))
        ac.inc(nc.scalar.activation(negA[:, :], abb[:, 1:2], AF.Exp))
        nc.vector.wait_ge(ac.h, ac.n)
        vch(nc.vector.tensor_scalar_mul(negA[:, :], negA[:, :], -1.0))

        for tt in range(NT):
            if tt > 0:
                nc.sync.wait_ge(gv.h, gv.n)
            ln.emit(sm, xfm, tt, lambda dc: xlnb[:, dc * TT:(dc + 1) * TT], ones)
            nc.tensor.wait_ge(gv.h, gv.n)
            ws_b = [wsl[0].n, wsl[1].n]
            ps_b = [psl[0].n, psl[1].n]
            pe_p = pe.n
            ac_p = ac.n

            def win_load(oc):
                nc.gpsimd.wait_ge(pe.h, pe_p + max(0, oc - 1))
                wsl[oc % 2].inc(nc.gpsimd.dma_start(out=winb[oc % 2][:, :],
                                                    in_=wins[oc, :, :]), 16)
            win_load(0)
            win_load(1)
            for oc in range(PROJC):
                pps = pA if oc % 2 == 0 else pB
                nc.tensor.wait_ge(wsl[oc % 2].h, ws_b[oc % 2] + 16 * (oc // 2 + 1))
                if oc >= 2:
                    nc.tensor.wait_ge(ac.h, ac_p + oc - 1)
                for dc in range(DC):
                    ins = nc.tensor.matmul(pps[:, :], winb[oc % 2][:, dc * 128:(dc + 1) * 128],
                                           xlnb[:, dc * TT:(dc + 1) * TT],
                                           start=(dc == 0), stop=(dc == DC - 1))
                pe.inc(ins)
                nc.scalar.wait_ge(pe.h, pe_p + oc + 1)
                nc.scalar.wait_ge(psl[oc % 2].h, ps_b[oc % 2] + 16 * (oc // 2))
                ac.inc(nc.scalar.copy(ptile[oc % 2][:, :], pps[:, :]))
                nc.sync.wait_ge(ac.h, ac.n)
                psl[oc % 2].inc(nc.sync.dma_start(
                    out=projD[oc, :, tt * TT:(tt + 1) * TT], in_=ptile[oc % 2][:, :]), 16)
                if oc + 2 < PROJC:
                    win_load(oc + 2)
        nc.gpsimd.wait_ge(psl[0].h, psl[0].n)
        nc.gpsimd.wait_ge(psl[1].h, psl[1].n)

        for c in range(CONVC):
            nc.vector.wait_ge(pe.h, pe.n)
            for k in range(4):
                ins = nc.vector.tensor_scalar(dgt[:, k * 128:(k + 1) * 128], idb[:, :],
                                              cwb[:, c * 4 + k:c * 4 + k + 1], None,
                                              ALU.mult)
            ve.inc(ins)
            ac_c = ac.n
            for tt in range(NT):
                nc.sync.wait_ge(ac.h, ac_c + max(0, tt - 1))
                if tt == 0:
                    sy.inc(nc.sync.dma_start(out=cvt[:, 3:TT + 3],
                                             in_=projD[32 + c, :, 0:TT]), 16)
                    nc.gpsimd.wait_ge(sy.h, sy.n)
                    gv.inc(nc.gpsimd.memset(cvt[:, 0:3], 0.0))
                else:
                    sy.inc(nc.sync.dma_start(
                        out=cvt[:, :],
                        in_=projD[32 + c, :, tt * TT - 3:(tt + 1) * TT]), 16)
                nc.tensor.wait_ge(sy.h, sy.n)
                nc.tensor.wait_ge(gv.h, gv.n)
                nc.tensor.wait_ge(ve.h, ve.n)
                nc.tensor.wait_ge(ac.h, ac_c + tt)
                for k in range(4):
                    ins = nc.tensor.matmul(pA[:, :], dgt[:, k * 128:(k + 1) * 128],
                                           cvt[:, k:k + TT],
                                           start=(k == 0), stop=(k == 3))
                pe.inc(ins)
                nc.scalar.wait_ge(pe.h, pe.n)
                nc.scalar.wait_ge(psl[tt % 2].h, psl[tt % 2].n)
                ac.inc(nc.scalar.activation(ptile[tt % 2][:, :], pA[:, :], AF.Silu,
                                            bias=cbb[:, c:c + 1]))
                nc.sync.wait_ge(ac.h, ac.n)
                psl[tt % 2].inc(nc.sync.dma_start(
                    out=convD[c, :, tt * TT:(tt + 1) * TT], in_=ptile[tt % 2][:, :]), 16)
                nc.scalar.wait_ge(psl[tt % 2].h, psl[tt % 2].n)
        nc.gpsimd.wait_ge(psl[0].h, psl[0].n)
        nc.gpsimd.wait_ge(psl[1].h, psl[1].n)

        xa = ar1[:, 12288:16384]
        sqsb = ar1[:, 8192:12288]
        sT = ar1[:, 0:8192]
        yg = ln.xt[:, 0:4096]
        Xp = ln.xt[:, 4096:6144].bitcast(BF16)
        Xpp = ln.xt[:, 6144:8192].bitcast(BF16)
        nc.vector.wait_ge(gv.h, gv.n)
        ve.inc(nc.vector.memset(S[:, :], 0.0))
        nc.sync.wait_ge(psl[0].h, psl[0].n)
        nc.sync.wait_ge(psl[1].h, psl[1].n)
        for ch in range(NCH):
            ccol = slice(ch * 128, (ch + 1) * 128)
            nc.sync.wait_ge(gv.h, gv.n)
            nc.sync.wait_ge(pe.h, pe.n)
            cl_b = cld.n
            for ic in range(32):
                cld.inc(nc.sync.dma_start(out=xa[:, ic * 128:(ic + 1) * 128],
                                          in_=convD[ic, :, ccol]), 16)
            cld.inc(nc.sync.dma_start(out=Btile[:, :], in_=convD[32, :, ccol]), 16)
            cld.inc(nc.sync.dma_start(out=Ctile[:, :], in_=convD[33, :, ccol]), 16)
            cld.inc(nc.sync.dma_start(out=dtraw[:, :], in_=projD[66, 0:64, ccol]), 16)
            zt = d64[:, 0:128]
            a1 = d64[:, 128:256]
            a2 = d64[:, 256:384]
            dtp = d64[:, 384:512]
            s_t = d64[:, 512:640]
            chm = d64[:, 640:768]
            fF = d64[:, 768:896]
            nc.vector.wait_ge(cld.h, cld.n)
            nc.gpsimd.wait_ge(cld.h, cld.n)
            vch(nc.vector.tensor_scalar(zt, dtraw[:, :], abb[:, 0:1], None, ALU.add))
            nc.scalar.wait_ge(ve.h, ve.n)
            ac.inc(nc.scalar.activation(a1, zt, AF.Abs))
            nc.scalar.wait_ge(ac.h, ac.n)
            ac.inc(nc.scalar.activation(a1, a1, AF.Exp, scale=-1.0))
            nc.scalar.wait_ge(ac.h, ac.n)
            ac.inc(nc.scalar.activation(a1, a1, AF.Ln, bias=1.0))
            ac.inc(nc.scalar.activation(a2, zt, AF.Relu))
            nc.vector.wait_ge(ac.h, ac.n)
            vch(nc.vector.tensor_add(dtp, a1, a2))
            vch(nc.vector.tensor_scalar_min(dtp, dtp, 1e6))
            vch(nc.vector.tensor_scalar(s_t, dtp, negA[:, :], None, ALU.mult))
            vch(nc.vector.tensor_tensor_scan(chm, s_t, s_t, 0.0, ALU.add, ALU.bypass))
            vch(nc.vector.tensor_scalar(fF, chm, chm[:, 127:128], None, ALU.subtract))
            nc.scalar.wait_ge(ve.h, ve.n)
            ac.inc(nc.scalar.activation(fF, fF, AF.Exp, scale=-1.0))
            nc.vector.wait_ge(ac.h, ac.n)
            nc.sync.wait_ge(ve.h, ve.n)
            nc.sync.wait_ge(bnc.h, bnc.n)
            bnc.inc(nc.sync.dma_start(out=crowD[:, :], in_=chm), 16)
            nc.sync.wait_ge(bnc.h, bnc.n)
            bnc.inc(nc.sync.dma_start(out=grow[:, :],
                                      in_=crowD[:, 127:128].transpose([1, 0]),
                                      allow_slow_non_contiguous=True), 16)
            nc.scalar.wait_ge(bnc.h, bnc.n)
            ac.inc(nc.scalar.activation(grow[:, :], grow[:, :], AF.Exp))
            nc.tensor.wait_ge(ve.h, ve.n)
            nc.tensor.wait_ge(ac.h, ac.n)
            pe.inc(nc.tensor.transpose(pC[:, 0:64], chm, idf[0:64, 0:64]))
            nc.scalar.wait_ge(pe.h, pe.n)
            ac.inc(nc.scalar.copy(cT[:, :], pC[:, 0:64]))
            nc.tensor.wait_ge(ac.h, ac.n)
            pe.inc(nc.tensor.transpose(pC[:, 0:64], dtp, idf[0:64, 0:64]))
            nc.scalar.wait_ge(pe.h, pe.n)
            ac.inc(nc.scalar.copy(dtpT[:, :], pC[:, 0:64]))
            nc.tensor.wait_ge(ac.h, ac.n)
            pe.inc(nc.tensor.transpose(pC[:, 0:64], fF, idf[0:64, 0:64]))
            nc.scalar.wait_ge(pe.h, pe.n)
            ac.inc(nc.scalar.copy(fT[:, :], pC[:, 0:64]))
            nc.tensor.wait_ge(ac.h, ac.n)
            pe.inc(nc.tensor.matmul(pC[:, 0:128], Btile[:, :], Ctile[:, :],
                                    start=True, stop=True))
            nc.vector.wait_ge(pe.h, pe.n)
            vch(nc.vector.tensor_tensor(CBm[:, :], pC[:, 0:128], mskb[:, :], ALU.mult))
            pe.inc(nc.tensor.transpose(pTb[:, 0:128], Btile[:, :], idb[:, :]))
            nc.scalar.wait_ge(pe.h, pe.n)
            ac.inc(nc.scalar.copy(Btb[:, :], pTb[:, 0:128]))
            nc.tensor.wait_ge(ac.h, ac.n)
            for grp in range(8):
                nc.tensor.wait_ge(ve.h, ve.n)
                for j in range(4):
                    ic = grp * 4 + j
                    ins = nc.tensor.transpose(pTb[:, j * 128:(j + 1) * 128],
                                              xa[:, ic * 128:(ic + 1) * 128], idb[:, :])
                pe.inc(ins)
                nc.vector.wait_ge(pe.h, pe.n)
                ve.inc(nc.vector.tensor_tensor(
                    Xp[:, grp * 512:(grp + 1) * 512].rearrange("p (h q) -> p h q", h=8),
                    pTb[:, 0:512].rearrange("p (h q) -> p h q", h=8),
                    dtpT[:, grp * 8:(grp + 1) * 8].unsqueeze(2).broadcast_to([128, 8, 64]), ALU.mult))
            nc.gpsimd.wait_ge(ve.h, ve.n)
            gv.inc(nc.gpsimd.tensor_tensor(
                Xpp[:, :].rearrange("p (h q) -> p h q", h=64),
                Xp[:, :].rearrange("p (h q) -> p h q", h=64),
                fT[:, :].unsqueeze(2).broadcast_to([128, 64, 64]),
                ALU.mult))
            npc = 16
            for pc_i in range(npc):
                hb = pc_i * 4
                nc.sync.wait_ge(pe.h, pe.n)
                bnc.inc(nc.sync.dma_start(out=crow[:, :],
                                          in_=crowD[hb:hb + 4, :].rearrange("a b -> (a b)").unsqueeze(0)), 16)
                nc.tensor.wait_ge(bnc.h, bnc.n)
                nc.tensor.wait_ge(ve.h, ve.n)
                pe.inc(nc.tensor.matmul(pA[:, :], onf[0:1, :], crow[:, :],
                                        start=True, stop=True))
                nc.vector.wait_ge(pe.h, pe.n)
                vch(nc.vector.tensor_tensor(
                    dsb[:, :].rearrange("p (h t) -> p h t", h=4),
                    pA[:, :].rearrange("p (h t) -> p h t", h=4),
                    cT[:, hb:hb + 4].unsqueeze(2).broadcast_to([128, 4, 128]),
                    ALU.subtract))
                vch(nc.vector.tensor_scalar_min(dsb[:, :], dsb[:, :], 0.0))
                nc.scalar.wait_ge(ve.h, ve.n)
                ac.inc(nc.scalar.activation(dsb[:, :], dsb[:, :], AF.Exp))
                nc.vector.wait_ge(ac.h, ac.n)
                vch(nc.vector.tensor_tensor(
                    sT[:, pc_i * 512:(pc_i + 1) * 512].rearrange("p (h t) -> p h t", h=4),
                    dsb[:, :].rearrange("p (h t) -> p h t", h=4),
                    CBm[:, :].unsqueeze(1).broadcast_to([128, 4, 128]),
                    ALU.mult))
            pe_y = pe.n
            gl_b = [gsl[0].n, gsl[1].n]
            ac_after_silu = {}

            def gate_load(pr):
                if pr >= 2:
                    nc.sync.wait_ge(ac.h, ac_after_silu[pr - 2])
                gsl[pr % 2].inc(nc.sync.dma_start(out=gtl[pr % 2][:, :],
                                                  in_=projD[pr, :, ccol]), 16)
            gate_load(0)
            gate_load(1)
            vch(nc.vector.tensor_copy(Sb[:, :], S[:, :]))
            for pr in range(32):
                if pr % 2 == 0:
                    nc.sync.wait_ge(pe.h, pe.n)
                    nc.sync.wait_ge(ac.h, ac.n)
                    bnc.inc(nc.sync.dma_start(
                        out=crow[:, :],
                        in_=crowD[pr * 2:pr * 2 + 4, :].rearrange("a b -> (a b)").unsqueeze(0)), 16)
                    nc.scalar.wait_ge(bnc.h, bnc.n)
                    ac.inc(nc.scalar.activation(crowE[:, :], crow[:, :], AF.Exp))
                sl = (pr % 2) * 2
                nc.tensor.wait_ge(ve.h, ve.n)
                nc.tensor.wait_ge(ac.h, ac.n)
                pe.inc(nc.tensor.matmul(pB[:, 0:128], Sb[:, pr * 128:(pr + 1) * 128],
                                        Ctile[:, :], start=True, stop=True))
                pe.inc(nc.tensor.matmul(pD[0:64, 0:128], onf[0:1, 0:64],
                                        crowE[:, sl * 128:(sl + 1) * 128],
                                        start=True, stop=True))
                pe.inc(nc.tensor.matmul(pD[64:128, 0:128], onf[0:1, 0:64],
                                        crowE[:, (sl + 1) * 128:(sl + 2) * 128],
                                        start=True, stop=True))
                pe.inc(nc.tensor.matmul(pC[0:64, 0:128],
                                        Xp[:, pr * 128:pr * 128 + 64],
                                        sT[:, (pr * 2) * 128:(pr * 2 + 1) * 128],
                                        start=True, stop=True))
                pe.inc(nc.tensor.matmul(pC[64:128, 0:128],
                                        Xp[:, pr * 128 + 64:(pr + 1) * 128],
                                        sT[:, (pr * 2 + 1) * 128:(pr * 2 + 2) * 128],
                                        start=True, stop=True))
                nc.scalar.wait_ge(pe.h, pe.n)
                ac.inc(nc.scalar.copy(cpbs[:, :], pD[:, 0:128]))
                nc.vector.wait_ge(ac.h, ac.n)
                nc.vector.wait_ge(gv.h, gv.n)
                vch(nc.vector.tensor_tensor(tf1[:, :], pB[:, 0:128], cpbs[:, :],
                                            ALU.mult))
                nc.vector.wait_ge(gsl[pr % 2].h, gl_b[pr % 2] + 16 * (pr // 2 + 1))
                vch(nc.vector.scalar_tensor_tensor(
                    tf2[:, :], xa[:, pr * 128:(pr + 1) * 128], dxb[:, pr:pr + 1],
                    tf1[:, :], ALU.mult, ALU.add))
                vch(nc.vector.tensor_add(tf1[:, :], pC[:, 0:128], tf2[:, :]))
                nc.scalar.wait_ge(gv.h, gv.n)
                nc.scalar.wait_ge(gsl[pr % 2].h, gl_b[pr % 2] + 16 * (pr // 2 + 1))
                ac.inc(nc.scalar.activation(sgt[:, :], gtl[pr % 2][:, :], AF.Silu))
                ac_after_silu[pr] = ac.n
                nc.gpsimd.wait_ge(ve.h, ve.n)
                nc.gpsimd.wait_ge(ac.h, ac.n)
                gv.inc(nc.gpsimd.tensor_mul(yg[:, pr * 128:(pr + 1) * 128],
                                            tf1[:, :], sgt[:, :]))
                if pr + 2 < 32:
                    gate_load(pr + 2)
            nc.gpsimd.wait_ge(gv.h, gv.n)
            for pc_i in range(8):
                cols = slice(pc_i * 512, (pc_i + 1) * 512)
                nc.tensor.wait_ge(ve.h, ve.n)
                pe.inc(nc.tensor.matmul(pA[:, :], Btb[:, :], Xpp[:, cols],
                                        start=True, stop=True))
                pe.inc(nc.tensor.matmul(pB[:, :], onf[0:1, :],
                                        grow[:, :].unsqueeze(2).broadcast_to([1, 64, 64])[:, pc_i * 8:(pc_i + 1) * 8, :],
                                        start=True, stop=True))
                nc.vector.wait_ge(pe.h, pe.n)
                vch(nc.vector.tensor_tensor(S[:, cols], S[:, cols], pB[:, :], ALU.mult))
                vch(nc.vector.tensor_tensor(S[:, cols], S[:, cols], pA[:, :], ALU.add))
            nc.tensor.wait_ge(gv.h, gv.n)
            nc.scalar.wait_ge(gv.h, gv.n)
            ac_n = ac.n
            ac.inc(nc.scalar.activation(sqsb[:, :], yg[:, :], AF.Square))
            nc.tensor.wait_ge(ac.h, ac.n)
            nc.tensor.wait_ge(ve.h, ve.n)
            for ic in range(32):
                ins = nc.tensor.matmul(pD[0:1, 0:128], ones[:, 0:1],
                                       sqsb[:, ic * 128:(ic + 1) * 128],
                                       start=(ic == 0), stop=(ic == 31))
            pe.inc(ins)
            nc.scalar.wait_ge(pe.h, pe.n)
            ac.inc(nc.scalar.activation(rrow[:, :], pD[0:1, 0:128], AF.Ln,
                                        scale=1.0 / 4096.0, bias=epsT[:, :]))
            nc.scalar.wait_ge(ac.h, ac.n)
            ac.inc(nc.scalar.activation(rrow[:, :], rrow[:, :], AF.Exp, scale=-0.5))
            nc.tensor.wait_ge(ac.h, ac.n)
            pe.inc(nc.tensor.matmul(pD[:, 0:128], onf[0:1, :], rrow[:, :],
                                    start=True, stop=True))
            st_b = [stsl[0].n, stsl[1].n]
            nc.vector.wait_ge(pe.h, pe.n)
            for ic in range(32):
                nc.vector.wait_ge(stsl[ic % 2].h, st_b[ic % 2] + 16 * (ic // 2))
                ve.inc(nc.vector.scalar_tensor_tensor(
                    ptile[ic % 2][:, 0:128], yg[:, ic * 128:(ic + 1) * 128],
                    nwb[:, ic:ic + 1], pD[:, 0:128], ALU.mult, ALU.mult))
                nc.scalar.wait_ge(ve.h, ve.n)
                stsl[ic % 2].inc(nc.scalar.dma_start(
                    out=ynD[ic, :, ccol], in_=ptile[ic % 2][:, 0:128]), 16)

        nc.gpsimd.wait_ge(stsl[0].h, stsl[0].n)
        nc.gpsimd.wait_ge(stsl[1].h, stsl[1].n)
        nc.sync.wait_ge(stsl[0].h, stsl[0].n)
        nc.sync.wait_ge(stsl[1].h, stsl[1].n)
        ynt = ar1[:, 0:16384]
        for tt in range(NT):
            yl_b = cld.n
            nc.sync.wait_ge(pe.h, pe.n)
            for ic in range(32):
                cld.inc(nc.sync.dma_start(out=ynt[:, ic * TT:(ic + 1) * TT],
                                          in_=ynD[ic, :, tt * TT:(tt + 1) * TT]), 16)
            ws_b = [wsl[0].n, wsl[1].n]
            xs_b = [xsl[0].n, xsl[1].n]
            st_b = [stsl[0].n, stsl[1].n]
            pe_w = pe.n

            def wout_load(dc):
                nc.gpsimd.wait_ge(pe.h, pe_w + max(0, dc - 1))
                wsl[dc % 2].inc(nc.gpsimd.dma_start(out=woutb[dc % 2][:, :],
                                                    in_=wouts[dc, :, :]), 16)
            wout_load(0)
            wout_load(1)
            ve_w = ve.n
            for dc in range(DC):
                nc.tensor.wait_ge(wsl[dc % 2].h, ws_b[dc % 2] + 16 * (dc // 2 + 1))
                nc.tensor.wait_ge(cld.h, cld.n)
                nc.tensor.wait_ge(ve.h, ve_w + dc)
                nc.sync.wait_ge(stsl[dc % 2].h, st_b[dc % 2] + 16 * (dc // 2))
                xsl[dc % 2].inc(nc.sync.dma_start(
                    out=xrt[dc % 2][:, :], in_=xfm[dc, :, tt * TT:(tt + 1) * TT]), 16)
                for ic in range(32):
                    ins = nc.tensor.matmul(pA[:, :], woutb[dc % 2][:, ic * 128:(ic + 1) * 128],
                                           ynt[:, ic * TT:(ic + 1) * TT],
                                           start=(ic == 0), stop=(ic == 31))
                pe.inc(ins)
                nc.vector.wait_ge(pe.h, pe.n)
                nc.vector.wait_ge(xsl[dc % 2].h, xs_b[dc % 2] + 16 * (dc // 2 + 1))
                if dc >= 2:
                    nc.vector.wait_ge(stsl[dc % 2].h, st_b[dc % 2] + 16 * (dc // 2))
                ve.inc(nc.vector.tensor_add(xot[dc % 2][:, :], pA[:, :], xrt[dc % 2][:, :]))
                nc.scalar.wait_ge(ve.h, ve.n)
                stsl[dc % 2].inc(nc.scalar.dma_start(
                    out=xo[dc, :, tt * TT:(tt + 1) * TT], in_=xot[dc % 2][:, :]), 16)
                if dc + 2 < DC:
                    wout_load(dc + 2)
        nc.sync.wait_ge(stsl[0].h, stsl[0].n)
        nc.sync.wait_ge(stsl[1].h, stsl[1].n)
        nc.vector.wait_ge(stsl[0].h, stsl[0].n)
        nc.vector.wait_ge(stsl[1].h, stsl[1].n)


def build_mamba(TM=4096):
    nc = bass.Bass("TRN2", target_bir_lowering=False, debug=False)
    aps = declare_drams(nc, mamba_dram_spec(TM))
    es = contextlib.ExitStack()
    with es:
        sm = make_sems(nc, es)
        emit_mamba(nc, es, sm, TM, **aps)
    return nc


def run_mamba(h, xs):
    full = [np.concatenate([xs[2 * b], xs[2 * b + 1]], axis=2) for b in range(4)]
    im = []
    for c in range(NCORES):
        b = c % 4
        im.append({"xfm": full[b], "wins": h["wins"], "wouts": h["wouts"],
                   "convw": h["convw"], "convbc": h["convbc"], "nrmw": h["nrmw"],
                   "dexp": h["dexp"], "abias": h["abias"], "identb": h["identb"],
                   "identf": h["identf"], "onesc": h["onesc"], "onesf": h["onesf"],
                   "maskc": h["maskc"]})
    res = _run("mamba", build_mamba, im)
    out = []
    for c in range(NCORES):
        b, hh = c >> 1, c & 1
        out.append(np.ascontiguousarray(res.results[b]["xo"][:, :, hh * T:(hh + 1) * T]))
    return out


FFN_DRAM = dict(
    w1s=([64, 128, 2048], "F32", "ExternalInput"),
    w3s=([64, 128, 2048], "F32", "ExternalInput"),
    onesc=([128, 128], "BF16", "ExternalInput"),
)
MOE_DRAM = dict(
    rts=([1, 128, 2048], "F32", "ExternalInput"),
    wgs=([NE, HC_E, 128, 2048], "F32", "ExternalInput"),
    wus=([NE, HC_E, 128, 2048], "F32", "ExternalInput"),
    wds=([NE, DC, 128, 1408], "F32", "ExternalInput"),
    sgs=([HC_S, 128, 2048], "F32", "ExternalInput"),
    sus=([HC_S, 128, 2048], "F32", "ExternalInput"),
    sds=([DC, 128, 2816], "F32", "ExternalInput"),
    onesc=([128, 128], "BF16", "ExternalInput"),
    identf=([128, 128], "F32", "ExternalInput"),
    selc=([8, 8 * 128], "BF16", "ExternalInput"),
)
PREP_DRAM = dict(
    w2t=([8192, 2048], "F32", "ExternalInput"),
    h128=([128, 128], "F32", "ExternalInput"),
    h64=([64, 64], "F32", "ExternalInput"),
    w2ps=([16, 128, 8192], "BF16", None),
    t1d=([64, 128, 2048], "F32", None),
)


def emit_halfsel(nc, es, sm, src_full, dst, hs, flip=False):
    sy, ve, gv = sm["sy"], sm["ve"], sm["gv"]
    t0 = es.enter_context(nc.sbuf_tensor(uq("hs_t0"), [128, TT], F32))
    t1 = es.enter_context(nc.sbuf_tensor(uq("hs_t1"), [128, TT], F32))
    t2 = es.enter_context(nc.sbuf_tensor(uq("hs_t2"), [128, TT], F32))
    hsb = es.enter_context(nc.sbuf_tensor(uq("hs_m"), [128, 2], F32))
    st = sm["stsl"][0]
    sy.inc(nc.sync.dma_start(out=hsb[:, :], in_=hs), 16)
    c0, c1 = (1, 0) if flip else (0, 1)
    for dc in range(DC):
        for tt in range(NTT):
            nc.sync.wait_ge(st.h, st.n)
            sy.inc(nc.sync.dma_start(
                out=t0[:, :], in_=src_full[dc, :, tt * TT:(tt + 1) * TT]), 16)
            sy.inc(nc.sync.dma_start(
                out=t1[:, :], in_=src_full[dc, :, T + tt * TT:T + (tt + 1) * TT]), 16)
            nc.vector.wait_ge(sy.h, sy.n)
            ve.inc(nc.vector.tensor_scalar(t2[:, :], t0[:, :], hsb[:, c0:c0 + 1],
                                           None, ALU.mult))
            nc.vector.wait_ge(ve.h, ve.n)
            ve.inc(nc.vector.scalar_tensor_tensor(
                t2[:, :], t1[:, :], hsb[:, c1:c1 + 1], t2[:, :], ALU.mult, ALU.add))
            nc.sync.wait_ge(ve.h, ve.n)
            st.inc(nc.sync.dma_start(
                out=dst[dc, :, tt * TT:(tt + 1) * TT], in_=t2[:, :]), 16)
    nc.sync.wait_ge(st.h, st.n)
    nc.gpsimd.wait_ge(st.h, st.n)


def build_fused():
    nc = bass.Bass("TRN2", target_bir_lowering=False, debug=False)
    xin = nc.dram_tensor("xin", [DC, 128, T], F32, kind="ExternalInput").ap()
    xfinal = nc.dram_tensor("xfinal", [DC, 128, T], F32, kind="ExternalOutput").ap()
    hs1 = nc.dram_tensor("hs1", [128, 2], F32, kind="ExternalInput").ap()
    hs2 = nc.dram_tensor("hs2", [128, 2], F32, kind="ExternalInput").ap()
    xa = nc.dram_tensor("xa", [DC, 128, T], F32).ap()
    xb = nc.dram_tensor("xb", [DC, 128, T], F32).ap()
    xc = nc.dram_tensor("xc", [DC, 128, T], F32).ap()
    xd = nc.dram_tensor("xd", [DC, 128, T], F32).ap()
    xe = nc.dram_tensor("xe", [DC, 128, T], F32).ap()
    g1out = nc.dram_tensor("g1out", [2 * D, T], F32).ap()
    g2out = nc.dram_tensor("g2out", [2 * D, T], F32).ap()
    mxfm = nc.dram_tensor("mxfm", [DC, 128, 2 * T], F32).ap()
    mxo = nc.dram_tensor("mxo", [DC, 128, 2 * T], F32).ap()
    axrem = nc.dram_tensor("axrem", [DC, 128, T], F32).ap()

    p_aps = declare_drams(nc, PREP_DRAM, "p_")
    f_aps = declare_drams(nc, FFN_DRAM, "f_")
    spec_m = mamba_dram_spec(2 * T)
    del spec_m["xfm"]
    del spec_m["xo"]
    spec_m = {k: (v[0], v[1], v[2]) for k, v in spec_m.items()}
    m_aps = declare_drams(nc, spec_m, "m_")
    e_aps = declare_drams(nc, MOE_DRAM, "e_")
    spec_a = dict(ATTN_DRAM)
    del spec_a["xloc"]
    del spec_a["xrem"]
    del spec_a["xo"]
    a_aps = declare_drams(nc, spec_a, "a_")

    es0 = contextlib.ExitStack()
    with es0:
        sm = make_sems(nc, es0)
        cc = S_(nc, es0, "cc")

        def bar():
            nc.all_engine_barrier()

        def gather(src, out):
            cc.inc(nc.gpsimd.collective_compute(
                "AllGather", ALU.bypass,
                replica_groups=[[0, 1], [2, 3], [4, 5], [6, 7]],
                ins=[src.rearrange("c p t -> (c p) t")],
                outs=[out]), 1)
            nc.gpsimd.wait_ge(cc.h, cc.n)
            bar()

        with contextlib.ExitStack() as es:
            emit_prep(nc, es, sm, p_aps["w2t"], p_aps["h128"], p_aps["h64"],
                      p_aps["w2ps"], p_aps["t1d"])
        bar()
        with contextlib.ExitStack() as es:
            emit_ffn(nc, es, sm, xin, f_aps["w1s"], f_aps["w3s"], p_aps["w2ps"],
                     f_aps["onesc"], xa)
        bar()
        gather(xa, g1out)
        sm["sy"].inc(nc.sync.dma_start(
            out=mxfm.rearrange("c p (r t) -> (c p) r t", r=2),
            in_=g1out.rearrange("(r f) t -> f r t", r=2)), 16)
        nc.gpsimd.wait_ge(sm["sy"].h, sm["sy"].n)
        bar()
        with contextlib.ExitStack() as es:
            emit_mamba(nc, es, sm, 2 * T, mxfm, **m_aps, xo=mxo)
        bar()
        with contextlib.ExitStack() as es:
            emit_halfsel(nc, es, sm, mxo, xb, hs1)
        bar()
        with contextlib.ExitStack() as es:
            emit_ffn(nc, es, sm, xb, f_aps["w1s"], f_aps["w3s"], p_aps["w2ps"],
                     f_aps["onesc"], xc)
        bar()
        with contextlib.ExitStack() as es:
            emit_moe(nc, es, sm, xc, e_aps["rts"], e_aps["wgs"], e_aps["wus"],
                     e_aps["wds"], e_aps["sgs"], e_aps["sus"], e_aps["sds"],
                     e_aps["onesc"], e_aps["identf"], e_aps["selc"], xd)
        bar()
        gather(xd, g2out)
        sm["sy"].inc(nc.sync.dma_start(
            out=mxfm.rearrange("c p (r t) -> (c p) r t", r=2),
            in_=g2out.rearrange("(r f) t -> f r t", r=2)), 16)
        nc.gpsimd.wait_ge(sm["sy"].h, sm["sy"].n)
        nc.sync.wait_ge(sm["sy"].h, sm["sy"].n)
        bar()
        with contextlib.ExitStack() as es:
            emit_halfsel(nc, es, sm, mxfm, axrem, hs2, flip=True)
        bar()
        with contextlib.ExitStack() as es:
            emit_attn(nc, es, sm, xd, axrem, a_aps["wdqs"], a_aps["wdkvs"],
                      a_aps["wuqr"], a_aps["wukv"], a_aps["wos"], a_aps["gq"],
                      a_aps["gkv"], a_aps["qA"], a_aps["qB"], a_aps["kA"],
                      a_aps["kB"], a_aps["dmaskc"], a_aps["droprc"],
                      a_aps["onesc"], xe, a_aps["qd"], a_aps["vd"])
        bar()
        with contextlib.ExitStack() as es:
            emit_moe(nc, es, sm, xe, e_aps["rts"], e_aps["wgs"], e_aps["wus"],
                     e_aps["wds"], e_aps["sgs"], e_aps["sus"], e_aps["sds"],
                     e_aps["onesc"], e_aps["identf"], e_aps["selc"], xfinal)
        bar()
    return nc


def run_fused(h, inputs):
    xs = shard_x(np.asarray(inputs["L0_attn_hidden_states"], np.float32))
    im = []
    for c in range(NCORES):
        half = c & 1
        hs = np.zeros((128, 2), np.float32)
        hs[:, 0] = 1.0 - half
        hs[:, 1] = half
        d = {"xin": xs[c], "hs1": hs, "hs2": hs,
             "p_w2t": h["w2t"], "p_h128": h["h128"], "p_h64": h["h64"],
             "f_w1s": h["w1s"], "f_w3s": h["w3s"], "f_onesc": h["onesc"],
             "m_wins": h["wins"], "m_wouts": h["wouts"], "m_convw": h["convw"],
             "m_convbc": h["convbc"], "m_nrmw": h["nrmw"], "m_dexp": h["dexp"],
             "m_abias": h["abias"], "m_identb": h["identb"],
             "m_identf": h["identf"], "m_onesc": h["onesc"],
             "m_onesf": h["onesf"], "m_maskc": h["maskc"],
             "e_rts": h["rts"], "e_wgs": h["wgs"], "e_wus": h["wus"],
             "e_wds": h["wds"], "e_sgs": h["sgs"], "e_sus": h["sus"],
             "e_sds": h["sds"], "e_onesc": h["onesc"], "e_identf": h["identf"],
             "e_selc": h["selc"],
             "a_wdqs": h["wdqs"], "a_wdkvs": h["wdkvs"], "a_wuqr": h["wuqr"],
             "a_wukv": h["wukv"], "a_wos": h["wos"], "a_gq": h["gq"],
             "a_gkv": h["gkv"], "a_qA": h[f"qA{half}"], "a_qB": h[f"qB{half}"],
             "a_kA": h[f"kA{half}"], "a_kB": h[f"kB{half}"],
             "a_dmaskc": h["dmaskc"], "a_droprc": h[f"drop{half}"],
             "a_onesc": h["onesc"]}
        im.append(d)
    res = _run("fused", build_fused, im)
    return unshard_x([res.results[c]["xfinal"] for c in range(NCORES)])


def kernel(**inputs):
    h = _prep_host(inputs)
    return run_fused(h, inputs)


def kernel_unfused(**inputs):
    return _kernel_unfused_body(**inputs)


# version 05 (final)
# speedup vs baseline: 1.2090x; 1.2090x over previous
import math
import contextlib
import numpy as np
import ml_dtypes
import concourse.bass as bass
import concourse.mybir as mybir
from concourse.bass_utils import run_bass_kernel_spmd

F32 = mybir.dt.float32
BF16 = mybir.dt.bfloat16
AF = mybir.ActivationFunctionType
ALU = mybir.AluOpType
AX = mybir.AxisListType
BF = ml_dtypes.bfloat16

NCORES = 8
D = 2048
DC = D // 128
T = 2048
TT = 512
NTT = T // TT


def tile_w(w):
    K, M = w.shape
    assert K % 128 == 0
    KC = K // 128
    MC = (M + 127) // 128
    if M % 128:
        w = np.concatenate([w, np.zeros((K, MC * 128 - M), w.dtype)], axis=1)
    return np.ascontiguousarray(
        w.reshape(KC, 128, MC, 128).transpose(2, 1, 0, 3).reshape(MC, 128, KC * 128)
    )


def fm_chunks(x):
    t, d = x.shape
    return np.ascontiguousarray(x.T.reshape(d // 128, 128, t))


def fm_to_tokmajor(xc):
    dc, p, t = xc.shape
    return np.ascontiguousarray(xc.reshape(dc * p, t).T)


def hadamard(n):
    h = np.array([[1.0]], np.float32)
    while h.shape[0] < n:
        h = np.block([[h, h], [h, -h]])
    return h



_UNIQ = [0]


def uq(name):
    _UNIQ[0] += 1
    return f"{name}_{_UNIQ[0]}"


class S_:

    def __init__(self, nc, es, name):
        self.h = es.enter_context(nc.semaphore(name))
        self.n = 0

    def inc(self, instr, k=1):
        instr.then_inc(self.h, k)
        self.n += k
        return instr

    def wait(self, eng, n=None):
        eng.wait_ge(self.h, self.n if n is None else n)


def emit_prep(nc, es, sm, w2t, h128, h64, w2ps, t1d):
    h128b = es.enter_context(nc.sbuf_tensor(uq("h128b"), [128, 128], BF16))
    h64b = es.enter_context(nc.sbuf_tensor(uq("h64b"), [64, 64], BF16))
    blk = [es.enter_context(nc.sbuf_tensor(uq(f"blk{i}"), [128, 2048], BF16)) for i in range(2)]
    ev1 = [es.enter_context(nc.sbuf_tensor(uq(f"ev1_{i}"), [128, 2048], BF16)) for i in range(2)]
    rhs2 = [es.enter_context(nc.sbuf_tensor(uq(f"rhs2_{i}"), [64, 512], BF16)) for i in range(2)]
    ev2 = [es.enter_context(nc.sbuf_tensor(uq(f"ev2_{i}"), [64, 512], BF16)) for i in range(2)]
    ps1x = [es.enter_context(nc.psum_tensor(uq(f"ps1_{i}"), [128, 512], F32)) for i in range(2)]
    ps2x = [es.enter_context(nc.psum_tensor(uq(f"ps2_{i}"), [64, 512], F32)) for i in range(2)]
    if True:
        gp, pe, ac, sy = sm["gp"], sm["pe"], sm["ac"], sm["sy"]
        vst, dsl = sm["vst"], sm["dsl"]

        nc.gpsimd.dma_start(out=h128b[:, :], in_=h128[:, :]).then_inc(gp.h, 16)
        gp.n += 16
        nc.gpsimd.dma_start(out=h64b[:, :], in_=h64[:, :]).then_inc(gp.h, 16)
        gp.n += 16
        gp_hs = gp.n

        p1_pe = pe.n
        p1_ac = ac.n
        ld1 = [0] * 64
        st1 = [0] * 64

        def p1_load(i1):
            b2 = i1 % 2
            if i1 >= 2:
                nc.gpsimd.wait_ge(pe.h, p1_pe + 4 * (i1 - 1))
            gp.inc(nc.gpsimd.dma_start(
                out=blk[b2][:, :], in_=w2t[i1 * 128:(i1 + 1) * 128, :]), 16)
            ld1[i1] = gp.n

        p1_load(0)
        p1_load(1)
        for i1 in range(64):
            b = i1 % 2
            nc.tensor.wait_ge(gp.h, ld1[i1])
            for dt in range(4):
                j = 4 * i1 + dt
                if j >= 2:
                    nc.tensor.wait_ge(ac.h, p1_ac + j - 1)
                pe.inc(nc.tensor.matmul(
                    ps1x[j % 2][:, :], h128b[:, :], blk[b][:, dt * 512:(dt + 1) * 512],
                    start=True, stop=True))
                nc.scalar.wait_ge(pe.h, p1_pe + j + 1)
                if dt == 0 and i1 >= 2:
                    ssem = sy if b == 0 else dsl[0]
                    nc.scalar.wait_ge(ssem.h, st1[i1 - 2])
                ac.inc(nc.scalar.copy(ev1[b][:, dt * 512:(dt + 1) * 512],
                                      ps1x[j % 2][:, :]))
            ssem, seng = (sy, nc.sync) if b == 0 else (dsl[0], nc.scalar)
            seng.wait_ge(ac.h, p1_ac + 4 * i1 + 4)
            ssem.inc(seng.dma_start(out=t1d[i1, :, :], in_=ev1[b][:, :]), 16)
            st1[i1] = ssem.n
            if i1 + 2 < 64:
                p1_load(i1 + 2)

        scale = 1.0 / math.sqrt(8192.0)
        nc.gpsimd.wait_ge(sy.h, sy.n)
        nc.gpsimd.wait_ge(dsl[0].h, dsl[0].n)
        nc.vector.wait_ge(sy.h, sy.n)
        nc.vector.wait_ge(dsl[0].h, dsl[0].n)
        p2_pe = pe.n
        p2_ac = ac.n
        ld2 = [0] * 512
        st2 = [0] * 512

        def p2_load(k):
            b2 = k % 2
            j2k, dtk = divmod(k, 4)
            eng, sem = (nc.gpsimd, gp) if b2 == 0 else (nc.sync, vst)
            if k >= 2:
                eng.wait_ge(pe.h, p2_pe + k - 1)
            sem.inc(eng.dma_start(
                out=rhs2[b2][:, :],
                in_=t1d[:, j2k, dtk * 512:(dtk + 1) * 512]), 16)
            ld2[k] = sem.n

        p2_load(0)
        p2_load(1)
        for k in range(512):
            b = k % 2
            j2, dt = divmod(k, 4)
            sem = gp if b == 0 else vst
            nc.tensor.wait_ge(sem.h, ld2[k])
            if k >= 2:
                nc.tensor.wait_ge(ac.h, p2_ac + k - 1)
            pe.inc(nc.tensor.matmul(ps2x[b][:, :], h64b[:, :], rhs2[b][:, :],
                                    start=True, stop=True))
            if k + 2 < 512:
                p2_load(k + 2)
            nc.scalar.wait_ge(pe.h, p2_pe + k + 1)
            if k >= 2:
                ssem = sy if b == 0 else dsl[1]
                nc.scalar.wait_ge(ssem.h, st2[k - 2])
            ac.inc(nc.scalar.activation(ev2[b][:, :], ps2x[b][:, :], AF.Copy,
                                        scale=scale))
            dst = w2ps[dt * 4:(dt + 1) * 4, j2, :].rearrange(
                "dcc (j1 q) -> j1 dcc q", q=128)
            src = ev2[b][:, :].rearrange("j1 (dcc q) -> j1 dcc q", q=128)
            ssem, seng = (sy, nc.sync) if b == 0 else (dsl[1], nc.scalar)
            seng.wait_ge(ac.h, p2_ac + k + 1)
            ssem.inc(seng.dma_start(out=dst, in_=src), 16)
            st2[k] = ssem.n
        nc.sync.wait_ge(sy.h, sy.n)
        nc.sync.wait_ge(dsl[1].h, dsl[1].n)
        nc.vector.wait_ge(sy.h, sy.n)
        nc.vector.wait_ge(dsl[1].h, dsl[1].n)
        nc.gpsimd.wait_ge(dsl[1].h, dsl[1].n)


def make_sems(nc, es):
    sm = {}
    for name in ["sy", "gp", "pe", "ac", "ve", "gv", "st", "vst", "qst", "qld",
                 "cld", "bnc"]:
        sm[name] = S_(nc, es, name)
    for name in ["wsl", "w2sl", "stsl", "dsl", "osl", "vsl", "xsl", "psl", "gsl"]:
        sm[name] = [S_(nc, es, f"{name}{i}") for i in range(2)]
    return sm


def build_prep():
    nc = bass.Bass("TRN2", target_bir_lowering=False, debug=False)
    w2t = nc.dram_tensor("w2t", [8192, 2048], F32, kind="ExternalInput").ap()
    h128 = nc.dram_tensor("h128", [128, 128], F32, kind="ExternalInput").ap()
    h64 = nc.dram_tensor("h64", [64, 64], F32, kind="ExternalInput").ap()
    w2ps = nc.dram_tensor("w2ps", [16, 128, 8192], BF16, kind="ExternalOutput").ap()
    t1d = nc.dram_tensor("t1d", [64, 128, 2048], BF16).ap()
    es = contextlib.ExitStack()
    with es:
        sm = make_sems(nc, es)
        emit_prep(nc, es, sm, w2t, h128, h64, w2ps, t1d)
    return nc


def emit_ffn(nc, es, sm, xfm, w1s, w3s, w2ps, onesc, xo):
    ones = es.enter_context(nc.sbuf_tensor(uq("ones"), [128, 128], BF16))
    xt = es.enter_context(nc.sbuf_tensor(uq("xt"), [128, DC * TT], F32))
    xsq = es.enter_context(nc.sbuf_tensor(uq("xsq"), [128, DC * TT], BF16))
    xln = es.enter_context(nc.sbuf_tensor(uq("xln"), [128, DC * TT], BF16))
    w1b = [es.enter_context(nc.sbuf_tensor(uq(f"w1b{i}"), [128, 2048], BF16)) for i in range(2)]
    w3b = [es.enter_context(nc.sbuf_tensor(uq(f"w3b{i}"), [128, 2048], BF16)) for i in range(2)]
    w2b = [es.enter_context(nc.sbuf_tensor(uq(f"w2b{i}"), [128, 8192], BF16)) for i in range(2)]
    hid = es.enter_context(nc.sbuf_tensor(uq("hid"), [128, 64 * TT], BF16))
    gtmp = es.enter_context(nc.sbuf_tensor(uq("gtmp"), [128, TT], BF16))
    rows = es.enter_context(nc.sbuf_tensor(uq("rows"), [1, TT * 5], F32))
    rowsb = es.enter_context(nc.sbuf_tensor(uq("rowsb"), [1, TT * 2], BF16))
    mb = es.enter_context(nc.sbuf_tensor(uq("mb"), [128, TT], F32))
    rb = es.enter_context(nc.sbuf_tensor(uq("rb"), [128, TT], F32))
    t32 = [es.enter_context(nc.sbuf_tensor(uq(f"t32_{i}"), [128, TT], F32)) for i in range(2)]
    xot = [es.enter_context(nc.sbuf_tensor(uq(f"xot{i}"), [128, TT], F32)) for i in range(2)]

    psG = es.enter_context(nc.psum_tensor(uq("psG"), [128, TT], F32))
    psU = es.enter_context(nc.psum_tensor(uq("psU"), [128, TT], F32))
    psO = es.enter_context(nc.psum_tensor(uq("psO"), [128, TT], F32))
    psR1 = es.enter_context(nc.psum_tensor(uq("psR1"), [1, TT], F32))
    psR2 = es.enter_context(nc.psum_tensor(uq("psR2"), [1, TT], F32))

    if True:
        sy, gp, pe, ac, ve, gv = (sm["sy"], sm["gp"], sm["pe"], sm["ac"],
                                  sm["ve"], sm["gv"])
        wsl, w2sl, stsl = sm["wsl"], sm["w2sl"], sm["stsl"]

        gp.inc(nc.gpsimd.dma_start(out=ones[:, :], in_=onesc[:, :]), 16)

        for tt in range(NTT):
            if tt > 0:
                nc.sync.wait_ge(ve.h, ve.n)
            for dc in range(DC):
                sy.inc(nc.sync.dma_start(
                    out=xt[:, dc * TT:(dc + 1) * TT],
                    in_=xfm[dc, :, tt * TT:(tt + 1) * TT]), 16)
            nc.vector.wait_ge(sy.h, sy.n)
            ve.inc(nc.vector.tensor_copy(xln[:, :], xt[:, :]))
            nc.scalar.wait_ge(sy.h, sy.n)
            ac.inc(nc.scalar.activation(xsq[:, :], xt[:, :], AF.Square))
            nc.tensor.wait_ge(ve.h, ve.n)
            nc.tensor.wait_ge(ac.h, ac.n)
            nc.tensor.wait_ge(gp.h, 16)
            for dc in range(DC):
                nc.tensor.matmul(psR1[:, :], ones[:, 0:1],
                                 xln[:, dc * TT:(dc + 1) * TT],
                                 start=(dc == 0), stop=(dc == DC - 1))
            for dc in range(DC):
                ins = nc.tensor.matmul(psR2[:, :], ones[:, 0:1],
                                       xsq[:, dc * TT:(dc + 1) * TT],
                                       start=(dc == 0), stop=(dc == DC - 1))
            pe.inc(ins)
            nc.vector.wait_ge(pe.h, pe.n)
            r_mean = rows[:, 0:TT]
            r_msq = rows[:, TT:2 * TT]
            r_var = rows[:, 2 * TT:3 * TT]
            r_std = rows[:, 3 * TT:4 * TT]
            r_rstd = rows[:, 4 * TT:5 * TT]
            r_mbr = rows[:, 3 * TT:4 * TT]

            def vch(ins):
                ve.inc(ins)
                nc.vector.wait_ge(ve.h, ve.n)
            vch(nc.vector.tensor_scalar_mul(r_mean, psR1[:, :], 1.0 / D))
            vch(nc.vector.tensor_scalar_mul(r_msq, psR2[:, :], 1.0 / D))
            vch(nc.vector.tensor_mul(r_var, r_mean, r_mean))
            vch(nc.vector.tensor_sub(r_var, r_msq, r_var))
            ve.inc(nc.vector.tensor_scalar_add(r_var, r_var, 1e-5))
            nc.scalar.wait_ge(ve.h, ve.n)
            ac.inc(nc.scalar.activation(r_std, r_var, AF.Sqrt))
            nc.vector.wait_ge(ac.h, ac.n)
            vch(nc.vector.reciprocal(r_rstd, r_std))
            vch(nc.vector.tensor_copy(rowsb[:, 0:TT], r_rstd))
            vch(nc.vector.tensor_mul(r_mbr, r_mean, r_rstd))
            ve.inc(nc.vector.tensor_copy(rowsb[:, TT:2 * TT], r_mbr))
            nc.tensor.wait_ge(ve.h, ve.n)
            pe_snap = pe.n
            pe.inc(nc.tensor.matmul(psG[:, :], ones[0:1, :], rowsb[:, 0:TT],
                                    start=True, stop=True))
            pe.inc(nc.tensor.matmul(psU[:, :], ones[0:1, :], rowsb[:, TT:2 * TT],
                                    start=True, stop=True))
            nc.scalar.wait_ge(pe.h, pe_snap + 1)
            ac.inc(nc.scalar.copy(rb[:, :], psG[:, :]))
            nc.scalar.wait_ge(pe.h, pe_snap + 2)
            ac.inc(nc.scalar.copy(mb[:, :], psU[:, :]))
            nc.vector.wait_ge(ac.h, ac.n)
            nc.gpsimd.wait_ge(ac.h, ac.n)
            gv_base = gv.n
            ve_ap_base = ve.n
            for dc in range(DC):
                if dc >= 2:
                    nc.vector.wait_ge(gv.h, gv_base + dc - 1)
                ve.inc(nc.vector.tensor_mul(t32[dc % 2][:, :],
                                            xt[:, dc * TT:(dc + 1) * TT], rb[:, :]))
                nc.gpsimd.wait_ge(ve.h, ve_ap_base + dc + 1)
                gv.inc(nc.gpsimd.tensor_sub(xln[:, dc * TT:(dc + 1) * TT],
                                            t32[dc % 2][:, :], mb[:, :]))
            gv_xln = gv.n

            pe_gu_base = pe.n
            wsl_base = [wsl[0].n, wsl[1].n]
            for hc in range(64):
                if hc >= 2:
                    nc.gpsimd.wait_ge(pe.h, pe_gu_base + 2 * hc - 2)
                wsl[hc % 2].inc(nc.gpsimd.dma_start(out=w1b[hc % 2][:, :], in_=w1s[hc, :, :]), 16)
                wsl[hc % 2].inc(nc.gpsimd.dma_start(out=w3b[hc % 2][:, :], in_=w3s[hc, :, :]), 16)
            ac_base = ac.n
            ve_base = ve.n
            for hc in range(64):
                nc.tensor.wait_ge(wsl[hc % 2].h, wsl_base[hc % 2] + 32 * (hc // 2 + 1))
                if hc == 0:
                    nc.tensor.wait_ge(gv.h, gv_xln)
                else:
                    nc.tensor.wait_ge(ac.h, ac_base + hc)
                    nc.tensor.wait_ge(ve.h, ve_base + hc)
                for dc in range(DC):
                    ins = nc.tensor.matmul(psG[:, :], w1b[hc % 2][:, dc * 128:(dc + 1) * 128],
                                           xln[:, dc * TT:(dc + 1) * TT],
                                           start=(dc == 0), stop=(dc == DC - 1))
                pe.inc(ins)
                for dc in range(DC):
                    ins = nc.tensor.matmul(psU[:, :], w3b[hc % 2][:, dc * 128:(dc + 1) * 128],
                                           xln[:, dc * TT:(dc + 1) * TT],
                                           start=(dc == 0), stop=(dc == DC - 1))
                pe.inc(ins)
                nc.scalar.wait_ge(pe.h, pe_gu_base + 2 * hc + 1)
                nc.scalar.wait_ge(ve.h, ve_base + hc)
                ac.inc(nc.scalar.activation(gtmp[:, :], psG[:, :], AF.Silu))
                nc.vector.wait_ge(pe.h, pe_gu_base + 2 * hc + 2)
                nc.vector.wait_ge(ac.h, ac_base + hc + 1)
                ve.inc(nc.vector.tensor_tensor(hid[:, hc * TT:(hc + 1) * TT],
                                               psU[:, :], gtmp[:, :], ALU.mult))
            ve_hid = ve.n

            pe_w2_base = pe.n
            w2sl_base = [w2sl[0].n, w2sl[1].n]
            for dc in range(DC):
                if dc >= 2:
                    nc.sync.wait_ge(pe.h, pe_w2_base + dc - 1)
                w2sl[dc % 2].inc(nc.sync.dma_start(out=w2b[dc % 2][:, :], in_=w2ps[dc, :, :]), 16)
            ve_w2_base = ve.n
            stsl_base = [stsl[0].n, stsl[1].n]
            for dc in range(DC):
                nc.tensor.wait_ge(w2sl[dc % 2].h, w2sl_base[dc % 2] + 16 * (dc // 2 + 1))
                if dc == 0:
                    nc.tensor.wait_ge(ve.h, ve_hid)
                else:
                    nc.tensor.wait_ge(ve.h, ve_w2_base + dc)
                for hc in range(64):
                    ins = nc.tensor.matmul(psO[:, :], w2b[dc % 2][:, hc * 128:(hc + 1) * 128],
                                           hid[:, hc * TT:(hc + 1) * TT],
                                           start=(hc == 0), stop=(hc == 63))
                pe.inc(ins)
                nc.vector.wait_ge(pe.h, pe_w2_base + dc + 1)
                if dc >= 2:
                    nc.vector.wait_ge(stsl[dc % 2].h, stsl_base[dc % 2] + 16 * (dc // 2))
                else:
                    nc.vector.wait_ge(stsl[dc % 2].h, stsl_base[dc % 2])
                ve.inc(nc.vector.scalar_tensor_tensor(
                    xot[dc % 2][:, :], psO[:, :], 0.5, xt[:, dc * TT:(dc + 1) * TT],
                    ALU.mult, ALU.add))
                nc.scalar.wait_ge(ve.h, ve.n)
                stsl[dc % 2].inc(nc.scalar.dma_start(
                    out=xo[dc, :, tt * TT:(tt + 1) * TT], in_=xot[dc % 2][:, :]), 16)

        nc.sync.wait_ge(stsl[0].h, stsl[0].n)
        nc.sync.wait_ge(stsl[1].h, stsl[1].n)
        nc.vector.wait_ge(stsl[0].h, stsl[0].n)
        nc.vector.wait_ge(stsl[1].h, stsl[1].n)


def build_ffn():
    nc = bass.Bass("TRN2", target_bir_lowering=False, debug=False)
    xfm = nc.dram_tensor("xfm", [DC, 128, T], F32, kind="ExternalInput").ap()
    w1s = nc.dram_tensor("w1s", [64, 128, 2048], F32, kind="ExternalInput").ap()
    w3s = nc.dram_tensor("w3s", [64, 128, 2048], F32, kind="ExternalInput").ap()
    w2ps = nc.dram_tensor("w2ps", [16, 128, 8192], BF16, kind="ExternalInput").ap()
    onesc = nc.dram_tensor("onesc", [128, 128], BF16, kind="ExternalInput").ap()
    xo = nc.dram_tensor("xo", [DC, 128, T], F32, kind="ExternalOutput").ap()
    es = contextlib.ExitStack()
    with es:
        sm = make_sems(nc, es)
        emit_ffn(nc, es, sm, xfm, w1s, w3s, w2ps, onesc, xo)
    return nc


_cache = {}


def _prep_host(inputs):
    h = {}
    h["w1s"] = tile_w(np.ascontiguousarray(inputs["L0_ffn_W1"].T))
    h["w3s"] = tile_w(np.ascontiguousarray(inputs["L0_ffn_W3"].T))
    h["w2t"] = np.ascontiguousarray(inputs["L0_ffn_W2"].T)
    h["h128"] = hadamard(128)
    h["h64"] = hadamard(64)
    h["onesc"] = np.ones((128, 128), BF)
    h["identf"] = np.eye(128, dtype=np.float32)
    sel = np.zeros((8, 8 * 128), np.float32)
    for e in range(8):
        sel[e, e * 128:(e + 1) * 128] = 1.0
    h["selc"] = sel.astype(BF)
    h["rts"] = tile_w(np.ascontiguousarray(inputs["L1_ffn_router_w"].T))
    h["wgs"] = np.stack([tile_w(inputs["L1_ffn_wg"][e]) for e in range(8)])
    h["wus"] = np.stack([tile_w(inputs["L1_ffn_wu"][e]) for e in range(8)])
    h["wds"] = np.stack([tile_w(inputs["L1_ffn_wd"][e]) for e in range(8)])
    h["sgs"] = tile_w(inputs["L1_ffn_sg"])
    h["sus"] = tile_w(inputs["L1_ffn_su"])
    h["sds"] = tile_w(inputs["L1_ffn_sd"])
    h["wdqs"] = tile_w(inputs["L1_attn_w_dq"])
    h["wdkvs"] = tile_w(inputs["L1_attn_w_dkv"])
    wuq = inputs["L1_attn_w_uq"]
    wuqr = np.empty((128, 2048), np.float32)
    for hh in range(16):
        wuqr[:, hh * 128:hh * 128 + 96] = wuq[:, hh * 96:(hh + 1) * 96]
        wuqr[:, hh * 128 + 96:(hh + 1) * 128] = wuq[:, 1536 + hh * 32:1536 + (hh + 1) * 32]
    h["wuqr"] = wuqr
    h["wukv"] = np.ascontiguousarray(inputs["L1_attn_w_ukv"])
    h["wos"] = tile_w(inputs["L1_attn_w_o"])
    h["gq"] = np.ascontiguousarray(inputs["L1_attn_gamma_q"].reshape(128, 1))
    h["gkv"] = np.ascontiguousarray(inputs["L1_attn_gamma_kv"].reshape(128, 1))

    def rope_tables(nd, pos):
        nf = nd // 2
        inv = 10000.0 ** (-np.arange(0, nd, 2, dtype=np.float64) / nd)
        ang = pos[None, :] * inv[:, None]
        A = np.empty((nd, len(pos)), np.float32)
        B = np.empty((nd, len(pos)), np.float32)
        A[0::2] = np.cos(ang); A[1::2] = np.cos(ang)
        B[0::2] = -np.sin(ang); B[1::2] = np.sin(ang)
        return A.astype(BF), B.astype(BF)

    for half in range(2):
        loc = np.arange(T, dtype=np.float64) + half * T
        rem = np.arange(T, dtype=np.float64) + (1 - half) * T
        qa, qb = rope_tables(32, loc)
        qA = np.zeros((128, T), np.float32); qB = np.zeros((128, T), np.float32)
        qA = qA.astype(BF); qB = qB.astype(BF)
        qA[:96] = BF(1.0); qA[96:] = qa; qB[96:] = qb
        h[f"qA{half}"], h[f"qB{half}"] = np.ascontiguousarray(qA), np.ascontiguousarray(qB)
        ka, kb = rope_tables(64, np.concatenate([loc, rem]))
        h[f"kA{half}"], h[f"kB{half}"] = np.ascontiguousarray(ka), np.ascontiguousarray(kb)
        drop = np.zeros((1, 32), np.float32)
        if half == 0:
            drop[0, 16:] = 1.0
        h[f"drop{half}"] = drop
    dm = np.zeros((4, 128, 512), np.float32)
    for rr in range(4):
        for k in range(128):
            dm[rr, k, rr * 128 + k:] = 1.0
    h["dmaskc"] = dm.astype(BF)
    win = np.zeros((8576, 2048), np.float32)
    win[:8512] = inputs["L0_attn_W_in"]
    h["wins"] = tile_w(np.ascontiguousarray(win.T))
    h["wouts"] = tile_w(np.ascontiguousarray(inputs["L0_attn_W_out"].T))
    cw = inputs["L0_attn_conv_weight"]
    h["convw"] = np.ascontiguousarray(cw.reshape(34, 128, 4).transpose(1, 0, 2).reshape(128, 34 * 4))
    h["convbc"] = np.ascontiguousarray(inputs["L0_attn_conv_bias"].reshape(34, 128).T)
    h["nrmw"] = np.ascontiguousarray(inputs["L0_attn_norm_weight"].reshape(32, 128).T)
    dex = np.repeat(inputs["L0_attn_D"], 64)
    h["dexp"] = np.ascontiguousarray(dex.reshape(32, 128).T)
    h["abias"] = np.ascontiguousarray(
        np.stack([inputs["L0_attn_dt_bias"], inputs["L0_attn_A_log"]], axis=1))
    h["identb"] = np.eye(128, dtype=np.float32).astype(BF)
    h["onesf"] = np.ones((128, 128), np.float32)
    mk = np.zeros((128, 128), np.float32)
    for u in range(128):
        mk[u, u:] = 1.0
    h["maskc"] = mk.astype(BF)
    return h


def _run(nc_key, build_fn, in_maps):
    if nc_key not in _cache:
        _cache[nc_key] = build_fn()
    nc = _cache[nc_key]
    return run_bass_kernel_spmd(nc, in_maps, core_ids=list(range(NCORES)))


def shard_x(x):
    out = []
    for c in range(NCORES):
        b, hh = c >> 1, c & 1
        out.append(fm_chunks(np.ascontiguousarray(x[b, hh * T:(hh + 1) * T, :])))
    return out


def unshard_x(shards):
    x = np.empty((4, 4096, 2048), np.float32)
    for c in range(NCORES):
        b, hh = c >> 1, c & 1
        x[b, hh * T:(hh + 1) * T, :] = fm_to_tokmajor(shards[c])
    return x


def run_prep(h):
    im = [{"w2t": h["w2t"], "h128": h["h128"], "h64": h["h64"]}] * NCORES
    res = _run("prep", build_prep, im)
    return res.results[0]["w2ps"]


def run_ffn(h, xs, w2ps):
    im = [{"xfm": xs[c], "w1s": h["w1s"], "w3s": h["w3s"], "w2ps": w2ps,
           "onesc": h["onesc"]} for c in range(NCORES)]
    res = _run("ffn", build_ffn, im)
    return [res.results[c]["xo"] for c in range(NCORES)]


def _kernel_unfused_body(**inputs):
    h = _prep_host(inputs)
    xs = shard_x(np.asarray(inputs["L0_attn_hidden_states"], np.float32))
    w2ps = run_prep(h)
    xs = run_ffn(h, xs, w2ps)
    xs = run_mamba(h, xs)
    xs = run_ffn(h, xs, w2ps)
    xs = run_moe(h, xs)
    xs = run_attn(h, xs)
    xs = run_moe(h, xs)
    return unshard_x(xs)


class LNEmitter:

    def __init__(self, nc, es, psA, psB, psr1=None, psr2=None, xsq_buf=None,
                 rows_buf=None, rowsb_buf=None, mb_buf=None, rb_buf=None,
                 t32_bufs=None):
        self.nc = nc
        self.psA, self.psB = psA, psB
        self.ext = rows_buf is not None
        self.xt = es.enter_context(nc.sbuf_tensor(uq("ln_xt"), [128, DC * TT], F32))
        if xsq_buf is None:
            self.xsq = es.enter_context(nc.sbuf_tensor(uq("ln_xsq"), [128, DC * TT], BF16))
        else:
            self.xsq = xsq_buf
        if self.ext:
            self.rows, self.rowsb = rows_buf, rowsb_buf
            self.mb, self.rb = mb_buf, rb_buf
            self.t32 = t32_bufs
        if not self.ext:
            self.rows = es.enter_context(nc.sbuf_tensor(uq("ln_rows"), [1, TT * 5], F32))
            self.rowsb = es.enter_context(nc.sbuf_tensor(uq("ln_rowsb"), [1, TT * 2], BF16))
            self.mb = es.enter_context(nc.sbuf_tensor(uq("ln_mb"), [128, TT], F32))
            self.rb = es.enter_context(nc.sbuf_tensor(uq("ln_rb"), [128, TT], F32))
            self.t32 = [es.enter_context(nc.sbuf_tensor(uq(f"ln_t32_{i}"), [128, TT], F32))
                        for i in range(2)]
        if psr1 is None:
            self.psR1 = es.enter_context(nc.psum_tensor(uq("ln_psR1"), [1, TT], F32))[:, :]
            self.psR2 = es.enter_context(nc.psum_tensor(uq("ln_psR2"), [1, TT], F32))[:, :]
        else:
            self.psR1, self.psR2 = psr1, psr2

    def emit(self, sm, xfm, tt, dest_chunk, ones):
        nc = self.nc
        sy, pe, ac, ve, gv = sm["sy"], sm["pe"], sm["ac"], sm["ve"], sm["gv"]
        for dc in range(DC):
            sy.inc(nc.sync.dma_start(
                out=self.xt[:, dc * TT:(dc + 1) * TT],
                in_=xfm[dc, :, tt * TT:(tt + 1) * TT]), 16)
        nc.vector.wait_ge(sy.h, sy.n)
        for dc in range(DC):
            ins = nc.vector.tensor_copy(dest_chunk(dc), self.xt[:, dc * TT:(dc + 1) * TT])
        ve.inc(ins)
        nc.scalar.wait_ge(sy.h, sy.n)
        ac.inc(nc.scalar.activation(self.xsq[:, :], self.xt[:, :], AF.Square))
        nc.tensor.wait_ge(ve.h, ve.n)
        nc.tensor.wait_ge(ac.h, ac.n)
        for dc in range(DC):
            nc.tensor.matmul(self.psR1, ones[:, 0:1], dest_chunk(dc),
                             start=(dc == 0), stop=(dc == DC - 1))
        for dc in range(DC):
            ins = nc.tensor.matmul(self.psR2, ones[:, 0:1],
                                   self.xsq[:, dc * TT:(dc + 1) * TT],
                                   start=(dc == 0), stop=(dc == DC - 1))
        pe.inc(ins)
        nc.vector.wait_ge(pe.h, pe.n)
        rows, rowsb = self.rows, self.rowsb
        r_mean, r_msq = rows[:, 0:TT], rows[:, TT:2 * TT]
        r_var, r_std, r_rstd = rows[:, 2 * TT:3 * TT], rows[:, 3 * TT:4 * TT], rows[:, 4 * TT:5 * TT]

        def vch(ins):
            ve.inc(ins)
            nc.vector.wait_ge(ve.h, ve.n)
        vch(nc.vector.tensor_scalar_mul(r_mean, self.psR1, 1.0 / D))
        vch(nc.vector.tensor_scalar_mul(r_msq, self.psR2, 1.0 / D))
        vch(nc.vector.tensor_mul(r_var, r_mean, r_mean))
        vch(nc.vector.tensor_sub(r_var, r_msq, r_var))
        ve.inc(nc.vector.tensor_scalar_add(r_var, r_var, 1e-5))
        nc.scalar.wait_ge(ve.h, ve.n)
        ac.inc(nc.scalar.activation(r_std, r_var, AF.Sqrt))
        nc.vector.wait_ge(ac.h, ac.n)
        vch(nc.vector.reciprocal(r_rstd, r_std))
        vch(nc.vector.tensor_copy(rowsb[:, 0:TT], r_rstd))
        vch(nc.vector.tensor_mul(r_std, r_mean, r_rstd))
        ve.inc(nc.vector.tensor_copy(rowsb[:, TT:2 * TT], r_std))
        nc.tensor.wait_ge(ve.h, ve.n)
        pe_snap = pe.n
        pe.inc(nc.tensor.matmul(self.psA[:, 0:TT], ones[0:1, :], rowsb[:, 0:TT],
                                start=True, stop=True))
        pe.inc(nc.tensor.matmul(self.psB[:, 0:TT], ones[0:1, :], rowsb[:, TT:2 * TT],
                                start=True, stop=True))
        nc.scalar.wait_ge(pe.h, pe_snap + 1)
        ac.inc(nc.scalar.copy(self.rb[:, :], self.psA[:, 0:TT]))
        nc.scalar.wait_ge(pe.h, pe_snap + 2)
        ac.inc(nc.scalar.copy(self.mb[:, :], self.psB[:, 0:TT]))
        nc.vector.wait_ge(ac.h, ac.n)
        nc.gpsimd.wait_ge(ac.h, ac.n)
        gv_base = gv.n
        ve_ap = ve.n
        for dc in range(DC):
            if dc >= 2:
                nc.vector.wait_ge(gv.h, gv_base + dc - 1)
            ve.inc(nc.vector.tensor_mul(self.t32[dc % 2][:, :],
                                        self.xt[:, dc * TT:(dc + 1) * TT], self.rb[:, :]))
            nc.gpsimd.wait_ge(ve.h, ve_ap + dc + 1)
            gv.inc(nc.gpsimd.tensor_sub(dest_chunk(dc), self.t32[dc % 2][:, :],
                                        self.mb[:, :]))


NE = 8
HC_E = 11
HC_S = 22


def emit_moe(nc, es, sm, xfm, rts, wgs, wus, wds, sgs, sus, sds, onesc, identf,
             selc, xo):
    ones = es.enter_context(nc.sbuf_tensor(uq("ones"), [128, 128], BF16))
    idf = es.enter_context(nc.sbuf_tensor(uq("idf"), [128, 128], F32))
    sel = es.enter_context(nc.sbuf_tensor(uq("sel"), [8, 8 * 128], BF16))
    rtsb = es.enter_context(nc.sbuf_tensor(uq("rtsb"), [128, 2048], BF16))
    xf = es.enter_context(nc.sbuf_tensor(uq("xf"), [128, DC * T], BF16))
    hid = es.enter_context(nc.sbuf_tensor(uq("hid"), [128, HC_E * T], BF16))
    wgb = [es.enter_context(nc.sbuf_tensor(uq(f"wgb{i}"), [128, 2048], BF16)) for i in range(2)]
    wub = [es.enter_context(nc.sbuf_tensor(uq(f"wub{i}"), [128, 2048], BF16)) for i in range(2)]
    wdb = [es.enter_context(nc.sbuf_tensor(uq(f"wdb{i}"), [128, 1408], BF16)) for i in range(2)]
    sgt = es.enter_context(nc.sbuf_tensor(uq("sgt"), [128, 1024], BF16))
    ut = es.enter_context(nc.sbuf_tensor(uq("ut"), [128, 1024], BF16))
    geb = es.enter_context(nc.sbuf_tensor(uq("geb"), [128, T], BF16))
    gw = es.enter_context(nc.sbuf_tensor(uq("gw"), [128, 16 * 8], F32))
    grow = es.enter_context(nc.sbuf_tensor(uq("grow"), [8, T], BF16))
    gsc = es.enter_context(nc.sbuf_tensor(uq("gsc"), [128, 64], F32))
    ot = [es.enter_context(nc.sbuf_tensor(uq(f"ot{i}"), [128, TT], F32)) for i in range(2)]

    psG = es.enter_context(nc.psum_tensor(uq("psG"), [128, 1024], F32))
    psU = es.enter_context(nc.psum_tensor(uq("psU"), [128, 1024], F32))
    psO = es.enter_context(nc.psum_tensor(uq("psO"), [128, TT], F32))
    psR = es.enter_context(nc.psum_tensor(uq("psR"), [128, 8], F32))
    psT = es.enter_context(nc.psum_tensor(uq("psT"), [8, 128], F32))
    psB = es.enter_context(nc.psum_tensor(uq("psB"), [128, TT], F32))

    if True:
        sy, gp, pe, ac, ve, gv = (sm["sy"], sm["gp"], sm["pe"], sm["ac"],
                                  sm["ve"], sm["gv"])
        wsl, dsl, osl = sm["wsl"], sm["dsl"], sm["osl"]
        ln = LNEmitter(nc, es, psG, psU, psr1=psO[0:1, 0:TT], psr2=psB[0:1, 0:TT], xsq_buf=hid[:, 0:DC * TT])
        sm = {"sy": sy, "pe": pe, "ac": ac, "ve": ve, "gv": gv}

        gp.inc(nc.gpsimd.dma_start(out=ones[:, :], in_=onesc[:, :]), 16)
        gp.inc(nc.gpsimd.dma_start(out=idf[:, :], in_=identf[:, :]), 16)
        gp.inc(nc.gpsimd.dma_start(out=sel[:, :], in_=selc[:, :]), 16)
        gp.inc(nc.gpsimd.dma_start(out=rtsb[:, :], in_=rts[0, :, :]), 16)
        for dc in range(DC):
            sy.inc(nc.sync.dma_start(out=xo[dc, :, :], in_=xfm[dc, :, :]), 16)
        nc.tensor.wait_ge(gp.h, gp.n)

        for tt in range(NTT):
            if tt > 0:
                nc.sync.wait_ge(gv.h, gv.n)
            ln.emit(sm, xfm, tt,
                    lambda dc, _t=tt: xf[:, dc * T + _t * TT: dc * T + (_t + 1) * TT],
                    ones)

        nc.tensor.wait_ge(gv.h, gv.n)

        def vch(ins):
            ve.inc(ins)
            nc.vector.wait_ge(ve.h, ve.n)
        for ts in range(16):
            nc.tensor.wait_ge(ve.h, ve.n)
            for dc in range(DC):
                ins = nc.tensor.matmul(psR[:, :],
                                       xf[:, dc * T + ts * 128: dc * T + (ts + 1) * 128],
                                       rtsb[:, dc * 128: dc * 128 + 8],
                                       start=(dc == 0), stop=(dc == DC - 1))
            pe.inc(ins)
            nc.vector.wait_ge(pe.h, pe.n)
            g0 = gsc[:, 0:8]
            nm = gsc[:, 8:9]
            ex = gsc[:, 9:17]
            s1 = gsc[:, 17:18]
            rs = gsc[:, 18:19]
            gg = gsc[:, 19:27]
            m1 = gsc[:, 27:28]
            vch(nc.vector.tensor_copy(g0, psR[:, :]))
            vch(nc.vector.reduce_max(nm, g0, axis=AX.X, negate=True))
            nc.scalar.wait_ge(ve.h, ve.n)
            ac.inc(nc.scalar.activation(ex, g0, AF.Exp, bias=nm, accum_out=s1))
            nc.vector.wait_ge(ac.h, ac.n)
            vch(nc.vector.reciprocal(rs, s1))
            vch(nc.vector.tensor_scalar(gg, ex, rs, None, ALU.mult))
            vch(nc.vector.reduce_max(m1, gg, axis=AX.X))
            eq = gsc[:, 28:36]
            vch(nc.vector.tensor_scalar(eq, gg, m1, None, ALU.is_ge))
            tt2 = gsc[:, 36:44]
            vch(nc.vector.scalar_tensor_tensor(tt2, eq, -1e30, gg, ALU.mult, ALU.add))
            m2 = gsc[:, 44:45]
            vch(nc.vector.reduce_max(m2, tt2, axis=AX.X))
            keep = gsc[:, 45:53]
            vch(nc.vector.tensor_scalar(keep, gg, m2, None, ALU.is_ge))
            den = gsc[:, 53:54]
            vch(nc.vector.tensor_add(den, m1, m2))
            vch(nc.vector.tensor_scalar(den, den, 1e-20, None, ALU.add))
            rd = gsc[:, 54:55]
            vch(nc.vector.reciprocal(rd, den))
            vch(nc.vector.scalar_tensor_tensor(gw[:, ts * 8:(ts + 1) * 8],
                                               gg, rd, keep, ALU.mult, ALU.mult))
        nc.tensor.wait_ge(ve.h, ve.n)
        for ts in range(16):
            nc.tensor.wait_ge(ac.h, ac.n)
            pe.inc(nc.tensor.transpose(psT[:, :], gw[:, ts * 8:(ts + 1) * 8], idf[:, :]))
            nc.scalar.wait_ge(pe.h, pe.n)
            ac.inc(nc.scalar.copy(grow[:, ts * 128:(ts + 1) * 128], psT[:, 0:128]))

        nc.gpsimd.wait_ge(sy.h, sy.n)
        for e in range(NE + 2):
            shared = e >= NE
            hh = e - NE
            if not shared:
                nc.tensor.wait_ge(ac.h, ac.n)
                nc.tensor.wait_ge(gv.h, gv.n)
                for q in range(4):
                    nc.tensor.wait_ge(ac.h, ac.n)
                    pe.inc(nc.tensor.matmul(psB[:, :], sel[:, e * 128:(e + 1) * 128],
                                            grow[:, q * TT:(q + 1) * TT],
                                            start=True, stop=True))
                    nc.scalar.wait_ge(pe.h, pe.n)
                    ac.inc(nc.scalar.copy(geb[:, q * TT:(q + 1) * TT], psB[:, :]))
            pe_gu = pe.n
            wsl_b = [wsl[0].n, wsl[1].n]

            def gu_load(hc):
                src_g = sgs[hh * HC_E + hc, :, :] if shared else wgs[e, hc, :, :]
                src_u = sus[hh * HC_E + hc, :, :] if shared else wus[e, hc, :, :]
                if hc >= 2:
                    nc.gpsimd.wait_ge(pe.h, pe_gu + 4 * hc - 4)
                wsl[hc % 2].inc(nc.gpsimd.dma_start(out=wgb[hc % 2][:, :], in_=src_g), 16)
                wsl[hc % 2].inc(nc.gpsimd.dma_start(out=wub[hc % 2][:, :], in_=src_u), 16)
            gu_load(0)
            gu_load(1)
            ac_gu = ac.n
            ve_gu = ve.n
            gv_gu = gv.n
            for hc in range(HC_E):
                nc.tensor.wait_ge(wsl[hc % 2].h, wsl_b[hc % 2] + 32 * (hc // 2 + 1))
                for th in range(2):
                    k = 2 * hc + th
                    if k >= 1:
                        nc.tensor.wait_ge(ac.h, ac_gu + k)
                        nc.tensor.wait_ge(ve.h, ve_gu + k)
                    for dc in range(DC):
                        for tq in range(2):
                            ins = nc.tensor.matmul(
                                psG[:, tq * TT:(tq + 1) * TT],
                                wgb[hc % 2][:, dc * 128:(dc + 1) * 128],
                                xf[:, dc * T + th * 1024 + tq * TT: dc * T + th * 1024 + (tq + 1) * TT],
                                start=(dc == 0), stop=(dc == DC - 1))
                    pe.inc(ins)
                    for dc in range(DC):
                        for tq in range(2):
                            ins = nc.tensor.matmul(
                                psU[:, tq * TT:(tq + 1) * TT],
                                wub[hc % 2][:, dc * 128:(dc + 1) * 128],
                                xf[:, dc * T + th * 1024 + tq * TT: dc * T + th * 1024 + (tq + 1) * TT],
                                start=(dc == 0), stop=(dc == DC - 1))
                    pe.inc(ins)
                    nc.scalar.wait_ge(pe.h, pe_gu + 2 * k + 1)
                    if k >= 1:
                        nc.scalar.wait_ge(gv.h, gv_gu + k)
                    ac.inc(nc.scalar.activation(sgt[:, :], psG[:, :], AF.Silu))
                    nc.vector.wait_ge(pe.h, pe_gu + 2 * k + 2)
                    if k >= 1:
                        nc.vector.wait_ge(gv.h, gv_gu + k)
                    if shared:
                        ve.inc(nc.vector.tensor_copy(ut[:, :], psU[:, :]))
                    else:
                        ve.inc(nc.vector.tensor_tensor(
                            ut[:, :], psU[:, :],
                            geb[:, th * 1024:(th + 1) * 1024], ALU.mult))
                    nc.gpsimd.wait_ge(ac.h, ac_gu + k + 1)
                    nc.gpsimd.wait_ge(ve.h, ve_gu + k + 1)
                    gv.inc(nc.gpsimd.tensor_mul(
                        hid[:, hc * T + th * 1024: hc * T + (th + 1) * 1024],
                        sgt[:, :], ut[:, :]))
                if hc + 2 < HC_E:
                    gu_load(hc + 2)
            gv_hid = gv.n
            dsl_b = [dsl[0].n, dsl[1].n]
            pe_wd = pe.n
            ac_wd = ac.n
            osl_b = [osl[0].n, osl[1].n]

            def wd_load(dc):
                src_d = sds[dc, :, hh * 1408:(hh + 1) * 1408] if shared else wds[e, dc, :, :]
                if dc >= 2:
                    nc.gpsimd.wait_ge(pe.h, pe_wd + 4 * (dc - 1))
                dsl[dc % 2].inc(nc.gpsimd.dma_start(out=wdb[dc % 2][:, :], in_=src_d), 16)
            wd_load(0)
            wd_load(1)
            for dc in range(DC):
                nc.tensor.wait_ge(dsl[dc % 2].h, dsl_b[dc % 2] + 16 * (dc // 2 + 1))
                if dc == 0:
                    nc.tensor.wait_ge(gv.h, gv_hid)
                for tq in range(4):
                    k = 4 * dc + tq
                    if k >= 1:
                        nc.tensor.wait_ge(ac.h, ac_wd + k)
                    for hc in range(HC_E):
                        ins = nc.tensor.matmul(
                            psO[:, :], wdb[dc % 2][:, hc * 128:(hc + 1) * 128],
                            hid[:, hc * T + tq * TT: hc * T + (tq + 1) * TT],
                            start=(hc == 0), stop=(hc == HC_E - 1))
                    pe.inc(ins)
                    nc.scalar.wait_ge(pe.h, pe_wd + k + 1)
                    if k >= 2:
                        nc.scalar.wait_ge(osl[k % 2].h, osl_b[k % 2] + 16 * (k // 2))
                    ac.inc(nc.scalar.activation(ot[k % 2][:, :], psO[:, :], AF.Copy,
                                                scale=0.5))
                    nc.gpsimd.wait_ge(ac.h, ac.n)
                    osl[k % 2].inc(nc.gpsimd.dma_start(
                        out=xo[dc, :, tq * TT:(tq + 1) * TT], in_=ot[k % 2][:, :],
                        accum_op=ALU.add), 16)
                if dc + 2 < DC:
                    wd_load(dc + 2)
            nc.gpsimd.wait_ge(osl[0].h, osl[0].n)
            nc.gpsimd.wait_ge(osl[1].h, osl[1].n)
        nc.sync.wait_ge(osl[0].h, osl[0].n)
        nc.sync.wait_ge(osl[1].h, osl[1].n)
        nc.vector.wait_ge(osl[0].h, osl[0].n)
        nc.vector.wait_ge(osl[1].h, osl[1].n)


def build_moe():
    nc = bass.Bass("TRN2", target_bir_lowering=False, debug=False)
    xfm = nc.dram_tensor("xfm", [DC, 128, T], F32, kind="ExternalInput").ap()
    rts = nc.dram_tensor("rts", [1, 128, 2048], F32, kind="ExternalInput").ap()
    wgs = nc.dram_tensor("wgs", [NE, HC_E, 128, 2048], F32, kind="ExternalInput").ap()
    wus = nc.dram_tensor("wus", [NE, HC_E, 128, 2048], F32, kind="ExternalInput").ap()
    wds = nc.dram_tensor("wds", [NE, DC, 128, 1408], F32, kind="ExternalInput").ap()
    sgs = nc.dram_tensor("sgs", [HC_S, 128, 2048], F32, kind="ExternalInput").ap()
    sus = nc.dram_tensor("sus", [HC_S, 128, 2048], F32, kind="ExternalInput").ap()
    sds = nc.dram_tensor("sds", [DC, 128, 2816], F32, kind="ExternalInput").ap()
    onesc = nc.dram_tensor("onesc", [128, 128], BF16, kind="ExternalInput").ap()
    identf = nc.dram_tensor("identf", [128, 128], F32, kind="ExternalInput").ap()
    selc = nc.dram_tensor("selc", [8, 8 * 128], BF16, kind="ExternalInput").ap()
    xo = nc.dram_tensor("xo", [DC, 128, T], F32, kind="ExternalOutput").ap()
    es = contextlib.ExitStack()
    with es:
        sm = make_sems(nc, es)
        emit_moe(nc, es, sm, xfm, rts, wgs, wus, wds, sgs, sus, sds, onesc,
                 identf, selc, xo)
    return nc


def run_moe(h, xs):
    im = [{"xfm": xs[c], "rts": h["rts"], "wgs": h["wgs"], "wus": h["wus"],
           "wds": h["wds"], "sgs": h["sgs"], "sus": h["sus"], "sds": h["sds"],
           "onesc": h["onesc"], "identf": h["identf"], "selc": h["selc"]}
          for c in range(NCORES)]
    res = _run("moe", build_moe, im)
    return [res.results[c]["xo"] for c in range(NCORES)]


NKT = 32
QTN = 4


def emit_attn(nc, es, sm, xloc, xrem, wdqs, wdkvs, wuqr, wukv, wos, gq, gkv,
              qA, qB, kA, kB, dmaskc, droprc, onesc, xo, qd, vd):
    ones = es.enter_context(nc.sbuf_tensor(uq("ones"), [128, 128], BF16))
    wdqb = es.enter_context(nc.sbuf_tensor(uq("wdqb"), [128, 2048], BF16))
    wdkvb = es.enter_context(nc.sbuf_tensor(uq("wdkvb"), [128, 2 * 2048], BF16))
    wuqb = es.enter_context(nc.sbuf_tensor(uq("wuqb"), [128, 2048], BF16))
    wukvb = es.enter_context(nc.sbuf_tensor(uq("wukvb"), [128, 1280], BF16))
    gqb = es.enter_context(nc.sbuf_tensor(uq("gqb"), [128, 1], F32))
    gkvb = es.enter_context(nc.sbuf_tensor(uq("gkvb"), [128, 1], F32))
    qAb = es.enter_context(nc.sbuf_tensor(uq("qAb"), [128, T], BF16))
    qBb = es.enter_context(nc.sbuf_tensor(uq("qBb"), [128, T], BF16))
    kAb = es.enter_context(nc.sbuf_tensor(uq("kAb"), [64, 2 * T], BF16))
    kBb = es.enter_context(nc.sbuf_tensor(uq("kBb"), [64, 2 * T], BF16))
    dmask = es.enter_context(nc.sbuf_tensor(uq("dmask"), [128, 4 * TT], BF16))
    dropr = es.enter_context(nc.sbuf_tensor(uq("dropr"), [1, NKT], F32))
    cqnall = es.enter_context(nc.sbuf_tensor(uq("cqnall"), [128, T + TT], BF16))
    ckvnall = es.enter_context(nc.sbuf_tensor(uq("ckvnall"), [128, 2 * T], BF16))
    krall = es.enter_context(nc.sbuf_tensor(uq("krall"), [64, TT], BF16))
    kall = [es.enter_context(nc.sbuf_tensor(uq(f"kall{g}"), [128, 2 * T], BF16))
            for g in range(4)]
    sc1 = es.enter_context(nc.sbuf_tensor(uq("sc1"), [128, 1024], BF16))
    sc2 = es.enter_context(nc.sbuf_tensor(uq("sc2"), [128, 1024], BF16))
    sf1 = es.enter_context(nc.sbuf_tensor(uq("sf1"), [128, TT], F32))
    sf2 = es.enter_context(nc.sbuf_tensor(uq("sf2"), [128, TT], F32))
    rows = es.enter_context(nc.sbuf_tensor(uq("rows"), [1, 5 * TT + 64], F32))
    rowsb = es.enter_context(nc.sbuf_tensor(uq("rowsb"), [1, 1024], BF16))
    lnrb = es.enter_context(nc.sbuf_tensor(uq("lnrb"), [128, TT], F32))
    kmax = es.enter_context(nc.sbuf_tensor(uq("kmax"), [1, 8], F32))
    qpr = es.enter_context(nc.sbuf_tensor(uq("qpr"), [128, 2 * TT], BF16))
    vtb = [es.enter_context(nc.sbuf_tensor(uq(f"vtb{i}"), [128, 1024], BF16)) for i in range(2)]
    sc3 = vtb[0]
    pexp = [es.enter_context(nc.sbuf_tensor(uq(f"pexp{i}"), [128, TT], BF16)) for i in range(2)]
    biasT = es.enter_context(nc.sbuf_tensor(uq("biasT"), [128, 2 * NKT], F32))
    oa = es.enter_context(nc.sbuf_tensor(uq("oa"), [128, 32 * TT], BF16))
    rlb = es.enter_context(nc.sbuf_tensor(uq("rlb"), [128, TT], F32))
    wob = es.enter_context(nc.sbuf_tensor(uq("wob"), [128, 4096], BF16))
    xot = [es.enter_context(nc.sbuf_tensor(uq(f"xot{i}"), [128, TT], F32)) for i in range(2)]

    psS = es.enter_context(nc.psum_tensor(uq("psS"), [128, TT], F32))
    psS2 = es.enter_context(nc.psum_tensor(uq("psS2"), [128, TT], F32))
    psB = es.enter_context(nc.psum_tensor(uq("psB"), [128, TT], F32))
    psL = es.enter_context(nc.psum_tensor(uq("psL"), [1, TT], F32))
    psO = [es.enter_context(nc.psum_tensor(uq(f"psO{i}"), [128, TT], F32)) for i in range(4)]

    if True:
        sy, gp, pe, ac, ve, gv = (sm["sy"], sm["gp"], sm["pe"], sm["ac"],
                                  sm["ve"], sm["gv"])
        vsl, wsl, stsl, xsl = sm["vsl"], sm["wsl"], sm["stsl"], sm["xsl"]
        vst, qst, qld = sm["vst"], sm["qst"], sm["qld"]
        ln = LNEmitter(nc, es, psO[0], psO[1], psr1=psS[0:1, :], psr2=psB[0:1, :],
                       xsq_buf=oa[:, 0:DC * TT], rows_buf=rows, rowsb_buf=rowsb,
                       mb_buf=rlb, rb_buf=lnrb, t32_bufs=[xot[0], xot[1]])
        xlnb = oa[:, DC * TT:2 * DC * TT]
        sm = {"sy": sy, "pe": pe, "ac": ac, "ve": ve, "gv": gv}

        def vch(ins):
            ve.inc(ins)
            nc.vector.wait_ge(ve.h, ve.n)

        for t, d in [(ones, onesc), (wdqb, wdqs[0]), (wuqb, wuqr), (wukvb, wukv),
                     (gqb, gq), (gkvb, gkv), (qAb, qA), (qBb, qB), (kAb, kA),
                     (kBb, kB), (dropr, droprc)]:
            gp.inc(nc.gpsimd.dma_start(out=t[:, :] if t is not dmask else t, in_=d), 16)
        gp.inc(nc.gpsimd.dma_start(out=wdkvb[:, :].rearrange("p (m q) -> p m q", m=2), in_=wdkvs[:, :, :].transpose([1, 0, 2])), 16)
        gp.inc(nc.gpsimd.dma_start(out=dmask[:, :].rearrange("p (m q) -> p m q", m=4), in_=dmaskc[:, :, :].transpose([1, 0, 2])), 16)
        nc.tensor.wait_ge(gp.h, gp.n)
        nc.vector.wait_ge(gp.h, gp.n)
        nc.scalar.wait_ge(gp.h, gp.n)

        for tt in range(2 * NTT):
            xsrc = xloc if tt < NTT else xrem
            tl = tt if tt < NTT else tt - NTT
            if tt > 0:
                nc.sync.wait_ge(gv.h, gv.n)
            ln.emit(sm, xsrc, tl,
                    lambda dc: xlnb[:, dc * TT:(dc + 1) * TT], ones)
            nc.tensor.wait_ge(gv.h, gv.n)
            col = slice(tt * TT, (tt + 1) * TT)

            def t5ln(ps_in, gcol, dest, nrows=128):
                nc.scalar.wait_ge(pe.h, pe.n)
                nc.scalar.wait_ge(ve.h, ve.n)
                nc.scalar.wait_ge(gv.h, gv.n)
                ac.inc(nc.scalar.copy(sc1[:, 0:TT], ps_in))
                ac.inc(nc.scalar.activation(sc2[:, 0:TT], ps_in, AF.Square))
                nc.tensor.wait_ge(ac.h, ac.n)
                pe.inc(nc.tensor.matmul(psS[0:1, :], ones[:, 0:1], sc1[:, 0:TT],
                                        start=True, stop=True))
                pe.inc(nc.tensor.matmul(psB[0:1, :], ones[:, 0:1], sc2[:, 0:TT],
                                        start=True, stop=True))
                nc.vector.wait_ge(pe.h, pe.n)
                r_m, r_q, r_v, r_s, r_r = (rows[:, 0:TT], rows[:, TT:2 * TT],
                                           rows[:, 2 * TT:3 * TT], rows[:, 3 * TT:4 * TT],
                                           rows[:, 4 * TT:5 * TT])
                vch(nc.vector.tensor_scalar_mul(r_m, psS[0:1, :], 1.0 / nrows))
                vch(nc.vector.tensor_scalar_mul(r_q, psB[0:1, :], 1.0 / nrows))
                vch(nc.vector.tensor_mul(r_v, r_m, r_m))
                vch(nc.vector.tensor_sub(r_v, r_q, r_v))
                ve.inc(nc.vector.tensor_scalar_add(r_v, r_v, 1e-5))
                nc.scalar.wait_ge(ve.h, ve.n)
                ac.inc(nc.scalar.activation(r_s, r_v, AF.Sqrt))
                nc.vector.wait_ge(ac.h, ac.n)
                vch(nc.vector.reciprocal(r_r, r_s))
                vch(nc.vector.tensor_copy(rowsb[:, 0:TT], r_r))
                vch(nc.vector.tensor_mul(r_s, r_m, r_r))
                ve.inc(nc.vector.tensor_copy(rowsb[:, TT:2 * TT], r_s))
                nc.tensor.wait_ge(ve.h, ve.n)
                p_s = pe.n
                pe.inc(nc.tensor.matmul(psO[0][:, :], ones[0:1, :], rowsb[:, 0:TT],
                                        start=True, stop=True))
                pe.inc(nc.tensor.matmul(psO[1][:, :], ones[0:1, :], rowsb[:, TT:2 * TT],
                                        start=True, stop=True))
                nc.scalar.wait_ge(pe.h, p_s + 1)
                ac.inc(nc.scalar.copy(sf1[:, :], psO[0][:, :]))
                nc.scalar.wait_ge(pe.h, p_s + 2)
                ac.inc(nc.scalar.copy(sf2[:, :], psO[1][:, :]))
                nc.vector.wait_ge(ac.h, ac.n)
                ve.inc(nc.vector.tensor_mul(sf1[:, :], sc1[:, 0:TT], sf1[:, :]))
                nc.gpsimd.wait_ge(ve.h, ve.n)
                gv.inc(nc.gpsimd.tensor_sub(sc3[:, 0:TT], sf1[:, :], sf2[:, :]))
                nc.vector.wait_ge(gv.h, gv.n)
                ve.inc(nc.vector.tensor_scalar(dest, sc3[:, 0:TT], gcol, None, ALU.mult))
                nc.vector.wait_ge(ve.h, ve.n)

            nc.tensor.wait_ge(ac.h, ac.n)
            for dc in range(DC):
                ins = nc.tensor.matmul(psO[2][:, :], wdqb[:, dc * 128:(dc + 1) * 128],
                                       xlnb[:, dc * TT:(dc + 1) * TT],
                                       start=(dc == 0), stop=(dc == DC - 1))
            pe.inc(ins)
            cq_col = slice(tt * TT, (tt + 1) * TT) if tt < NTT else slice(T, T + TT)
            t5ln(psO[2][:, :], gqb[:, :], cqnall[:, cq_col])
            for dc in range(DC):
                ins = nc.tensor.matmul(psO[2][:, :], wdkvb[:, dc * 128:(dc + 1) * 128],
                                       xlnb[:, dc * TT:(dc + 1) * TT],
                                       start=(dc == 0), stop=(dc == DC - 1))
            pe.inc(ins)
            for dc in range(DC):
                ins = nc.tensor.matmul(psO[3][:, :], wdkvb[:, 2048 + dc * 128:2048 + (dc + 1) * 128],
                                       xlnb[:, dc * TT:(dc + 1) * TT],
                                       start=(dc == 0), stop=(dc == DC - 1))
            pe.inc(ins)
            t5ln(psO[2][:, :], gkvb[:, :], ckvnall[:, col])
            nc.scalar.wait_ge(pe.h, pe.n)
            nc.scalar.wait_ge(ve.h, ve.n)
            nc.scalar.wait_ge(gv.h, gv.n)
            ac.inc(nc.scalar.copy(sc1[0:64, 0:TT], psO[3][0:64, :]))
            nc.vector.wait_ge(ac.h, ac.n)
            mask32 = [(i // 2 * 2) + (1 - i % 2) for i in range(32)]
            vch(nc.vector.stream_shuffle(sc2[0:64, 0:TT], sc1[0:64, 0:TT], mask32))
            vch(nc.vector.tensor_mul(sf1[0:64, :], sc1[0:64, 0:TT], kAb[:, col]))
            nc.gpsimd.wait_ge(ve.h, ve.n)
            gv.inc(nc.gpsimd.tensor_mul(sf2[0:64, :], sc2[0:64, 0:TT], kBb[:, col]))
            nc.vector.wait_ge(gv.h, gv.n)
            vch(nc.vector.tensor_add(krall[:, :], sf1[0:64, :], sf2[0:64, :]))
            nc.tensor.wait_ge(ve.h, ve.n)
            pe_k = pe.n
            pe.inc(nc.tensor.matmul(psO[2][:, :], wukvb[:, 0:128], ckvnall[:, col],
                                    start=True, stop=True))
            pe.inc(nc.tensor.matmul(psO[3][:, :], wukvb[:, 128:256], ckvnall[:, col],
                                    start=True, stop=True))
            nc.scalar.wait_ge(pe.h, pe.n)
            ac.inc(nc.scalar.copy(kall[0][0:64, tt * TT:(tt + 1) * TT], psO[2][0:64, :]))
            ac.inc(nc.scalar.copy(kall[1][0:64, tt * TT:(tt + 1) * TT], psO[2][64:128, :]))
            ac.inc(nc.scalar.copy(kall[2][0:64, tt * TT:(tt + 1) * TT], psO[3][0:64, :]))
            ac.inc(nc.scalar.copy(kall[3][0:64, tt * TT:(tt + 1) * TT], psO[3][64:128, :]))
            nc.vector.wait_ge(ve.h, ve.n)
            for g in range(4):
                ins = nc.vector.tensor_copy(kall[g][64:128, tt * TT:(tt + 1) * TT],
                                            krall[:, :])
            ve.inc(ins)
            for sub in range(4):
                nc.tensor.wait_ge(ac.h, ac.n)
                lh = ckvnall[:, tt * TT + sub * 128: tt * TT + (sub + 1) * 128]
                pe.inc(nc.tensor.matmul(psO[2][:, :], lh, wukvb[:, 256:768],
                                        start=True, stop=True))
                pe.inc(nc.tensor.matmul(psO[3][:, :], lh, wukvb[:, 768:1280],
                                        start=True, stop=True))
                nc.scalar.wait_ge(pe.h, pe.n)
                ac.inc(nc.scalar.copy(sc3[:, 0:512], psO[2][:, :]))
                ac.inc(nc.scalar.copy(sc3[:, 512:1024], psO[3][:, :]))
                nc.sync.wait_ge(ac.h, ac.n)
                vst.inc(nc.sync.dma_start(out=vd[tt * 4 + sub, :, :], in_=sc3[:, 0:1024]), 16)
                nc.scalar.wait_ge(vst.h, vst.n)

        for qt in range(QTN):
            for blk in range(16):
                nc.tensor.wait_ge(ac.h, ac.n)
                for _ in [0]:
                    ins = nc.tensor.matmul(psO[2][:, :], wuqb[:, blk * 128:(blk + 1) * 128],
                                           cqnall[:, qt * TT:(qt + 1) * TT],
                                           start=True, stop=True)
                pe.inc(ins)
                nc.scalar.wait_ge(pe.h, pe.n)
                nc.scalar.wait_ge(ve.h, ve.n)
                nc.scalar.wait_ge(gv.h, gv.n)
                ac.inc(nc.scalar.copy(sc1[:, 0:TT], psO[2][:, :]))
                nc.vector.wait_ge(ac.h, ac.n)
                qcol = slice(qt * TT, (qt + 1) * TT)
                mask32 = [(i // 2 * 2) + (1 - i % 2) for i in range(32)]
                vch(nc.vector.stream_shuffle(sc2[:, 0:TT], sc1[:, 0:TT], mask32))
                vch(nc.vector.tensor_mul(sf1[:, :], sc1[:, 0:TT], qAb[:, qcol]))
                nc.gpsimd.wait_ge(ve.h, ve.n)
                gv.inc(nc.gpsimd.tensor_mul(sf2[:, :], sc2[:, 0:TT], qBb[:, qcol]))
                nc.vector.wait_ge(gv.h, gv.n)
                vch(nc.vector.tensor_add(sc3[:, 0:TT], sf1[:, :], sf2[:, :]))
                nc.sync.wait_ge(ve.h, ve.n)
                qst.inc(nc.sync.dma_start(out=qd[blk, :, qcol], in_=sc3[:, 0:TT]), 16)
                nc.vector.wait_ge(qst.h, qst.n)

        nc.vector.wait_ge(qst.h, qst.n)
        ve.inc(nc.vector.memset(kmax[:, :], 0.0))
        for g in range(4):
            for piece in range(8):
                nc.scalar.wait_ge(pe.h, pe.n)
                nc.scalar.wait_ge(ve.h, ve.n)
                ac.inc(nc.scalar.activation(sc1[:, 0:TT],
                                            kall[g][:, piece * TT:(piece + 1) * TT],
                                            AF.Square))
                nc.tensor.wait_ge(ac.h, ac.n)
                nc.tensor.wait_ge(ve.h, ve.n)
                pe.inc(nc.tensor.matmul(psS[0:1, :], ones[:, 0:1], sc1[:, 0:TT],
                                        start=True, stop=True))
                nc.scalar.wait_ge(pe.h, pe.n)
                ac.inc(nc.scalar.activation(rows[:, 0:TT], psS[0:1, :], AF.Sqrt))
                nc.vector.wait_ge(ac.h, ac.n)
                vch(nc.vector.reduce_max(rows[:, TT:TT + 1], rows[:, 0:TT], axis=AX.X))
                vch(nc.vector.tensor_max(kmax[:, g:g + 1], kmax[:, g:g + 1],
                                         rows[:, TT:TT + 1]))

        scale = 1.0 / math.sqrt(128.0)
        mask32 = [(i // 2 * 2) + (1 - i % 2) for i in range(32)]
        for qt in range(QTN):
            kts = list(range(4 * qt + 4)) + list(range(16, NKT))
            nkt = len(kts)
            for hp in range(8):
                nc.sync.wait_ge(pe.h, pe.n)
                for hi in range(2):
                    qld.inc(nc.sync.dma_start(out=qpr[:, hi * TT:(hi + 1) * TT],
                                              in_=qd[2 * hp + hi, :, qt * TT:(qt + 1) * TT]), 16)
                for hi, h in enumerate((2 * hp, 2 * hp + 1)):
                    g = h // 4
                    nc.scalar.wait_ge(qld.h, qld.n)
                    nc.scalar.wait_ge(pe.h, pe.n)
                    nc.scalar.wait_ge(ve.h, ve.n)
                    ac.inc(nc.scalar.activation(sc1[:, 0:TT], qpr[:, hi * TT:(hi + 1) * TT],
                                                AF.Square))
                    nc.tensor.wait_ge(ac.h, ac.n)
                    nc.tensor.wait_ge(ve.h, ve.n)
                    pe.inc(nc.tensor.matmul(psS[0:1, :], ones[:, 0:1], sc1[:, 0:TT],
                                            start=True, stop=True))
                    nc.scalar.wait_ge(pe.h, pe.n)
                    ac.inc(nc.scalar.activation(rows[:, 0:TT], psS[0:1, :], AF.Sqrt))
                    nc.vector.wait_ge(ac.h, ac.n)
                    vch(nc.vector.reduce_max(rows[:, TT:TT + 1], rows[:, 0:TT], axis=AX.X))
                    vch(nc.vector.tensor_mul(rows[:, TT:TT + 1], rows[:, TT:TT + 1],
                                             kmax[:, g:g + 1]))
                    vch(nc.vector.tensor_scalar_mul(rows[:, TT:TT + 1],
                                                    rows[:, TT:TT + 1], scale))
                    vch(nc.vector.tensor_scalar(rows[:, 2 * TT:2 * TT + NKT], dropr[:, :],
                                                -1e30, rows[:, TT:TT + 1],
                                                ALU.mult, ALU.subtract))
                    ve.inc(nc.vector.tensor_copy(rowsb[:, 0:NKT], rows[:, 2 * TT:2 * TT + NKT]))
                    nc.tensor.wait_ge(ve.h, ve.n)
                    pe.inc(nc.tensor.matmul(psB[:, 0:NKT], ones[0:1, :], rowsb[:, 0:NKT],
                                            start=True, stop=True))
                    nc.scalar.wait_ge(pe.h, pe.n)
                    ac.inc(nc.scalar.copy(biasT[:, hi * NKT:(hi + 1) * NKT], psB[:, 0:NKT]))

                vs_b = [vsl[0].n, vsl[1].n]
                pe_kt = pe.n
                steps = [(ki, kt, hi) for ki, kt in enumerate(kts) for hi in (0, 1)]
                SN = len(steps)
                qk_n = [0] * SN
                pv_n = [0] * SN
                exp_n = [0] * SN
                psSx = [psS, psS2]

                def v_load(ki):
                    if ki >= 2:
                        nc.sync.wait_ge(pe.h, pv_n[2 * (ki - 2) + 1])
                    else:
                        nc.sync.wait_ge(pe.h, pe_kt)
                    vsl[ki % 2].inc(nc.sync.dma_start(out=vtb[ki % 2][:, :],
                                                      in_=vd[kts[ki], :, :]), 16)
                v_load(0)
                if nkt > 1:
                    v_load(1)
                ac_kt = ac.n

                def emit_qk(s):
                    ki2, kt2, hi2 = steps[s]
                    g2 = (2 * hp + hi2) // 4
                    if s >= 2:
                        nc.tensor.wait_ge(ac.h, exp_n[s - 2])
                    else:
                        nc.tensor.wait_ge(ac.h, ac_kt)
                    pe.inc(nc.tensor.matmul(psSx[s % 2][:, :],
                                            kall[g2][:, kt2 * 128:(kt2 + 1) * 128],
                                            qpr[:, hi2 * TT:(hi2 + 1) * TT],
                                            start=True, stop=True))
                    qk_n[s] = pe.n

                emit_qk(0)
                for s in range(SN):
                    ki, kt, hi = steps[s]
                    g = (2 * hp + hi) // 4
                    diag = kt < 16 and kt >= 4 * qt
                    nc.scalar.wait_ge(pe.h, qk_n[s])
                    if s >= 2:
                        nc.scalar.wait_ge(pe.h, pv_n[s - 2])
                    ac.inc(nc.scalar.activation(
                        pexp[hi][:, :], psSx[s % 2][:, :], AF.Exp, scale=scale,
                        bias=biasT[:, hi * NKT + kt: hi * NKT + kt + 1]))
                    exp_n[s] = ac.n
                    use_sem_h, use_n = ac.h, exp_n[s]
                    if diag:
                        nc.vector.wait_ge(ac.h, exp_n[s])
                        ve.inc(nc.vector.tensor_tensor(
                            pexp[hi][:, :], pexp[hi][:, :],
                            dmask[:, (kt - 4 * qt) * TT:(kt - 4 * qt + 1) * TT], ALU.mult))
                        use_sem_h, use_n = ve.h, ve.n
                    if s + 1 < SN:
                        emit_qk(s + 1)
                    first, last = (ki == 0), (ki == nkt - 1)
                    nc.tensor.wait_ge(use_sem_h, use_n)
                    nc.tensor.wait_ge(vsl[ki % 2].h, vs_b[ki % 2] + 16 * (ki // 2 + 1))
                    nc.tensor.matmul(psO[2 * hi][:, :],
                                     vtb[ki % 2][:, g * 256:g * 256 + 128],
                                     pexp[hi][:, :], start=first, stop=last)
                    nc.tensor.matmul(psO[2 * hi + 1][:, :],
                                     vtb[ki % 2][:, g * 256 + 128:g * 256 + 256],
                                     pexp[hi][:, :], start=first, stop=last)
                    ldst = psL[0:1, :] if hi == 0 else psB[0:1, 0:TT]
                    pe.inc(nc.tensor.matmul(ldst, ones[:, 0:1], pexp[hi][:, :],
                                            start=first, stop=last))
                    pv_n[s] = pe.n
                    if hi == 1 and ki + 2 < nkt:
                        v_load(ki + 2)
                for hi, h in enumerate((2 * hp, 2 * hp + 1)):
                    nc.vector.wait_ge(pe.h, pe.n)
                    lsrc = psL[0:1, :] if hi == 0 else psB[0:1, 0:TT]
                    vch(nc.vector.reciprocal(rows[:, 0:TT], lsrc))
                    ve.inc(nc.vector.tensor_copy(rowsb[:, 0:TT], rows[:, 0:TT]))
                    nc.tensor.wait_ge(ve.h, ve.n)
                    nc.tensor.wait_ge(ac.h, ac.n)
                    pe.inc(nc.tensor.matmul(psS[:, :], ones[0:1, :], rowsb[:, 0:TT],
                                            start=True, stop=True))
                    nc.scalar.wait_ge(pe.h, pe.n)
                    ac.inc(nc.scalar.copy(rlb[:, :], psS[:, :]))
                    nc.vector.wait_ge(ac.h, ac.n)
                    nc.gpsimd.wait_ge(ac.h, ac.n)
                    ve.inc(nc.vector.tensor_tensor(
                        oa[:, (h * 2) * TT:(h * 2 + 1) * TT],
                        psO[2 * hi][:, :], rlb[:, :], ALU.mult))
                    ve.inc(nc.vector.tensor_tensor(
                        oa[:, (h * 2 + 1) * TT:(h * 2 + 2) * TT],
                        psO[2 * hi + 1][:, :], rlb[:, :], ALU.mult))
                nc.tensor.wait_ge(ve.h, ve.n)

            ws_b = wsl[0].n
            pe_wo = pe.n

            def wo_load(dc):
                nc.gpsimd.wait_ge(pe.h, pe_wo + dc)
                wsl[0].inc(nc.gpsimd.dma_start(out=wob[:, :], in_=wos[dc, :, :]), 16)
            st_b = [stsl[0].n, stsl[1].n]
            xs_b = [xsl[0].n, xsl[1].n]
            for dc in range(DC):
                wo_load(dc)
                nc.tensor.wait_ge(wsl[0].h, ws_b + 16 * (dc + 1))
                if dc >= 1:
                    nc.tensor.wait_ge(ve.h, ve.n)
                if dc >= 2:
                    nc.sync.wait_ge(stsl[dc % 2].h, st_b[dc % 2] + 16 * (dc // 2))
                xsl[dc % 2].inc(nc.sync.dma_start(out=sf1[:, :] if dc % 2 == 0 else sf2[:, :],
                                                  in_=xloc[dc, :, qt * TT:(qt + 1) * TT]), 16)
                for hv in range(32):
                    ins = nc.tensor.matmul(psS[:, :], wob[:, hv * 128:(hv + 1) * 128],
                                           oa[:, hv * TT:(hv + 1) * TT],
                                           start=(hv == 0), stop=(hv == 31))
                pe.inc(ins)
                nc.vector.wait_ge(pe.h, pe.n)
                nc.vector.wait_ge(xsl[dc % 2].h, xs_b[dc % 2] + 16 * (dc // 2 + 1))
                if dc >= 2:
                    nc.vector.wait_ge(stsl[dc % 2].h, st_b[dc % 2] + 16 * (dc // 2))
                else:
                    nc.vector.wait_ge(stsl[dc % 2].h, st_b[dc % 2])
                ve.inc(nc.vector.tensor_add(xot[dc % 2][:, :], psS[:, :],
                                            sf1[:, :] if dc % 2 == 0 else sf2[:, :]))
                nc.scalar.wait_ge(ve.h, ve.n)
                stsl[dc % 2].inc(nc.scalar.dma_start(
                    out=xo[dc, :, qt * TT:(qt + 1) * TT], in_=xot[dc % 2][:, :]), 16)

        nc.sync.wait_ge(stsl[0].h, stsl[0].n)
        nc.sync.wait_ge(stsl[1].h, stsl[1].n)
        nc.vector.wait_ge(stsl[0].h, stsl[0].n)
        nc.vector.wait_ge(stsl[1].h, stsl[1].n)


ATTN_DRAM = dict(
    xloc=([DC, 128, T], "F32", "ExternalInput"),
    xrem=([DC, 128, T], "F32", "ExternalInput"),
    wdqs=([1, 128, 2048], "F32", "ExternalInput"),
    wdkvs=([2, 128, 2048], "F32", "ExternalInput"),
    wuqr=([128, 2048], "F32", "ExternalInput"),
    wukv=([128, 1280], "F32", "ExternalInput"),
    wos=([16, 128, 4096], "F32", "ExternalInput"),
    gq=([128, 1], "F32", "ExternalInput"),
    gkv=([128, 1], "F32", "ExternalInput"),
    qA=([128, T], "BF16", "ExternalInput"),
    qB=([128, T], "BF16", "ExternalInput"),
    kA=([64, 2 * T], "BF16", "ExternalInput"),
    kB=([64, 2 * T], "BF16", "ExternalInput"),
    dmaskc=([4, 128, TT], "BF16", "ExternalInput"),
    droprc=([1, 32], "F32", "ExternalInput"),
    onesc=([128, 128], "BF16", "ExternalInput"),
    xo=([DC, 128, T], "F32", "ExternalOutput"),
    qd=([16, 128, T], "BF16", None),
    vd=([32, 128, 1024], "BF16", None),
)


def declare_drams(nc, spec, prefix=""):
    aps = {}
    for nm, (shape, dtname, kind) in spec.items():
        dt_ = F32 if dtname == "F32" else BF16
        full = prefix + nm
        aps[nm] = (nc.dram_tensor(full, shape, dt_, kind=kind) if kind
                   else nc.dram_tensor(full, shape, dt_)).ap()
    return aps


def build_attn():
    nc = bass.Bass("TRN2", target_bir_lowering=False, debug=False)
    aps = declare_drams(nc, ATTN_DRAM)
    es = contextlib.ExitStack()
    with es:
        sm = make_sems(nc, es)
        emit_attn(nc, es, sm, **aps)
    return nc


def run_attn(h, xs):
    im = []
    for c in range(NCORES):
        half = c & 1
        im.append({"xloc": xs[c], "xrem": xs[c ^ 1], "wdqs": h["wdqs"],
                   "wdkvs": h["wdkvs"], "wuqr": h["wuqr"], "wukv": h["wukv"],
                   "wos": h["wos"], "gq": h["gq"], "gkv": h["gkv"],
                   "qA": h[f"qA{half}"], "qB": h[f"qB{half}"],
                   "kA": h[f"kA{half}"], "kB": h[f"kB{half}"],
                   "dmaskc": h["dmaskc"], "droprc": h[f"drop{half}"],
                   "onesc": h["onesc"]})
    res = _run("attn", build_attn, im)
    return [res.results[c]["xo"] for c in range(NCORES)]


PROJC = 67
CONVC = 34


def mamba_dram_spec(TM):
    return dict(
        xfm=([DC, 128, TM], "F32", "ExternalInput"),
        wins=([PROJC, 128, 2048], "F32", "ExternalInput"),
        wouts=([DC, 128, 4096], "F32", "ExternalInput"),
        convw=([128, CONVC * 4], "F32", "ExternalInput"),
        convbc=([128, CONVC], "F32", "ExternalInput"),
        nrmw=([128, 32], "F32", "ExternalInput"),
        dexp=([128, 32], "F32", "ExternalInput"),
        abias=([64, 2], "F32", "ExternalInput"),
        identb=([128, 128], "BF16", "ExternalInput"),
        identf=([128, 128], "F32", "ExternalInput"),
        onesc=([128, 128], "BF16", "ExternalInput"),
        onesf=([128, 128], "F32", "ExternalInput"),
        maskc=([128, 128], "BF16", "ExternalInput"),
        xo=([DC, 128, TM], "F32", "ExternalOutput"),
        projD=([PROJC, 128, TM], "BF16", None),
        convD=([CONVC, 128, TM], "BF16", None),
        ynD=([32, 128, TM], "BF16", None),
        crowD=([64, 128], "F32", None),
    )


def emit_mamba(nc, es, sm, TM, xfm, wins, wouts, convw, convbc, nrmw, dexp,
               abias, identb, identf, onesc, onesf, maskc, xo, projD, convD,
               ynD, crowD):
    NCH = TM // 128
    NT = TM // TT
    ar1 = es.enter_context(nc.sbuf_tensor(uq("ar1"), [128, 16384], BF16))
    ones = es.enter_context(nc.sbuf_tensor(uq("ones"), [128, 128], BF16))
    onf = es.enter_context(nc.sbuf_tensor(uq("onf"), [128, 128], F32))
    idb = es.enter_context(nc.sbuf_tensor(uq("idb"), [128, 128], BF16))
    idf = es.enter_context(nc.sbuf_tensor(uq("idf"), [128, 128], F32))
    mskb = es.enter_context(nc.sbuf_tensor(uq("mskb"), [128, 128], BF16))
    cwb = es.enter_context(nc.sbuf_tensor(uq("cwb"), [128, CONVC * 4], F32))
    cbb = es.enter_context(nc.sbuf_tensor(uq("cbb"), [128, CONVC], F32))
    nwb = es.enter_context(nc.sbuf_tensor(uq("nwb"), [128, 32], F32))
    dxb = es.enter_context(nc.sbuf_tensor(uq("dxb"), [128, 32], F32))
    abb = es.enter_context(nc.sbuf_tensor(uq("abb"), [64, 2], F32))
    winb = [es.enter_context(nc.sbuf_tensor(uq(f"winb{i}"), [128, 2048], BF16)) for i in range(2)]
    ptile = [es.enter_context(nc.sbuf_tensor(uq(f"ptile{i}"), [128, TT], BF16)) for i in range(2)]
    dgt2 = [es.enter_context(nc.sbuf_tensor(uq(f"dgt{i}"), [128, 4 * 128], BF16)) for i in range(2)]
    cvt2 = [es.enter_context(nc.sbuf_tensor(uq(f"cvt{i}"), [128, TT + 3], BF16)) for i in range(2)]
    S = es.enter_context(nc.sbuf_tensor(uq("Sst"), [128, 4096], F32))
    Sb = es.enter_context(nc.sbuf_tensor(uq("Sbst"), [128, 4096], BF16))
    d64 = es.enter_context(nc.sbuf_tensor(uq("d64"), [64, 128 * 8], F32))
    negA = es.enter_context(nc.sbuf_tensor(uq("negA"), [64, 1], F32))
    dtraw = [es.enter_context(nc.sbuf_tensor(uq(f"dtraw{i}"), [64, 128], BF16)) for i in range(2)]
    dtpT = es.enter_context(nc.sbuf_tensor(uq("dtpT"), [128, 64], BF16))
    fT = es.enter_context(nc.sbuf_tensor(uq("fT"), [128, 64], BF16))
    cT = es.enter_context(nc.sbuf_tensor(uq("cT"), [128, 64], F32))
    crow2 = [es.enter_context(nc.sbuf_tensor(uq(f"crow2_{i}"), [1, 512], F32)) for i in range(2)]
    crowE2 = [es.enter_context(nc.sbuf_tensor(uq(f"crowE2_{i}"), [1, 1024], F32)) for i in range(2)]
    grow = es.enter_context(nc.sbuf_tensor(uq("grow"), [1, 64], F32))
    epsT = es.enter_context(nc.sbuf_tensor(uq("epsT"), [1, 1], F32))
    Btile = [es.enter_context(nc.sbuf_tensor(uq(f"Btile{i}"), [128, 128], BF16)) for i in range(2)]
    Ctile = [es.enter_context(nc.sbuf_tensor(uq(f"Ctile{i}"), [128, 128], BF16)) for i in range(2)]
    Btb = es.enter_context(nc.sbuf_tensor(uq("Btb"), [128, 128], BF16))
    CBm = es.enter_context(nc.sbuf_tensor(uq("CBm"), [128, 128], BF16))
    dsb = es.enter_context(nc.sbuf_tensor(uq("dsb"), [128, 512], F32))
    cpbs = [es.enter_context(nc.sbuf_tensor(uq(f"cpbs{i}"), [128, 512], F32)) for i in range(2)]
    tf1 = [es.enter_context(nc.sbuf_tensor(uq(f"tf1_{i}"), [128, 512], F32)) for i in range(2)]
    tf2 = [es.enter_context(nc.sbuf_tensor(uq(f"tf2_{i}"), [128, 512], F32)) for i in range(2)]
    gtl = [es.enter_context(nc.sbuf_tensor(uq(f"gtl{i}"), [128, 512], BF16)) for i in range(2)]
    sgt = [es.enter_context(nc.sbuf_tensor(uq(f"sgt{i}"), [128, 512], BF16)) for i in range(2)]
    rrow = es.enter_context(nc.sbuf_tensor(uq("rrow"), [1, 128], F32))
    rrow4 = es.enter_context(nc.sbuf_tensor(uq("rrow4"), [1, 512], F32))
    woutb = [es.enter_context(nc.sbuf_tensor(uq(f"woutb{i}"), [128, 4096], BF16)) for i in range(2)]
    xrt = [es.enter_context(nc.sbuf_tensor(uq(f"xrt{i}"), [128, TT], F32)) for i in range(2)]
    xot = [es.enter_context(nc.sbuf_tensor(uq(f"xot{i}"), [128, TT], F32)) for i in range(2)]

    pA = es.enter_context(nc.psum_tensor(uq("pA"), [128, TT], F32))
    pB = es.enter_context(nc.psum_tensor(uq("pB"), [128, TT], F32))
    pC = es.enter_context(nc.psum_tensor(uq("pC"), [128, TT], F32))
    pD = es.enter_context(nc.psum_tensor(uq("pD"), [128, TT], F32))
    pE = es.enter_context(nc.psum_tensor(uq("pE"), [128, TT], F32))
    pF = es.enter_context(nc.psum_tensor(uq("pF"), [128, TT], F32))
    pG = es.enter_context(nc.psum_tensor(uq("pG"), [128, TT], F32))
    pTb = es.enter_context(nc.psum_tensor(uq("pTb"), [128, TT], BF16))

    if True:
        sy, gp, pe, ac, ve, gv = (sm["sy"], sm["gp"], sm["pe"], sm["ac"],
                                  sm["ve"], sm["gv"])
        wsl, psl, gsl, xsl, stsl = (sm["wsl"], sm["psl"], sm["gsl"], sm["xsl"],
                                    sm["stsl"])
        cld, bnc = sm["cld"], sm["bnc"]
        ln = LNEmitter(nc, es, pA, pB, psr1=pC[0:1, :], psr2=pD[0:1, :],
                       xsq_buf=ar1[:, 0:DC * TT])
        xlnb = ar1[:, DC * TT:2 * DC * TT]
        sm = {"sy": sy, "pe": pe, "ac": ac, "ve": ve, "gv": gv}

        def vch(ins):
            ve.inc(ins)
            nc.vector.wait_ge(ve.h, ve.n)

        for t, d in [(ones, onesc), (onf, onesf), (idb, identb), (idf, identf),
                     (mskb, maskc), (cwb, convw), (cbb, convbc), (nwb, nrmw),
                     (dxb, dexp), (abb, abias)]:
            gp.inc(nc.gpsimd.dma_start(out=t[:, :], in_=d), 16)
        nc.tensor.wait_ge(gp.h, gp.n)
        nc.vector.wait_ge(gp.h, gp.n)
        nc.scalar.wait_ge(gp.h, gp.n)
        gv.inc(nc.gpsimd.memset(epsT[:, :], 1e-6))
        ac.inc(nc.scalar.activation(negA[:, :], abb[:, 1:2], AF.Exp))
        nc.vector.wait_ge(ac.h, ac.n)
        vch(nc.vector.tensor_scalar_mul(negA[:, :], negA[:, :], -1.0))

        for tt in range(NT):
            if tt > 0:
                nc.sync.wait_ge(gv.h, gv.n)
            ln.emit(sm, xfm, tt, lambda dc: xlnb[:, dc * TT:(dc + 1) * TT], ones)
            nc.tensor.wait_ge(gv.h, gv.n)
            ws_b = [wsl[0].n, wsl[1].n]
            ps_b = [psl[0].n, psl[1].n]
            pe_p = pe.n
            ac_p = ac.n

            def win_load(oc):
                nc.gpsimd.wait_ge(pe.h, pe_p + max(0, oc - 1))
                wsl[oc % 2].inc(nc.gpsimd.dma_start(out=winb[oc % 2][:, :],
                                                    in_=wins[oc, :, :]), 16)
            win_load(0)
            win_load(1)
            for oc in range(PROJC):
                pps = pA if oc % 2 == 0 else pB
                nc.tensor.wait_ge(wsl[oc % 2].h, ws_b[oc % 2] + 16 * (oc // 2 + 1))
                if oc >= 2:
                    nc.tensor.wait_ge(ac.h, ac_p + oc - 1)
                for dc in range(DC):
                    ins = nc.tensor.matmul(pps[:, :], winb[oc % 2][:, dc * 128:(dc + 1) * 128],
                                           xlnb[:, dc * TT:(dc + 1) * TT],
                                           start=(dc == 0), stop=(dc == DC - 1))
                pe.inc(ins)
                nc.scalar.wait_ge(pe.h, pe_p + oc + 1)
                nc.scalar.wait_ge(psl[oc % 2].h, ps_b[oc % 2] + 16 * (oc // 2))
                ac.inc(nc.scalar.copy(ptile[oc % 2][:, :], pps[:, :]))
                nc.sync.wait_ge(ac.h, ac.n)
                psl[oc % 2].inc(nc.sync.dma_start(
                    out=projD[oc, :, tt * TT:(tt + 1) * TT], in_=ptile[oc % 2][:, :]), 16)
                if oc + 2 < PROJC:
                    win_load(oc + 2)
        nc.gpsimd.wait_ge(psl[0].h, psl[0].n)
        nc.gpsimd.wait_ge(psl[1].h, psl[1].n)

        NIT = CONVC * NT
        pe_cv0 = pe.n
        ac_cv0 = ac.n
        ps_cb = [psl[0].n, psl[1].n]
        ld_snap = [0] * NIT
        ms_snap = [0] * NIT
        dgt_snap = [0] * CONVC

        def conv_load(it):
            c2, t2 = divmod(it, NT)
            b2 = it % 2
            if it >= 2:
                nc.sync.wait_ge(pe.h, pe_cv0 + it - 1)
            if t2 == 0:
                sy.inc(nc.sync.dma_start(out=cvt2[b2][:, 3:TT + 3],
                                         in_=projD[32 + c2, :, 0:TT]), 16)
                nc.gpsimd.wait_ge(pe.h, pe_cv0 + max(0, it - 1))
                gv.inc(nc.gpsimd.memset(cvt2[b2][:, 0:3], 0.0))
            else:
                sy.inc(nc.sync.dma_start(
                    out=cvt2[b2][:, :],
                    in_=projD[32 + c2, :, t2 * TT - 3:(t2 + 1) * TT]), 16)
            ld_snap[it] = sy.n
            ms_snap[it] = gv.n

        def dgt_build(c2):
            if c2 >= 2:
                nc.vector.wait_ge(pe.h, pe_cv0 + NT * (c2 - 1))
            for k in range(4):
                ins = nc.vector.tensor_scalar(dgt2[c2 % 2][:, k * 128:(k + 1) * 128],
                                              idb[:, :],
                                              cwb[:, c2 * 4 + k:c2 * 4 + k + 1], None,
                                              ALU.mult)
            ve.inc(ins)
            dgt_snap[c2] = ve.n

        nc.vector.wait_ge(pe.h, pe.n)
        dgt_build(0)
        dgt_build(1)
        conv_load(0)
        conv_load(1)
        for it in range(NIT):
            c, tt = divmod(it, NT)
            b = it % 2
            bank = pA if b == 0 else pB
            nc.tensor.wait_ge(sy.h, ld_snap[it])
            nc.tensor.wait_ge(gv.h, ms_snap[it])
            nc.tensor.wait_ge(ve.h, dgt_snap[c])
            if it >= 2:
                nc.tensor.wait_ge(ac.h, ac_cv0 + it - 1)
            for k in range(4):
                ins = nc.tensor.matmul(bank[:, :], dgt2[c % 2][:, k * 128:(k + 1) * 128],
                                       cvt2[b][:, k:k + TT],
                                       start=(k == 0), stop=(k == 3))
            pe.inc(ins)
            if it + 2 < NIT:
                conv_load(it + 2)
            if tt == 0 and c + 2 < CONVC:
                dgt_build(c + 2)
            nc.scalar.wait_ge(pe.h, pe_cv0 + it + 1)
            nc.scalar.wait_ge(psl[b].h, ps_cb[b] + 16 * (it // 2))
            ac.inc(nc.scalar.activation(ptile[b][:, :], bank[:, :], AF.Silu,
                                        bias=cbb[:, c:c + 1]))
            nc.sync.wait_ge(ac.h, ac_cv0 + it + 1)
            psl[b].inc(nc.sync.dma_start(
                out=convD[c, :, tt * TT:(tt + 1) * TT], in_=ptile[b][:, :]), 16)
        nc.gpsimd.wait_ge(psl[0].h, psl[0].n)
        nc.gpsimd.wait_ge(psl[1].h, psl[1].n)

        xa_bufs = [ar1[:, 12288:16384], ar1[:, 8192:12288]]
        sqsb = Sb[:, :]
        sT = ar1[:, 0:8192]
        yg = ln.xt[:, 0:4096]
        Xp = ln.xt[:, 4096:6144].bitcast(BF16)
        Xpp = ln.xt[:, 6144:8192].bitcast(BF16)
        nc.vector.wait_ge(gv.h, gv.n)
        ve.inc(nc.vector.memset(S[:, :], 0.0))
        nc.sync.wait_ge(psl[0].h, psl[0].n)
        nc.sync.wait_ge(psl[1].h, psl[1].n)

        def chunk_loads(ch2, eng, sem):
            b2 = ch2 % 2
            cc2 = slice(ch2 * 128, (ch2 + 1) * 128)
            for ic in range(32):
                sem.inc(eng.dma_start(out=xa_bufs[b2][:, ic * 128:(ic + 1) * 128],
                                      in_=convD[ic, :, cc2]), 16)
            sem.inc(eng.dma_start(out=Btile[b2][:, :], in_=convD[32, :, cc2]), 16)
            sem.inc(eng.dma_start(out=Ctile[b2][:, :], in_=convD[33, :, cc2]), 16)
            sem.inc(eng.dma_start(out=dtraw[b2][:, :], in_=projD[66, 0:64, cc2]), 16)

        chunk_loads(0, nc.sync, cld)
        cld_snap = [cld.n]
        prev = {}
        for ch in range(NCH):
            buf = ch % 2
            xa = xa_bufs[buf]
            Bt_c, Ct_c, dtr = Btile[buf], Ctile[buf], dtraw[buf]
            ccol = slice(ch * 128, (ch + 1) * 128)
            zt = d64[:, 0:128]
            a1 = d64[:, 128:256]
            a2 = d64[:, 256:384]
            dtp = d64[:, 384:512]
            s_t = d64[:, 512:640]
            chm = d64[:, 640:768]
            fF = d64[:, 768:896]
            nc.vector.wait_ge(cld.h, cld_snap[ch])
            if ch > 0:
                nc.vector.wait_ge(bnc.h, prev["bnc"])
                nc.vector.wait_ge(pe.h, prev["pe_tr"])
            vch(nc.vector.tensor_scalar(zt, dtr[:, :], abb[:, 0:1], None, ALU.add))
            nc.scalar.wait_ge(ve.h, ve.n)
            ac.inc(nc.scalar.activation(a1, zt, AF.Abs))
            nc.scalar.wait_ge(ac.h, ac.n)
            ac.inc(nc.scalar.activation(a1, a1, AF.Exp, scale=-1.0))
            nc.scalar.wait_ge(ac.h, ac.n)
            ac.inc(nc.scalar.activation(a1, a1, AF.Ln, bias=1.0))
            ac.inc(nc.scalar.activation(a2, zt, AF.Relu))
            nc.vector.wait_ge(ac.h, ac.n)
            vch(nc.vector.tensor_add(dtp, a1, a2))
            vch(nc.vector.tensor_scalar_min(dtp, dtp, 1e6))
            vch(nc.vector.tensor_scalar(s_t, dtp, negA[:, :], None, ALU.mult))
            vch(nc.vector.tensor_tensor_scan(chm, s_t, s_t, 0.0, ALU.add, ALU.bypass))
            vch(nc.vector.tensor_scalar(fF, chm, chm[:, 127:128], None, ALU.subtract))
            nc.scalar.wait_ge(ve.h, ve.n)
            ac.inc(nc.scalar.activation(fF, fF, AF.Exp, scale=-1.0))
            nc.sync.wait_ge(ve.h, ve.n)
            bnc.inc(nc.sync.dma_start(out=crowD[:, :], in_=chm), 16)
            bnc_chm = bnc.n
            nc.sync.wait_ge(bnc.h, bnc_chm)
            if ch > 0:
                nc.sync.wait_ge(pe.h, prev["pe_state"])
            bnc.inc(nc.sync.dma_start(out=grow[:, :],
                                      in_=crowD[:, 127:128].transpose([1, 0]),
                                      allow_slow_non_contiguous=True), 16)
            bnc_grow = bnc.n
            nc.scalar.wait_ge(bnc.h, bnc_grow)
            ac.inc(nc.scalar.activation(grow[:, :], grow[:, :], AF.Exp))
            ac_grow = ac.n
            nc.tensor.wait_ge(ve.h, ve.n)
            nc.tensor.wait_ge(ac.h, ac.n)
            pe.inc(nc.tensor.transpose(pC[:, 0:64], chm, idf[0:64, 0:64]))
            nc.scalar.wait_ge(pe.h, pe.n)
            ac.inc(nc.scalar.copy(cT[:, :], pC[:, 0:64]))
            nc.tensor.wait_ge(ac.h, ac.n)
            pe.inc(nc.tensor.transpose(pC[:, 0:64], dtp, idf[0:64, 0:64]))
            nc.scalar.wait_ge(pe.h, pe.n)
            ac.inc(nc.scalar.copy(dtpT[:, :], pC[:, 0:64]))
            nc.tensor.wait_ge(ac.h, ac.n)
            pe.inc(nc.tensor.transpose(pC[:, 0:64], fF, idf[0:64, 0:64]))
            nc.scalar.wait_ge(pe.h, pe.n)
            ac.inc(nc.scalar.copy(fT[:, :], pC[:, 0:64]))
            nc.tensor.wait_ge(ac.h, ac.n)
            pe.inc(nc.tensor.matmul(pC[:, 0:128], Bt_c[:, :], Ct_c[:, :],
                                    start=True, stop=True))
            nc.vector.wait_ge(pe.h, pe.n)
            vch(nc.vector.tensor_tensor(CBm[:, :], pC[:, 0:128], mskb[:, :], ALU.mult))
            pe.inc(nc.tensor.transpose(pTb[:, 0:128], Bt_c[:, :], idb[:, :]))
            nc.scalar.wait_ge(pe.h, pe.n)
            ac.inc(nc.scalar.copy(Btb[:, :], pTb[:, 0:128]))
            nc.tensor.wait_ge(ac.h, ac.n)
            ve_xt = ve.n
            for grp in range(8):
                if grp > 0:
                    nc.tensor.wait_ge(ve.h, ve_xt + grp)
                for j in range(4):
                    ic = grp * 4 + j
                    ins = nc.tensor.transpose(pTb[:, j * 128:(j + 1) * 128],
                                              xa[:, ic * 128:(ic + 1) * 128], idb[:, :])
                pe.inc(ins)
                nc.vector.wait_ge(pe.h, pe.n)
                ve.inc(nc.vector.tensor_tensor(
                    Xp[:, grp * 512:(grp + 1) * 512].rearrange("p (h q) -> p h q", h=8),
                    pTb[:, 0:512].rearrange("p (h q) -> p h q", h=8),
                    dtpT[:, grp * 8:(grp + 1) * 8].unsqueeze(2).broadcast_to([128, 8, 64]), ALU.mult))
            pe_tr = pe.n
            nc.gpsimd.wait_ge(ve.h, ve.n)
            if ch > 0:
                nc.gpsimd.wait_ge(pe.h, prev["pe_state"])
            gv.inc(nc.gpsimd.tensor_tensor(
                Xpp[:, :].rearrange("p (h q) -> p h q", h=64),
                Xp[:, :].rearrange("p (h q) -> p h q", h=64),
                fT[:, :].unsqueeze(2).broadcast_to([128, 64, 64]),
                ALU.mult))
            gv_xpp = gv.n
            nc.vector.wait_ge(pe.h, pe.n)
            vch(nc.vector.tensor_copy(Sb[:, :], S[:, :]))
            ve_mk = ve.n
            ac_mk = ac.n
            pe_mk = pe.n
            for pc_i in range(16):
                hb = pc_i * 4
                if pc_i == 0:
                    nc.sync.wait_ge(bnc.h, bnc_chm)
                nc.sync.wait_ge(pe.h, pe_mk + max(0, pc_i - 1))
                bnc.inc(nc.sync.dma_start(
                    out=crow2[pc_i % 2][:, :],
                    in_=crowD[hb:hb + 4, :].rearrange("a b -> (a b)").unsqueeze(0)), 16)
                bnc_mk = bnc.n
                nc.tensor.wait_ge(bnc.h, bnc_mk)
                if pc_i == 0:
                    nc.tensor.wait_ge(ve.h, ve.n)
                else:
                    nc.tensor.wait_ge(ve.h, ve_mk + 3 * (pc_i - 1) + 1)
                pe.inc(nc.tensor.matmul(pA[:, :], onf[0:1, :], crow2[pc_i % 2][:, :],
                                        start=True, stop=True))
                nc.vector.wait_ge(pe.h, pe_mk + pc_i + 1)
                ve.inc(nc.vector.tensor_tensor(
                    dsb[:, :].rearrange("p (h t) -> p h t", h=4),
                    pA[:, :].rearrange("p (h t) -> p h t", h=4),
                    cT[:, hb:hb + 4].unsqueeze(2).broadcast_to([128, 4, 128]),
                    ALU.subtract))
                ve.inc(nc.vector.tensor_scalar_min(dsb[:, :], dsb[:, :], 0.0))
                nc.scalar.wait_ge(ve.h, ve_mk + 3 * pc_i + 2)
                ac.inc(nc.scalar.activation(dsb[:, :], dsb[:, :], AF.Exp))
                nc.vector.wait_ge(ac.h, ac_mk + pc_i + 1)
                ve.inc(nc.vector.tensor_tensor(
                    sT[:, pc_i * 512:(pc_i + 1) * 512].rearrange("p (h t) -> p h t", h=4),
                    dsb[:, :].rearrange("p (h t) -> p h t", h=4),
                    CBm[:, :].unsqueeze(1).broadcast_to([128, 4, 128]),
                    ALU.mult))
            pe_mask_end = pe.n
            ysets = [(pB, pC, pD), (pE, pF, pG)]
            pe_y0 = pe.n
            ac_y0 = ac.n
            ve_y0 = ve.n
            gv_y0 = gv.n
            gl_b = [gsl[0].n, gsl[1].n]
            ce_snap = [0] * 8

            def gate_load(g2):
                if g2 >= 2:
                    nc.sync.wait_ge(ac.h, ac_y0 + 3 * (g2 - 2) + 3)
                else:
                    nc.sync.wait_ge(ac.h, ac_y0)
                for j2 in range(4):
                    gsl[g2 % 2].inc(nc.sync.dma_start(
                        out=gtl[g2 % 2][:, j2 * 128:(j2 + 1) * 128],
                        in_=projD[4 * g2 + j2, :, ccol]), 16)

            def crowe_load(g2):
                if g2 >= 2:
                    nc.sync.wait_ge(pe.h, pe_y0 + 3 * (g2 - 2) + 1)
                elif ch > 0:
                    nc.sync.wait_ge(pe.h, prev["pe_y_end"])
                bnc.inc(nc.sync.dma_start(
                    out=crowE2[g2 % 2][:, :],
                    in_=crowD[8 * g2:8 * g2 + 8, :].rearrange("a b -> (a b)").unsqueeze(0)), 16)
                ce_snap[g2] = bnc.n

            gate_load(0)
            crowe_load(0)
            gate_load(1)
            crowe_load(1)
            for g in range(8):
                pBg, pCg, pDg = ysets[g % 2]
                nc.scalar.wait_ge(bnc.h, ce_snap[g])
                if g >= 2:
                    nc.scalar.wait_ge(pe.h, pe_y0 + 3 * (g - 2) + 1)
                ac.inc(nc.scalar.activation(crowE2[g % 2][:, :], crowE2[g % 2][:, :],
                                            AF.Exp))
                nc.tensor.wait_ge(ac.h, ac_y0 + 3 * g + 1)
                if g >= 2:
                    nc.tensor.wait_ge(ac.h, ac_y0 + 3 * (g - 2) + 2)
                    nc.tensor.wait_ge(ve.h, ve_y0 + 4 * (g - 2) + 2)
                nc.tensor.wait_ge(ve.h, ve_mk + 3 * (2 * g + 2))
                for j in range(4):
                    nc.tensor.matmul(pDg[0:64, j * 128:(j + 1) * 128], onf[0:1, 0:64],
                                     crowE2[g % 2][:, (2 * j) * 128:(2 * j + 1) * 128],
                                     start=True, stop=True)
                    ins = nc.tensor.matmul(
                        pDg[64:128, j * 128:(j + 1) * 128], onf[0:1, 0:64],
                        crowE2[g % 2][:, (2 * j + 1) * 128:(2 * j + 2) * 128],
                        start=True, stop=True)
                pe.inc(ins)
                for j in range(4):
                    pr = 4 * g + j
                    ins = nc.tensor.matmul(pBg[:, j * 128:(j + 1) * 128],
                                           Sb[:, pr * 128:(pr + 1) * 128], Ct_c[:, :],
                                           start=True, stop=True)
                pe.inc(ins)
                for j in range(4):
                    pr = 4 * g + j
                    nc.tensor.matmul(pCg[0:64, j * 128:(j + 1) * 128],
                                     Xp[:, pr * 128:pr * 128 + 64],
                                     sT[:, (2 * pr) * 128:(2 * pr + 1) * 128],
                                     start=True, stop=True)
                    ins = nc.tensor.matmul(pCg[64:128, j * 128:(j + 1) * 128],
                                           Xp[:, pr * 128 + 64:(pr + 1) * 128],
                                           sT[:, (2 * pr + 1) * 128:(2 * pr + 2) * 128],
                                           start=True, stop=True)
                pe.inc(ins)
                nc.scalar.wait_ge(pe.h, pe_y0 + 3 * g + 1)
                if g >= 2:
                    nc.scalar.wait_ge(ve.h, ve_y0 + 4 * (g - 2) + 1)
                else:
                    nc.scalar.wait_ge(ve.h, ve_y0)
                ac.inc(nc.scalar.copy(cpbs[g % 2][:, :], pDg[:, 0:512]))
                nc.scalar.wait_ge(gsl[g % 2].h, gl_b[g % 2] + 64 * (g // 2 + 1))
                if g >= 2:
                    nc.scalar.wait_ge(gv.h, gv_y0 + (g - 2) + 1)
                ac.inc(nc.scalar.activation(sgt[g % 2][:, :], gtl[g % 2][:, :],
                                            AF.Silu))
                nc.vector.wait_ge(pe.h, pe_y0 + 3 * g + 2)
                nc.vector.wait_ge(ac.h, ac_y0 + 3 * g + 2)
                if g >= 2:
                    nc.vector.wait_ge(gv.h, gv_y0 + (g - 2) + 1)
                ve.inc(nc.vector.tensor_tensor(tf1[g % 2][:, :], pBg[:, 0:512],
                                               cpbs[g % 2][:, :], ALU.mult))
                nc.vector.wait_ge(pe.h, pe_y0 + 3 * g + 3)
                ve.inc(nc.vector.tensor_tensor(tf1[g % 2][:, :], pCg[:, 0:512],
                                               tf1[g % 2][:, :], ALU.add))
                ve.inc(nc.vector.tensor_tensor(
                    tf2[g % 2][:, :].rearrange("p (i q) -> p i q", q=128),
                    xa[:, g * 512:(g + 1) * 512].rearrange("p (i q) -> p i q", q=128),
                    dxb[:, 4 * g:4 * g + 4].unsqueeze(2).broadcast_to([128, 4, 128]),
                    ALU.mult))
                ve.inc(nc.vector.tensor_tensor(tf1[g % 2][:, :], tf1[g % 2][:, :],
                                               tf2[g % 2][:, :], ALU.add))
                nc.gpsimd.wait_ge(ve.h, ve_y0 + 4 * g + 4)
                nc.gpsimd.wait_ge(ac.h, ac_y0 + 3 * g + 3)
                gv.inc(nc.gpsimd.tensor_mul(yg[:, g * 512:(g + 1) * 512],
                                            tf1[g % 2][:, :], sgt[g % 2][:, :]))
                if g + 2 < 8:
                    gate_load(g + 2)
                    crowe_load(g + 2)
            pe_y_end = pe.n
            if ch + 1 < NCH:
                chunk_loads(ch + 1, nc.gpsimd, cld)
                cld_snap.append(cld.n)
            pe_st0 = pe.n
            ve_st0 = ve.n
            for pc_i in range(8):
                cols = slice(pc_i * 512, (pc_i + 1) * 512)
                if pc_i == 0:
                    nc.tensor.wait_ge(ve.h, ve_y0 + 32)
                    nc.tensor.wait_ge(gv.h, gv_xpp)
                    nc.tensor.wait_ge(ac.h, ac_grow)
                else:
                    nc.tensor.wait_ge(ve.h, ve_st0 + 2 * (pc_i - 1) + 2)
                pe.inc(nc.tensor.matmul(pA[:, :], Btb[:, :], Xpp[:, cols],
                                        start=True, stop=True))
                pe.inc(nc.tensor.matmul(pB[:, :], onf[0:1, :],
                                        grow[:, :].unsqueeze(2).broadcast_to([1, 64, 64])[:, pc_i * 8:(pc_i + 1) * 8, :],
                                        start=True, stop=True))
                nc.vector.wait_ge(pe.h, pe_st0 + 2 * pc_i + 2)
                ve.inc(nc.vector.tensor_tensor(S[:, cols], S[:, cols], pB[:, :], ALU.mult))
                ve.inc(nc.vector.tensor_tensor(S[:, cols], S[:, cols], pA[:, :], ALU.add))
            pe_state = pe.n
            nc.scalar.wait_ge(gv.h, gv.n)
            nc.scalar.wait_ge(pe.h, pe_y_end)
            ac.inc(nc.scalar.activation(sqsb[:, :], yg[:, :], AF.Square))
            nc.tensor.wait_ge(ac.h, ac.n)
            nc.tensor.wait_ge(ve.h, ve.n)
            for ic in range(32):
                ins = nc.tensor.matmul(pD[0:1, 0:128], ones[:, 0:1],
                                       sqsb[:, ic * 128:(ic + 1) * 128],
                                       start=(ic == 0), stop=(ic == 31))
            pe.inc(ins)
            nc.scalar.wait_ge(pe.h, pe.n)
            ac.inc(nc.scalar.activation(rrow[:, :], pD[0:1, 0:128], AF.Ln,
                                        scale=1.0 / 4096.0, bias=epsT[:, :]))
            nc.scalar.wait_ge(ac.h, ac.n)
            ac.inc(nc.scalar.activation(rrow[:, :], rrow[:, :], AF.Exp, scale=-0.5))
            nc.scalar.wait_ge(ac.h, ac.n)
            for r4 in range(4):
                ac.inc(nc.scalar.copy(rrow4[:, r4 * 128:(r4 + 1) * 128], rrow[:, :]))
            nc.tensor.wait_ge(ac.h, ac.n)
            pe.inc(nc.tensor.matmul(pD[:, 0:512], onf[0:1, :], rrow4[:, :],
                                    start=True, stop=True))
            st_b = [stsl[0].n, stsl[1].n]
            ve_rms = ve.n
            nc.vector.wait_ge(pe.h, pe.n)
            nc.vector.wait_ge(gv.h, gv.n)
            for k in range(8):
                if k >= 2:
                    nc.vector.wait_ge(stsl[k % 2].h, st_b[k % 2] + 64 * (k // 2))
                ve.inc(nc.vector.tensor_tensor(tf1[k % 2][:, :],
                                               yg[:, k * 512:(k + 1) * 512],
                                               pD[:, 0:512], ALU.mult))
                ve.inc(nc.vector.tensor_tensor(
                    sgt[k % 2][:, :].rearrange("p (i q) -> p i q", q=128),
                    tf1[k % 2][:, :].rearrange("p (i q) -> p i q", q=128),
                    nwb[:, 4 * k:4 * k + 4].unsqueeze(2).broadcast_to([128, 4, 128]),
                    ALU.mult))
                nc.scalar.wait_ge(ve.h, ve_rms + 2 * k + 2)
                for j in range(4):
                    stsl[k % 2].inc(nc.scalar.dma_start(
                        out=ynD[4 * k + j, :, ccol],
                        in_=sgt[k % 2][:, j * 128:(j + 1) * 128]), 16)
            prev = {"bnc": bnc.n, "pe_tr": pe_tr, "pe_state": pe_state,
                    "pe_y_end": pe_y_end}

        nc.gpsimd.wait_ge(stsl[0].h, stsl[0].n)
        nc.gpsimd.wait_ge(stsl[1].h, stsl[1].n)
        nc.sync.wait_ge(stsl[0].h, stsl[0].n)
        nc.sync.wait_ge(stsl[1].h, stsl[1].n)
        ynt = ar1[:, 0:16384]
        for tt in range(NT):
            yl_b = cld.n
            nc.sync.wait_ge(pe.h, pe.n)
            for ic in range(32):
                cld.inc(nc.sync.dma_start(out=ynt[:, ic * TT:(ic + 1) * TT],
                                          in_=ynD[ic, :, tt * TT:(tt + 1) * TT]), 16)
            ws_b = [wsl[0].n, wsl[1].n]
            xs_b = [xsl[0].n, xsl[1].n]
            st_b = [stsl[0].n, stsl[1].n]
            pe_w = pe.n

            def wout_load(dc):
                nc.gpsimd.wait_ge(pe.h, pe_w + max(0, dc - 1))
                wsl[dc % 2].inc(nc.gpsimd.dma_start(out=woutb[dc % 2][:, :],
                                                    in_=wouts[dc, :, :]), 16)
            wout_load(0)
            wout_load(1)
            ve_w = ve.n
            for dc in range(DC):
                nc.tensor.wait_ge(wsl[dc % 2].h, ws_b[dc % 2] + 16 * (dc // 2 + 1))
                nc.tensor.wait_ge(cld.h, cld.n)
                nc.tensor.wait_ge(ve.h, ve_w + dc)
                nc.sync.wait_ge(stsl[dc % 2].h, st_b[dc % 2] + 16 * (dc // 2))
                xsl[dc % 2].inc(nc.sync.dma_start(
                    out=xrt[dc % 2][:, :], in_=xfm[dc, :, tt * TT:(tt + 1) * TT]), 16)
                for ic in range(32):
                    ins = nc.tensor.matmul(pA[:, :], woutb[dc % 2][:, ic * 128:(ic + 1) * 128],
                                           ynt[:, ic * TT:(ic + 1) * TT],
                                           start=(ic == 0), stop=(ic == 31))
                pe.inc(ins)
                nc.vector.wait_ge(pe.h, pe.n)
                nc.vector.wait_ge(xsl[dc % 2].h, xs_b[dc % 2] + 16 * (dc // 2 + 1))
                if dc >= 2:
                    nc.vector.wait_ge(stsl[dc % 2].h, st_b[dc % 2] + 16 * (dc // 2))
                ve.inc(nc.vector.tensor_add(xot[dc % 2][:, :], pA[:, :], xrt[dc % 2][:, :]))
                nc.scalar.wait_ge(ve.h, ve.n)
                stsl[dc % 2].inc(nc.scalar.dma_start(
                    out=xo[dc, :, tt * TT:(tt + 1) * TT], in_=xot[dc % 2][:, :]), 16)
                if dc + 2 < DC:
                    wout_load(dc + 2)
        nc.sync.wait_ge(stsl[0].h, stsl[0].n)
        nc.sync.wait_ge(stsl[1].h, stsl[1].n)
        nc.vector.wait_ge(stsl[0].h, stsl[0].n)
        nc.vector.wait_ge(stsl[1].h, stsl[1].n)


def build_mamba(TM=4096):
    nc = bass.Bass("TRN2", target_bir_lowering=False, debug=False)
    aps = declare_drams(nc, mamba_dram_spec(TM))
    es = contextlib.ExitStack()
    with es:
        sm = make_sems(nc, es)
        emit_mamba(nc, es, sm, TM, **aps)
    return nc


def run_mamba(h, xs):
    full = [np.concatenate([xs[2 * b], xs[2 * b + 1]], axis=2) for b in range(4)]
    im = []
    for c in range(NCORES):
        b = c % 4
        im.append({"xfm": full[b], "wins": h["wins"], "wouts": h["wouts"],
                   "convw": h["convw"], "convbc": h["convbc"], "nrmw": h["nrmw"],
                   "dexp": h["dexp"], "abias": h["abias"], "identb": h["identb"],
                   "identf": h["identf"], "onesc": h["onesc"], "onesf": h["onesf"],
                   "maskc": h["maskc"]})
    res = _run("mamba", build_mamba, im)
    out = []
    for c in range(NCORES):
        b, hh = c >> 1, c & 1
        out.append(np.ascontiguousarray(res.results[b]["xo"][:, :, hh * T:(hh + 1) * T]))
    return out


FFN_DRAM = dict(
    w1s=([64, 128, 2048], "F32", "ExternalInput"),
    w3s=([64, 128, 2048], "F32", "ExternalInput"),
    onesc=([128, 128], "BF16", "ExternalInput"),
)
MOE_DRAM = dict(
    rts=([1, 128, 2048], "F32", "ExternalInput"),
    wgs=([NE, HC_E, 128, 2048], "F32", "ExternalInput"),
    wus=([NE, HC_E, 128, 2048], "F32", "ExternalInput"),
    wds=([NE, DC, 128, 1408], "F32", "ExternalInput"),
    sgs=([HC_S, 128, 2048], "F32", "ExternalInput"),
    sus=([HC_S, 128, 2048], "F32", "ExternalInput"),
    sds=([DC, 128, 2816], "F32", "ExternalInput"),
    onesc=([128, 128], "BF16", "ExternalInput"),
    identf=([128, 128], "F32", "ExternalInput"),
    selc=([8, 8 * 128], "BF16", "ExternalInput"),
)
PREP_DRAM = dict(
    w2t=([8192, 2048], "F32", "ExternalInput"),
    h128=([128, 128], "F32", "ExternalInput"),
    h64=([64, 64], "F32", "ExternalInput"),
    w2ps=([16, 128, 8192], "BF16", None),
    t1d=([64, 128, 2048], "BF16", None),
)


def emit_halfsel(nc, es, sm, src_full, dst, hs, flip=False):
    sy, ve, gv = sm["sy"], sm["ve"], sm["gv"]
    t0 = es.enter_context(nc.sbuf_tensor(uq("hs_t0"), [128, TT], F32))
    t1 = es.enter_context(nc.sbuf_tensor(uq("hs_t1"), [128, TT], F32))
    t2 = es.enter_context(nc.sbuf_tensor(uq("hs_t2"), [128, TT], F32))
    hsb = es.enter_context(nc.sbuf_tensor(uq("hs_m"), [128, 2], F32))
    st = sm["stsl"][0]
    sy.inc(nc.sync.dma_start(out=hsb[:, :], in_=hs), 16)
    c0, c1 = (1, 0) if flip else (0, 1)
    for dc in range(DC):
        for tt in range(NTT):
            nc.sync.wait_ge(st.h, st.n)
            sy.inc(nc.sync.dma_start(
                out=t0[:, :], in_=src_full[dc, :, tt * TT:(tt + 1) * TT]), 16)
            sy.inc(nc.sync.dma_start(
                out=t1[:, :], in_=src_full[dc, :, T + tt * TT:T + (tt + 1) * TT]), 16)
            nc.vector.wait_ge(sy.h, sy.n)
            ve.inc(nc.vector.tensor_scalar(t2[:, :], t0[:, :], hsb[:, c0:c0 + 1],
                                           None, ALU.mult))
            nc.vector.wait_ge(ve.h, ve.n)
            ve.inc(nc.vector.scalar_tensor_tensor(
                t2[:, :], t1[:, :], hsb[:, c1:c1 + 1], t2[:, :], ALU.mult, ALU.add))
            nc.sync.wait_ge(ve.h, ve.n)
            st.inc(nc.sync.dma_start(
                out=dst[dc, :, tt * TT:(tt + 1) * TT], in_=t2[:, :]), 16)
    nc.sync.wait_ge(st.h, st.n)
    nc.gpsimd.wait_ge(st.h, st.n)


def build_fused():
    nc = bass.Bass("TRN2", target_bir_lowering=False, debug=False)
    xin = nc.dram_tensor("xin", [DC, 128, T], F32, kind="ExternalInput").ap()
    xfinal = nc.dram_tensor("xfinal", [DC, 128, T], F32, kind="ExternalOutput").ap()
    hs1 = nc.dram_tensor("hs1", [128, 2], F32, kind="ExternalInput").ap()
    hs2 = nc.dram_tensor("hs2", [128, 2], F32, kind="ExternalInput").ap()
    xa = nc.dram_tensor("xa", [DC, 128, T], F32).ap()
    xb = nc.dram_tensor("xb", [DC, 128, T], F32).ap()
    xc = nc.dram_tensor("xc", [DC, 128, T], F32).ap()
    xd = nc.dram_tensor("xd", [DC, 128, T], F32).ap()
    xe = nc.dram_tensor("xe", [DC, 128, T], F32).ap()
    g1out = nc.dram_tensor("g1out", [2 * D, T], F32).ap()
    g2out = nc.dram_tensor("g2out", [2 * D, T], F32).ap()
    mxfm = nc.dram_tensor("mxfm", [DC, 128, 2 * T], F32).ap()
    mxo = nc.dram_tensor("mxo", [DC, 128, 2 * T], F32).ap()
    axrem = nc.dram_tensor("axrem", [DC, 128, T], F32).ap()

    p_aps = declare_drams(nc, PREP_DRAM, "p_")
    f_aps = declare_drams(nc, FFN_DRAM, "f_")
    spec_m = mamba_dram_spec(2 * T)
    del spec_m["xfm"]
    del spec_m["xo"]
    spec_m = {k: (v[0], v[1], v[2]) for k, v in spec_m.items()}
    m_aps = declare_drams(nc, spec_m, "m_")
    e_aps = declare_drams(nc, MOE_DRAM, "e_")
    spec_a = dict(ATTN_DRAM)
    del spec_a["xloc"]
    del spec_a["xrem"]
    del spec_a["xo"]
    a_aps = declare_drams(nc, spec_a, "a_")
    CAST_LIST = [(f_aps, "w1s", FFN_DRAM), (f_aps, "w3s", FFN_DRAM),
                 (m_aps, "wins", spec_m), (m_aps, "wouts", spec_m),
                 (e_aps, "wgs", MOE_DRAM), (e_aps, "wus", MOE_DRAM),
                 (e_aps, "wds", MOE_DRAM), (e_aps, "sgs", MOE_DRAM),
                 (e_aps, "sus", MOE_DRAM), (e_aps, "sds", MOE_DRAM),
                 (a_aps, "wos", spec_a)]
    shadows = []
    for aps_d, nm, spec in CAST_LIST:
        shape = spec[nm][0]
        sh = nc.dram_tensor("bf_" + nm, shape, BF16).ap()
        shadows.append((aps_d, nm, sh))

    es0 = contextlib.ExitStack()
    with es0:
        sm = make_sems(nc, es0)
        cc = S_(nc, es0, "cc")

        def bar():
            nc.all_engine_barrier()

        def gather(src, out):
            cc.inc(nc.gpsimd.collective_compute(
                "AllGather", ALU.bypass,
                replica_groups=[[0, 1], [2, 3], [4, 5], [6, 7]],
                ins=[src.rearrange("c p t -> (c p) t")],
                outs=[out]), 1)
            nc.gpsimd.wait_ge(cc.h, cc.n)
            bar()

        for aps_d, nm, sh in shadows:
            fp = aps_d[nm]
            nflat = 1
            for s in fp.shape:
                nflat *= s
            rows = nflat // 4096
            sm["gp"].inc(nc.gpsimd.dma_start(
                out=sh.rearrange(" ".join(f"a{i}" for i in range(len(sh.shape)))
                                 + " -> (" + " ".join(f"a{i}" for i in range(len(sh.shape))) + ")")
                    .rearrange("(r c) -> r c", c=4096),
                in_=fp.rearrange(" ".join(f"a{i}" for i in range(len(fp.shape)))
                                 + " -> (" + " ".join(f"a{i}" for i in range(len(fp.shape))) + ")")
                    .rearrange("(r c) -> r c", c=4096)), 16)
            aps_d[nm] = sh
        with contextlib.ExitStack() as es:
            emit_prep(nc, es, sm, p_aps["w2t"], p_aps["h128"], p_aps["h64"],
                      p_aps["w2ps"], p_aps["t1d"])
        nc.gpsimd.wait_ge(sm["gp"].h, sm["gp"].n)
        bar()
        with contextlib.ExitStack() as es:
            emit_ffn(nc, es, sm, xin, f_aps["w1s"], f_aps["w3s"], p_aps["w2ps"],
                     f_aps["onesc"], xa)
        bar()
        gather(xa, g1out)
        sm["sy"].inc(nc.sync.dma_start(
            out=mxfm.rearrange("c p (r t) -> (c p) r t", r=2),
            in_=g1out.rearrange("(r f) t -> f r t", r=2)), 16)
        nc.gpsimd.wait_ge(sm["sy"].h, sm["sy"].n)
        bar()
        with contextlib.ExitStack() as es:
            emit_mamba(nc, es, sm, 2 * T, mxfm, **m_aps, xo=mxo)
        bar()
        with contextlib.ExitStack() as es:
            emit_halfsel(nc, es, sm, mxo, xb, hs1)
        bar()
        with contextlib.ExitStack() as es:
            emit_ffn(nc, es, sm, xb, f_aps["w1s"], f_aps["w3s"], p_aps["w2ps"],
                     f_aps["onesc"], xc)
        bar()
        with contextlib.ExitStack() as es:
            emit_moe(nc, es, sm, xc, e_aps["rts"], e_aps["wgs"], e_aps["wus"],
                     e_aps["wds"], e_aps["sgs"], e_aps["sus"], e_aps["sds"],
                     e_aps["onesc"], e_aps["identf"], e_aps["selc"], xd)
        bar()
        gather(xd, g2out)
        sm["sy"].inc(nc.sync.dma_start(
            out=mxfm.rearrange("c p (r t) -> (c p) r t", r=2),
            in_=g2out.rearrange("(r f) t -> f r t", r=2)), 16)
        nc.gpsimd.wait_ge(sm["sy"].h, sm["sy"].n)
        nc.sync.wait_ge(sm["sy"].h, sm["sy"].n)
        bar()
        with contextlib.ExitStack() as es:
            emit_halfsel(nc, es, sm, mxfm, axrem, hs2, flip=True)
        bar()
        with contextlib.ExitStack() as es:
            emit_attn(nc, es, sm, xd, axrem, a_aps["wdqs"], a_aps["wdkvs"],
                      a_aps["wuqr"], a_aps["wukv"], a_aps["wos"], a_aps["gq"],
                      a_aps["gkv"], a_aps["qA"], a_aps["qB"], a_aps["kA"],
                      a_aps["kB"], a_aps["dmaskc"], a_aps["droprc"],
                      a_aps["onesc"], xe, a_aps["qd"], a_aps["vd"])
        bar()
        with contextlib.ExitStack() as es:
            emit_moe(nc, es, sm, xe, e_aps["rts"], e_aps["wgs"], e_aps["wus"],
                     e_aps["wds"], e_aps["sgs"], e_aps["sus"], e_aps["sds"],
                     e_aps["onesc"], e_aps["identf"], e_aps["selc"], xfinal)
        bar()
    return nc


def run_fused(h, inputs):
    xs = shard_x(np.asarray(inputs["L0_attn_hidden_states"], np.float32))
    im = []
    for c in range(NCORES):
        half = c & 1
        hs = np.zeros((128, 2), np.float32)
        hs[:, 0] = 1.0 - half
        hs[:, 1] = half
        d = {"xin": xs[c], "hs1": hs, "hs2": hs,
             "p_w2t": h["w2t"], "p_h128": h["h128"], "p_h64": h["h64"],
             "f_w1s": h["w1s"], "f_w3s": h["w3s"], "f_onesc": h["onesc"],
             "m_wins": h["wins"], "m_wouts": h["wouts"], "m_convw": h["convw"],
             "m_convbc": h["convbc"], "m_nrmw": h["nrmw"], "m_dexp": h["dexp"],
             "m_abias": h["abias"], "m_identb": h["identb"],
             "m_identf": h["identf"], "m_onesc": h["onesc"],
             "m_onesf": h["onesf"], "m_maskc": h["maskc"],
             "e_rts": h["rts"], "e_wgs": h["wgs"], "e_wus": h["wus"],
             "e_wds": h["wds"], "e_sgs": h["sgs"], "e_sus": h["sus"],
             "e_sds": h["sds"], "e_onesc": h["onesc"], "e_identf": h["identf"],
             "e_selc": h["selc"],
             "a_wdqs": h["wdqs"], "a_wdkvs": h["wdkvs"], "a_wuqr": h["wuqr"],
             "a_wukv": h["wukv"], "a_wos": h["wos"], "a_gq": h["gq"],
             "a_gkv": h["gkv"], "a_qA": h[f"qA{half}"], "a_qB": h[f"qB{half}"],
             "a_kA": h[f"kA{half}"], "a_kB": h[f"kB{half}"],
             "a_dmaskc": h["dmaskc"], "a_droprc": h[f"drop{half}"],
             "a_onesc": h["onesc"]}
        im.append(d)
    res = _run("fused", build_fused, im)
    return unshard_x([res.results[c]["xfinal"] for c in range(NCORES)])


def kernel(**inputs):
    h = _prep_host(inputs)
    return run_fused(h, inputs)


def kernel_unfused(**inputs):
    return _kernel_unfused_body(**inputs)


# version 06
# speedup vs baseline: 1.2376x; 1.0236x over previous
import math
import contextlib
import numpy as np
import ml_dtypes
import concourse.bass as bass
import concourse.mybir as mybir
from concourse.bass_utils import run_bass_kernel_spmd

F32 = mybir.dt.float32
BF16 = mybir.dt.bfloat16
AF = mybir.ActivationFunctionType
ALU = mybir.AluOpType
AX = mybir.AxisListType
BF = ml_dtypes.bfloat16

NCORES = 8
D = 2048
DC = D // 128
T = 2048
TT = 512
NTT = T // TT


def tile_w(w):
    K, M = w.shape
    assert K % 128 == 0
    KC = K // 128
    MC = (M + 127) // 128
    if M % 128:
        w = np.concatenate([w, np.zeros((K, MC * 128 - M), w.dtype)], axis=1)
    return np.ascontiguousarray(
        w.reshape(KC, 128, MC, 128).transpose(2, 1, 0, 3).reshape(MC, 128, KC * 128)
    )


def fm_chunks(x):
    t, d = x.shape
    return np.ascontiguousarray(x.T.reshape(d // 128, 128, t))


def fm_to_tokmajor(xc):
    dc, p, t = xc.shape
    return np.ascontiguousarray(xc.reshape(dc * p, t).T)


def hadamard(n):
    h = np.array([[1.0]], np.float32)
    while h.shape[0] < n:
        h = np.block([[h, h], [h, -h]])
    return h



_UNIQ = [0]


def uq(name):
    _UNIQ[0] += 1
    return f"{name}_{_UNIQ[0]}"


class S_:

    def __init__(self, nc, es, name):
        self.h = es.enter_context(nc.semaphore(name))
        self.n = 0

    def inc(self, instr, k=1):
        instr.then_inc(self.h, k)
        self.n += k
        return instr

    def wait(self, eng, n=None):
        eng.wait_ge(self.h, self.n if n is None else n)


def emit_prep(nc, es, sm, w2t, h128, h64, w2ps, t1d):
    h128b = es.enter_context(nc.sbuf_tensor(uq("h128b"), [128, 128], BF16))
    h64b = es.enter_context(nc.sbuf_tensor(uq("h64b"), [64, 64], BF16))
    blk = [es.enter_context(nc.sbuf_tensor(uq(f"blk{i}"), [128, 2048], BF16)) for i in range(2)]
    ev1 = [es.enter_context(nc.sbuf_tensor(uq(f"ev1_{i}"), [128, 2048], BF16)) for i in range(2)]
    rhs2 = [es.enter_context(nc.sbuf_tensor(uq(f"rhs2_{i}"), [64, 512], BF16)) for i in range(2)]
    ev2 = [es.enter_context(nc.sbuf_tensor(uq(f"ev2_{i}"), [64, 512], BF16)) for i in range(2)]
    ps1x = [es.enter_context(nc.psum_tensor(uq(f"ps1_{i}"), [128, 512], F32)) for i in range(2)]
    ps2x = [es.enter_context(nc.psum_tensor(uq(f"ps2_{i}"), [64, 512], F32)) for i in range(2)]
    if True:
        gp, pe, ac, sy = sm["gp"], sm["pe"], sm["ac"], sm["sy"]
        vst, dsl = sm["vst"], sm["dsl"]

        nc.gpsimd.dma_start(out=h128b[:, :], in_=h128[:, :]).then_inc(gp.h, 16)
        gp.n += 16
        nc.gpsimd.dma_start(out=h64b[:, :], in_=h64[:, :]).then_inc(gp.h, 16)
        gp.n += 16
        gp_hs = gp.n

        p1_pe = pe.n
        p1_ac = ac.n
        ld1 = [0] * 64
        st1 = [0] * 64

        def p1_load(i1):
            b2 = i1 % 2
            if i1 >= 2:
                nc.gpsimd.wait_ge(pe.h, p1_pe + 4 * (i1 - 1))
            gp.inc(nc.gpsimd.dma_start(
                out=blk[b2][:, :], in_=w2t[i1 * 128:(i1 + 1) * 128, :]), 16)
            ld1[i1] = gp.n

        p1_load(0)
        p1_load(1)
        for i1 in range(64):
            b = i1 % 2
            nc.tensor.wait_ge(gp.h, ld1[i1])
            for dt in range(4):
                j = 4 * i1 + dt
                if j >= 2:
                    nc.tensor.wait_ge(ac.h, p1_ac + j - 1)
                pe.inc(nc.tensor.matmul(
                    ps1x[j % 2][:, :], h128b[:, :], blk[b][:, dt * 512:(dt + 1) * 512],
                    start=True, stop=True))
                nc.scalar.wait_ge(pe.h, p1_pe + j + 1)
                if dt == 0 and i1 >= 2:
                    ssem = sy if b == 0 else dsl[0]
                    nc.scalar.wait_ge(ssem.h, st1[i1 - 2])
                ac.inc(nc.scalar.copy(ev1[b][:, dt * 512:(dt + 1) * 512],
                                      ps1x[j % 2][:, :]))
            ssem, seng = (sy, nc.sync) if b == 0 else (dsl[0], nc.scalar)
            seng.wait_ge(ac.h, p1_ac + 4 * i1 + 4)
            ssem.inc(seng.dma_start(out=t1d[i1, :, :], in_=ev1[b][:, :]), 16)
            st1[i1] = ssem.n
            if i1 + 2 < 64:
                p1_load(i1 + 2)

        scale = 1.0 / math.sqrt(8192.0)
        nc.gpsimd.wait_ge(sy.h, sy.n)
        nc.gpsimd.wait_ge(dsl[0].h, dsl[0].n)
        nc.vector.wait_ge(sy.h, sy.n)
        nc.vector.wait_ge(dsl[0].h, dsl[0].n)
        p2_pe = pe.n
        p2_ac = ac.n
        ld2 = [0] * 512
        st2 = [0] * 512

        def p2_load(k):
            b2 = k % 2
            j2k, dtk = divmod(k, 4)
            eng, sem = (nc.gpsimd, gp) if b2 == 0 else (nc.sync, vst)
            if k >= 2:
                eng.wait_ge(pe.h, p2_pe + k - 1)
            sem.inc(eng.dma_start(
                out=rhs2[b2][:, :],
                in_=t1d[:, j2k, dtk * 512:(dtk + 1) * 512]), 16)
            ld2[k] = sem.n

        p2_load(0)
        p2_load(1)
        for k in range(512):
            b = k % 2
            j2, dt = divmod(k, 4)
            sem = gp if b == 0 else vst
            nc.tensor.wait_ge(sem.h, ld2[k])
            if k >= 2:
                nc.tensor.wait_ge(ac.h, p2_ac + k - 1)
            pe.inc(nc.tensor.matmul(ps2x[b][:, :], h64b[:, :], rhs2[b][:, :],
                                    start=True, stop=True))
            if k + 2 < 512:
                p2_load(k + 2)
            nc.scalar.wait_ge(pe.h, p2_pe + k + 1)
            if k >= 2:
                ssem = sy if b == 0 else dsl[1]
                nc.scalar.wait_ge(ssem.h, st2[k - 2])
            ac.inc(nc.scalar.activation(ev2[b][:, :], ps2x[b][:, :], AF.Copy,
                                        scale=scale))
            dst = w2ps[dt * 4:(dt + 1) * 4, j2, :].rearrange(
                "dcc (j1 q) -> j1 dcc q", q=128)
            src = ev2[b][:, :].rearrange("j1 (dcc q) -> j1 dcc q", q=128)
            ssem, seng = (sy, nc.sync) if b == 0 else (dsl[1], nc.scalar)
            seng.wait_ge(ac.h, p2_ac + k + 1)
            ssem.inc(seng.dma_start(out=dst, in_=src), 16)
            st2[k] = ssem.n
        nc.sync.wait_ge(sy.h, sy.n)
        nc.sync.wait_ge(dsl[1].h, dsl[1].n)
        nc.vector.wait_ge(sy.h, sy.n)
        nc.vector.wait_ge(dsl[1].h, dsl[1].n)
        nc.gpsimd.wait_ge(dsl[1].h, dsl[1].n)


def make_sems(nc, es):
    sm = {}
    for name in ["sy", "gp", "pe", "ac", "ve", "gv", "st", "vst", "qst", "qld",
                 "cld", "bnc"]:
        sm[name] = S_(nc, es, name)
    for name in ["wsl", "w2sl", "stsl", "dsl", "osl", "vsl", "xsl", "psl", "gsl"]:
        sm[name] = [S_(nc, es, f"{name}{i}") for i in range(2)]
    return sm


def build_prep():
    nc = bass.Bass("TRN2", target_bir_lowering=False, debug=False)
    w2t = nc.dram_tensor("w2t", [8192, 2048], F32, kind="ExternalInput").ap()
    h128 = nc.dram_tensor("h128", [128, 128], F32, kind="ExternalInput").ap()
    h64 = nc.dram_tensor("h64", [64, 64], F32, kind="ExternalInput").ap()
    w2ps = nc.dram_tensor("w2ps", [16, 128, 8192], BF16, kind="ExternalOutput").ap()
    t1d = nc.dram_tensor("t1d", [64, 128, 2048], BF16).ap()
    es = contextlib.ExitStack()
    with es:
        sm = make_sems(nc, es)
        emit_prep(nc, es, sm, w2t, h128, h64, w2ps, t1d)
    return nc


def emit_ffn(nc, es, sm, xfm, w1s, w3s, w2ps, onesc, xo):
    ones = es.enter_context(nc.sbuf_tensor(uq("ones"), [128, 128], BF16))
    xt = es.enter_context(nc.sbuf_tensor(uq("xt"), [128, DC * TT], F32))
    xsq = es.enter_context(nc.sbuf_tensor(uq("xsq"), [128, DC * TT], BF16))
    xln = es.enter_context(nc.sbuf_tensor(uq("xln"), [128, DC * TT], BF16))
    w1b = [es.enter_context(nc.sbuf_tensor(uq(f"w1b{i}"), [128, 2048], BF16)) for i in range(2)]
    w3b = [es.enter_context(nc.sbuf_tensor(uq(f"w3b{i}"), [128, 2048], BF16)) for i in range(2)]
    w2b = [es.enter_context(nc.sbuf_tensor(uq(f"w2b{i}"), [128, 8192], BF16)) for i in range(2)]
    hid = es.enter_context(nc.sbuf_tensor(uq("hid"), [128, 64 * TT], BF16))
    gtmp = es.enter_context(nc.sbuf_tensor(uq("gtmp"), [128, TT], BF16))
    rows = es.enter_context(nc.sbuf_tensor(uq("rows"), [1, TT * 5], F32))
    rowsb = es.enter_context(nc.sbuf_tensor(uq("rowsb"), [1, TT * 2], BF16))
    mb = es.enter_context(nc.sbuf_tensor(uq("mb"), [128, TT], F32))
    rb = es.enter_context(nc.sbuf_tensor(uq("rb"), [128, TT], F32))
    t32 = [es.enter_context(nc.sbuf_tensor(uq(f"t32_{i}"), [128, TT], F32)) for i in range(2)]
    xot = [es.enter_context(nc.sbuf_tensor(uq(f"xot{i}"), [128, TT], F32)) for i in range(2)]

    psG = es.enter_context(nc.psum_tensor(uq("psG"), [128, TT], F32))
    psU = es.enter_context(nc.psum_tensor(uq("psU"), [128, TT], F32))
    psO = es.enter_context(nc.psum_tensor(uq("psO"), [128, TT], F32))
    psR1 = es.enter_context(nc.psum_tensor(uq("psR1"), [1, TT], F32))
    psR2 = es.enter_context(nc.psum_tensor(uq("psR2"), [1, TT], F32))

    if True:
        sy, gp, pe, ac, ve, gv = (sm["sy"], sm["gp"], sm["pe"], sm["ac"],
                                  sm["ve"], sm["gv"])
        wsl, w2sl, stsl = sm["wsl"], sm["w2sl"], sm["stsl"]

        gp.inc(nc.gpsimd.dma_start(out=ones[:, :], in_=onesc[:, :]), 16)

        for tt in range(NTT):
            if tt > 0:
                nc.sync.wait_ge(ve.h, ve.n)
            for dc in range(DC):
                sy.inc(nc.sync.dma_start(
                    out=xt[:, dc * TT:(dc + 1) * TT],
                    in_=xfm[dc, :, tt * TT:(tt + 1) * TT]), 16)
            nc.vector.wait_ge(sy.h, sy.n)
            ve.inc(nc.vector.tensor_copy(xln[:, :], xt[:, :]))
            nc.scalar.wait_ge(sy.h, sy.n)
            ac.inc(nc.scalar.activation(xsq[:, :], xt[:, :], AF.Square))
            nc.tensor.wait_ge(ve.h, ve.n)
            nc.tensor.wait_ge(ac.h, ac.n)
            nc.tensor.wait_ge(gp.h, 16)
            for dc in range(DC):
                nc.tensor.matmul(psR1[:, :], ones[:, 0:1],
                                 xln[:, dc * TT:(dc + 1) * TT],
                                 start=(dc == 0), stop=(dc == DC - 1))
            for dc in range(DC):
                ins = nc.tensor.matmul(psR2[:, :], ones[:, 0:1],
                                       xsq[:, dc * TT:(dc + 1) * TT],
                                       start=(dc == 0), stop=(dc == DC - 1))
            pe.inc(ins)
            nc.vector.wait_ge(pe.h, pe.n)
            r_mean = rows[:, 0:TT]
            r_msq = rows[:, TT:2 * TT]
            r_var = rows[:, 2 * TT:3 * TT]
            r_std = rows[:, 3 * TT:4 * TT]
            r_rstd = rows[:, 4 * TT:5 * TT]
            r_mbr = rows[:, 3 * TT:4 * TT]

            def vch(ins):
                ve.inc(ins)
                nc.vector.wait_ge(ve.h, ve.n)
            vch(nc.vector.tensor_scalar_mul(r_mean, psR1[:, :], 1.0 / D))
            vch(nc.vector.tensor_scalar_mul(r_msq, psR2[:, :], 1.0 / D))
            vch(nc.vector.tensor_mul(r_var, r_mean, r_mean))
            vch(nc.vector.tensor_sub(r_var, r_msq, r_var))
            ve.inc(nc.vector.tensor_scalar_add(r_var, r_var, 1e-5))
            nc.scalar.wait_ge(ve.h, ve.n)
            ac.inc(nc.scalar.activation(r_std, r_var, AF.Sqrt))
            nc.vector.wait_ge(ac.h, ac.n)
            vch(nc.vector.reciprocal(r_rstd, r_std))
            vch(nc.vector.tensor_copy(rowsb[:, 0:TT], r_rstd))
            vch(nc.vector.tensor_mul(r_mbr, r_mean, r_rstd))
            ve.inc(nc.vector.tensor_copy(rowsb[:, TT:2 * TT], r_mbr))
            nc.tensor.wait_ge(ve.h, ve.n)
            pe_snap = pe.n
            pe.inc(nc.tensor.matmul(psG[:, :], ones[0:1, :], rowsb[:, 0:TT],
                                    start=True, stop=True))
            pe.inc(nc.tensor.matmul(psU[:, :], ones[0:1, :], rowsb[:, TT:2 * TT],
                                    start=True, stop=True))
            nc.scalar.wait_ge(pe.h, pe_snap + 1)
            ac.inc(nc.scalar.copy(rb[:, :], psG[:, :]))
            nc.scalar.wait_ge(pe.h, pe_snap + 2)
            ac.inc(nc.scalar.copy(mb[:, :], psU[:, :]))
            nc.vector.wait_ge(ac.h, ac.n)
            nc.gpsimd.wait_ge(ac.h, ac.n)
            gv_base = gv.n
            ve_ap_base = ve.n
            for dc in range(DC):
                if dc >= 2:
                    nc.vector.wait_ge(gv.h, gv_base + dc - 1)
                ve.inc(nc.vector.tensor_mul(t32[dc % 2][:, :],
                                            xt[:, dc * TT:(dc + 1) * TT], rb[:, :]))
                nc.gpsimd.wait_ge(ve.h, ve_ap_base + dc + 1)
                gv.inc(nc.gpsimd.tensor_sub(xln[:, dc * TT:(dc + 1) * TT],
                                            t32[dc % 2][:, :], mb[:, :]))
            gv_xln = gv.n

            pe_gu_base = pe.n
            wsl_base = [wsl[0].n, wsl[1].n]
            for hc in range(64):
                if hc >= 2:
                    nc.gpsimd.wait_ge(pe.h, pe_gu_base + 2 * hc - 2)
                wsl[hc % 2].inc(nc.gpsimd.dma_start(out=w1b[hc % 2][:, :], in_=w1s[hc, :, :]), 16)
                wsl[hc % 2].inc(nc.gpsimd.dma_start(out=w3b[hc % 2][:, :], in_=w3s[hc, :, :]), 16)
            ac_base = ac.n
            ve_base = ve.n
            for hc in range(64):
                nc.tensor.wait_ge(wsl[hc % 2].h, wsl_base[hc % 2] + 32 * (hc // 2 + 1))
                if hc == 0:
                    nc.tensor.wait_ge(gv.h, gv_xln)
                else:
                    nc.tensor.wait_ge(ac.h, ac_base + hc)
                    nc.tensor.wait_ge(ve.h, ve_base + hc)
                for dc in range(DC):
                    ins = nc.tensor.matmul(psG[:, :], w1b[hc % 2][:, dc * 128:(dc + 1) * 128],
                                           xln[:, dc * TT:(dc + 1) * TT],
                                           start=(dc == 0), stop=(dc == DC - 1))
                pe.inc(ins)
                for dc in range(DC):
                    ins = nc.tensor.matmul(psU[:, :], w3b[hc % 2][:, dc * 128:(dc + 1) * 128],
                                           xln[:, dc * TT:(dc + 1) * TT],
                                           start=(dc == 0), stop=(dc == DC - 1))
                pe.inc(ins)
                nc.scalar.wait_ge(pe.h, pe_gu_base + 2 * hc + 1)
                nc.scalar.wait_ge(ve.h, ve_base + hc)
                ac.inc(nc.scalar.activation(gtmp[:, :], psG[:, :], AF.Silu))
                nc.vector.wait_ge(pe.h, pe_gu_base + 2 * hc + 2)
                nc.vector.wait_ge(ac.h, ac_base + hc + 1)
                ve.inc(nc.vector.tensor_tensor(hid[:, hc * TT:(hc + 1) * TT],
                                               psU[:, :], gtmp[:, :], ALU.mult))
            ve_hid = ve.n

            pe_w2_base = pe.n
            w2sl_base = [w2sl[0].n, w2sl[1].n]
            for dc in range(DC):
                if dc >= 2:
                    nc.sync.wait_ge(pe.h, pe_w2_base + dc - 1)
                w2sl[dc % 2].inc(nc.sync.dma_start(out=w2b[dc % 2][:, :], in_=w2ps[dc, :, :]), 16)
            ve_w2_base = ve.n
            stsl_base = [stsl[0].n, stsl[1].n]
            for dc in range(DC):
                nc.tensor.wait_ge(w2sl[dc % 2].h, w2sl_base[dc % 2] + 16 * (dc // 2 + 1))
                if dc == 0:
                    nc.tensor.wait_ge(ve.h, ve_hid)
                else:
                    nc.tensor.wait_ge(ve.h, ve_w2_base + dc)
                for hc in range(64):
                    ins = nc.tensor.matmul(psO[:, :], w2b[dc % 2][:, hc * 128:(hc + 1) * 128],
                                           hid[:, hc * TT:(hc + 1) * TT],
                                           start=(hc == 0), stop=(hc == 63))
                pe.inc(ins)
                nc.vector.wait_ge(pe.h, pe_w2_base + dc + 1)
                if dc >= 2:
                    nc.vector.wait_ge(stsl[dc % 2].h, stsl_base[dc % 2] + 16 * (dc // 2))
                else:
                    nc.vector.wait_ge(stsl[dc % 2].h, stsl_base[dc % 2])
                ve.inc(nc.vector.scalar_tensor_tensor(
                    xot[dc % 2][:, :], psO[:, :], 0.5, xt[:, dc * TT:(dc + 1) * TT],
                    ALU.mult, ALU.add))
                nc.scalar.wait_ge(ve.h, ve.n)
                stsl[dc % 2].inc(nc.scalar.dma_start(
                    out=xo[dc, :, tt * TT:(tt + 1) * TT], in_=xot[dc % 2][:, :]), 16)

        nc.sync.wait_ge(stsl[0].h, stsl[0].n)
        nc.sync.wait_ge(stsl[1].h, stsl[1].n)
        nc.vector.wait_ge(stsl[0].h, stsl[0].n)
        nc.vector.wait_ge(stsl[1].h, stsl[1].n)


def build_ffn():
    nc = bass.Bass("TRN2", target_bir_lowering=False, debug=False)
    xfm = nc.dram_tensor("xfm", [DC, 128, T], F32, kind="ExternalInput").ap()
    w1s = nc.dram_tensor("w1s", [64, 128, 2048], F32, kind="ExternalInput").ap()
    w3s = nc.dram_tensor("w3s", [64, 128, 2048], F32, kind="ExternalInput").ap()
    w2ps = nc.dram_tensor("w2ps", [16, 128, 8192], BF16, kind="ExternalInput").ap()
    onesc = nc.dram_tensor("onesc", [128, 128], BF16, kind="ExternalInput").ap()
    xo = nc.dram_tensor("xo", [DC, 128, T], F32, kind="ExternalOutput").ap()
    es = contextlib.ExitStack()
    with es:
        sm = make_sems(nc, es)
        emit_ffn(nc, es, sm, xfm, w1s, w3s, w2ps, onesc, xo)
    return nc


_cache = {}


def _prep_host(inputs):
    h = {}
    h["w1s"] = tile_w(np.ascontiguousarray(inputs["L0_ffn_W1"].T))
    h["w3s"] = tile_w(np.ascontiguousarray(inputs["L0_ffn_W3"].T))
    h["w2t"] = np.ascontiguousarray(inputs["L0_ffn_W2"].T)
    h["h128"] = hadamard(128)
    h["h64"] = hadamard(64)
    h["onesc"] = np.ones((128, 128), BF)
    h["identf"] = np.eye(128, dtype=np.float32)
    sel = np.zeros((8, 8 * 128), np.float32)
    for e in range(8):
        sel[e, e * 128:(e + 1) * 128] = 1.0
    h["selc"] = sel.astype(BF)
    h["rts"] = tile_w(np.ascontiguousarray(inputs["L1_ffn_router_w"].T))
    h["wgs"] = np.stack([tile_w(inputs["L1_ffn_wg"][e]) for e in range(8)])
    h["wus"] = np.stack([tile_w(inputs["L1_ffn_wu"][e]) for e in range(8)])
    h["wds"] = np.stack([tile_w(inputs["L1_ffn_wd"][e]) for e in range(8)])
    h["sgs"] = tile_w(inputs["L1_ffn_sg"])
    h["sus"] = tile_w(inputs["L1_ffn_su"])
    h["sds"] = tile_w(inputs["L1_ffn_sd"])
    h["wdqs"] = tile_w(inputs["L1_attn_w_dq"])
    h["wdkvs"] = tile_w(inputs["L1_attn_w_dkv"])
    wuq = inputs["L1_attn_w_uq"]
    wuqr = np.empty((128, 2048), np.float32)
    for hh in range(16):
        wuqr[:, hh * 128:hh * 128 + 96] = wuq[:, hh * 96:(hh + 1) * 96]
        wuqr[:, hh * 128 + 96:(hh + 1) * 128] = wuq[:, 1536 + hh * 32:1536 + (hh + 1) * 32]
    h["wuqr"] = wuqr
    h["wukv"] = np.ascontiguousarray(inputs["L1_attn_w_ukv"])
    h["wos"] = tile_w(inputs["L1_attn_w_o"])
    h["gq"] = np.ascontiguousarray(inputs["L1_attn_gamma_q"].reshape(128, 1))
    h["gkv"] = np.ascontiguousarray(inputs["L1_attn_gamma_kv"].reshape(128, 1))

    def rope_tables(nd, pos):
        nf = nd // 2
        inv = 10000.0 ** (-np.arange(0, nd, 2, dtype=np.float64) / nd)
        ang = pos[None, :] * inv[:, None]
        A = np.empty((nd, len(pos)), np.float32)
        B = np.empty((nd, len(pos)), np.float32)
        A[0::2] = np.cos(ang); A[1::2] = np.cos(ang)
        B[0::2] = -np.sin(ang); B[1::2] = np.sin(ang)
        return A.astype(BF), B.astype(BF)

    for half in range(2):
        loc = np.arange(T, dtype=np.float64) + half * T
        rem = np.arange(T, dtype=np.float64) + (1 - half) * T
        qa, qb = rope_tables(32, loc)
        qA = np.zeros((128, T), np.float32); qB = np.zeros((128, T), np.float32)
        qA = qA.astype(BF); qB = qB.astype(BF)
        qA[:96] = BF(1.0); qA[96:] = qa; qB[96:] = qb
        h[f"qA{half}"], h[f"qB{half}"] = np.ascontiguousarray(qA), np.ascontiguousarray(qB)
        ka, kb = rope_tables(64, np.concatenate([loc, rem]))
        h[f"kA{half}"], h[f"kB{half}"] = np.ascontiguousarray(ka), np.ascontiguousarray(kb)
        drop = np.zeros((1, 32), np.float32)
        if half == 0:
            drop[0, 16:] = 1.0
        h[f"drop{half}"] = drop
    dm = np.zeros((4, 128, 512), np.float32)
    for rr in range(4):
        for k in range(128):
            dm[rr, k, rr * 128 + k:] = 1.0
    h["dmaskc"] = dm.astype(BF)
    win = np.zeros((8576, 2048), np.float32)
    win[:8512] = inputs["L0_attn_W_in"]
    h["wins"] = tile_w(np.ascontiguousarray(win.T))
    h["wouts"] = tile_w(np.ascontiguousarray(inputs["L0_attn_W_out"].T))
    cw = inputs["L0_attn_conv_weight"]
    h["convw"] = np.ascontiguousarray(cw.reshape(34, 128, 4).transpose(1, 0, 2).reshape(128, 34 * 4))
    h["convbc"] = np.ascontiguousarray(inputs["L0_attn_conv_bias"].reshape(34, 128).T)
    h["nrmw"] = np.ascontiguousarray(inputs["L0_attn_norm_weight"].reshape(32, 128).T)
    dex = np.repeat(inputs["L0_attn_D"], 64)
    h["dexp"] = np.ascontiguousarray(dex.reshape(32, 128).T)
    h["abias"] = np.ascontiguousarray(
        np.stack([inputs["L0_attn_dt_bias"], inputs["L0_attn_A_log"]], axis=1))
    h["identb"] = np.eye(128, dtype=np.float32).astype(BF)
    h["onesf"] = np.ones((128, 128), np.float32)
    mk = np.zeros((128, 128), np.float32)
    for u in range(128):
        mk[u, u:] = 1.0
    h["maskc"] = mk.astype(BF)
    return h


def _run(nc_key, build_fn, in_maps):
    if nc_key not in _cache:
        _cache[nc_key] = build_fn()
    nc = _cache[nc_key]
    return run_bass_kernel_spmd(nc, in_maps, core_ids=list(range(NCORES)))


def shard_x(x):
    out = []
    for c in range(NCORES):
        b, hh = c >> 1, c & 1
        out.append(fm_chunks(np.ascontiguousarray(x[b, hh * T:(hh + 1) * T, :])))
    return out


def unshard_x(shards):
    x = np.empty((4, 4096, 2048), np.float32)
    for c in range(NCORES):
        b, hh = c >> 1, c & 1
        x[b, hh * T:(hh + 1) * T, :] = fm_to_tokmajor(shards[c])
    return x


def run_prep(h):
    im = [{"w2t": h["w2t"], "h128": h["h128"], "h64": h["h64"]}] * NCORES
    res = _run("prep", build_prep, im)
    return res.results[0]["w2ps"]


def run_ffn(h, xs, w2ps):
    im = [{"xfm": xs[c], "w1s": h["w1s"], "w3s": h["w3s"], "w2ps": w2ps,
           "onesc": h["onesc"]} for c in range(NCORES)]
    res = _run("ffn", build_ffn, im)
    return [res.results[c]["xo"] for c in range(NCORES)]


def _kernel_unfused_body(**inputs):
    h = _prep_host(inputs)
    xs = shard_x(np.asarray(inputs["L0_attn_hidden_states"], np.float32))
    w2ps = run_prep(h)
    xs = run_ffn(h, xs, w2ps)
    xs = run_mamba(h, xs)
    xs = run_ffn(h, xs, w2ps)
    xs = run_moe(h, xs)
    xs = run_attn(h, xs)
    xs = run_moe(h, xs)
    return unshard_x(xs)


class LNEmitter:

    def __init__(self, nc, es, psA, psB, psr1=None, psr2=None, xsq_buf=None,
                 rows_buf=None, rowsb_buf=None, mb_buf=None, rb_buf=None,
                 t32_bufs=None):
        self.nc = nc
        self.psA, self.psB = psA, psB
        self.ext = rows_buf is not None
        self.xt = es.enter_context(nc.sbuf_tensor(uq("ln_xt"), [128, DC * TT], F32))
        if xsq_buf is None:
            self.xsq = es.enter_context(nc.sbuf_tensor(uq("ln_xsq"), [128, DC * TT], BF16))
        else:
            self.xsq = xsq_buf
        if self.ext:
            self.rows, self.rowsb = rows_buf, rowsb_buf
            self.mb, self.rb = mb_buf, rb_buf
            self.t32 = t32_bufs
        if not self.ext:
            self.rows = es.enter_context(nc.sbuf_tensor(uq("ln_rows"), [1, TT * 5], F32))
            self.rowsb = es.enter_context(nc.sbuf_tensor(uq("ln_rowsb"), [1, TT * 2], BF16))
            self.mb = es.enter_context(nc.sbuf_tensor(uq("ln_mb"), [128, TT], F32))
            self.rb = es.enter_context(nc.sbuf_tensor(uq("ln_rb"), [128, TT], F32))
            self.t32 = [es.enter_context(nc.sbuf_tensor(uq(f"ln_t32_{i}"), [128, TT], F32))
                        for i in range(2)]
        if psr1 is None:
            self.psR1 = es.enter_context(nc.psum_tensor(uq("ln_psR1"), [1, TT], F32))[:, :]
            self.psR2 = es.enter_context(nc.psum_tensor(uq("ln_psR2"), [1, TT], F32))[:, :]
        else:
            self.psR1, self.psR2 = psr1, psr2

    def emit(self, sm, xfm, tt, dest_chunk, ones):
        nc = self.nc
        sy, pe, ac, ve, gv = sm["sy"], sm["pe"], sm["ac"], sm["ve"], sm["gv"]
        for dc in range(DC):
            src = (xfm(dc, slice(tt * TT, (tt + 1) * TT)) if callable(xfm)
                   else xfm[dc, :, tt * TT:(tt + 1) * TT])
            sy.inc(nc.sync.dma_start(
                out=self.xt[:, dc * TT:(dc + 1) * TT], in_=src), 16)
        nc.vector.wait_ge(sy.h, sy.n)
        for dc in range(DC):
            ins = nc.vector.tensor_copy(dest_chunk(dc), self.xt[:, dc * TT:(dc + 1) * TT])
        ve.inc(ins)
        nc.scalar.wait_ge(sy.h, sy.n)
        ac.inc(nc.scalar.activation(self.xsq[:, :], self.xt[:, :], AF.Square))
        nc.tensor.wait_ge(ve.h, ve.n)
        nc.tensor.wait_ge(ac.h, ac.n)
        for dc in range(DC):
            nc.tensor.matmul(self.psR1, ones[:, 0:1], dest_chunk(dc),
                             start=(dc == 0), stop=(dc == DC - 1))
        for dc in range(DC):
            ins = nc.tensor.matmul(self.psR2, ones[:, 0:1],
                                   self.xsq[:, dc * TT:(dc + 1) * TT],
                                   start=(dc == 0), stop=(dc == DC - 1))
        pe.inc(ins)
        nc.vector.wait_ge(pe.h, pe.n)
        rows, rowsb = self.rows, self.rowsb
        r_mean, r_msq = rows[:, 0:TT], rows[:, TT:2 * TT]
        r_var, r_std, r_rstd = rows[:, 2 * TT:3 * TT], rows[:, 3 * TT:4 * TT], rows[:, 4 * TT:5 * TT]

        def vch(ins):
            ve.inc(ins)
            nc.vector.wait_ge(ve.h, ve.n)
        vch(nc.vector.tensor_scalar_mul(r_mean, self.psR1, 1.0 / D))
        vch(nc.vector.tensor_scalar_mul(r_msq, self.psR2, 1.0 / D))
        vch(nc.vector.tensor_mul(r_var, r_mean, r_mean))
        vch(nc.vector.tensor_sub(r_var, r_msq, r_var))
        ve.inc(nc.vector.tensor_scalar_add(r_var, r_var, 1e-5))
        nc.scalar.wait_ge(ve.h, ve.n)
        ac.inc(nc.scalar.activation(r_std, r_var, AF.Sqrt))
        nc.vector.wait_ge(ac.h, ac.n)
        vch(nc.vector.reciprocal(r_rstd, r_std))
        vch(nc.vector.tensor_copy(rowsb[:, 0:TT], r_rstd))
        vch(nc.vector.tensor_mul(r_std, r_mean, r_rstd))
        ve.inc(nc.vector.tensor_copy(rowsb[:, TT:2 * TT], r_std))
        nc.tensor.wait_ge(ve.h, ve.n)
        pe_snap = pe.n
        pe.inc(nc.tensor.matmul(self.psA[:, 0:TT], ones[0:1, :], rowsb[:, 0:TT],
                                start=True, stop=True))
        pe.inc(nc.tensor.matmul(self.psB[:, 0:TT], ones[0:1, :], rowsb[:, TT:2 * TT],
                                start=True, stop=True))
        nc.scalar.wait_ge(pe.h, pe_snap + 1)
        ac.inc(nc.scalar.copy(self.rb[:, :], self.psA[:, 0:TT]))
        nc.scalar.wait_ge(pe.h, pe_snap + 2)
        ac.inc(nc.scalar.copy(self.mb[:, :], self.psB[:, 0:TT]))
        nc.vector.wait_ge(ac.h, ac.n)
        nc.gpsimd.wait_ge(ac.h, ac.n)
        gv_base = gv.n
        ve_ap = ve.n
        for dc in range(DC):
            if dc >= 2:
                nc.vector.wait_ge(gv.h, gv_base + dc - 1)
            ve.inc(nc.vector.tensor_mul(self.t32[dc % 2][:, :],
                                        self.xt[:, dc * TT:(dc + 1) * TT], self.rb[:, :]))
            nc.gpsimd.wait_ge(ve.h, ve_ap + dc + 1)
            gv.inc(nc.gpsimd.tensor_sub(dest_chunk(dc), self.t32[dc % 2][:, :],
                                        self.mb[:, :]))


NE = 8
HC_E = 11
HC_S = 22


def emit_moe(nc, es, sm, xfm, rts, wgs, wus, wds, sgs, sus, sds, onesc, identf,
             selc, xo):
    ones = es.enter_context(nc.sbuf_tensor(uq("ones"), [128, 128], BF16))
    idf = es.enter_context(nc.sbuf_tensor(uq("idf"), [128, 128], F32))
    sel = es.enter_context(nc.sbuf_tensor(uq("sel"), [8, 8 * 128], BF16))
    rtsb = es.enter_context(nc.sbuf_tensor(uq("rtsb"), [128, 2048], BF16))
    xf = es.enter_context(nc.sbuf_tensor(uq("xf"), [128, DC * T], BF16))
    hid = es.enter_context(nc.sbuf_tensor(uq("hid"), [128, HC_E * T], BF16))
    wgb = [es.enter_context(nc.sbuf_tensor(uq(f"wgb{i}"), [128, 2048], BF16)) for i in range(2)]
    wub = [es.enter_context(nc.sbuf_tensor(uq(f"wub{i}"), [128, 2048], BF16)) for i in range(2)]
    wdb = [es.enter_context(nc.sbuf_tensor(uq(f"wdb{i}"), [128, 1408], BF16)) for i in range(2)]
    sgt = es.enter_context(nc.sbuf_tensor(uq("sgt"), [128, 1024], BF16))
    ut = es.enter_context(nc.sbuf_tensor(uq("ut"), [128, 1024], BF16))
    geb = es.enter_context(nc.sbuf_tensor(uq("geb"), [128, T], BF16))
    gw = es.enter_context(nc.sbuf_tensor(uq("gw"), [128, 16 * 8], F32))
    grow = es.enter_context(nc.sbuf_tensor(uq("grow"), [8, T], BF16))
    gsc = es.enter_context(nc.sbuf_tensor(uq("gsc"), [128, 64], F32))
    ot = [es.enter_context(nc.sbuf_tensor(uq(f"ot{i}"), [128, TT], F32)) for i in range(2)]

    psG = es.enter_context(nc.psum_tensor(uq("psG"), [128, 1024], F32))
    psU = es.enter_context(nc.psum_tensor(uq("psU"), [128, 1024], F32))
    psO = es.enter_context(nc.psum_tensor(uq("psO"), [128, TT], F32))
    psR = es.enter_context(nc.psum_tensor(uq("psR"), [128, 8], F32))
    psT = es.enter_context(nc.psum_tensor(uq("psT"), [8, 128], F32))
    psB = es.enter_context(nc.psum_tensor(uq("psB"), [128, TT], F32))

    if True:
        sy, gp, pe, ac, ve, gv = (sm["sy"], sm["gp"], sm["pe"], sm["ac"],
                                  sm["ve"], sm["gv"])
        wsl, dsl, osl = sm["wsl"], sm["dsl"], sm["osl"]
        ln = LNEmitter(nc, es, psG, psU, psr1=psO[0:1, 0:TT], psr2=psB[0:1, 0:TT], xsq_buf=hid[:, 0:DC * TT])
        sm = {"sy": sy, "pe": pe, "ac": ac, "ve": ve, "gv": gv}

        gp.inc(nc.gpsimd.dma_start(out=ones[:, :], in_=onesc[:, :]), 16)
        gp.inc(nc.gpsimd.dma_start(out=idf[:, :], in_=identf[:, :]), 16)
        gp.inc(nc.gpsimd.dma_start(out=sel[:, :], in_=selc[:, :]), 16)
        gp.inc(nc.gpsimd.dma_start(out=rtsb[:, :], in_=rts[0, :, :]), 16)
        for dc in range(DC):
            sy.inc(nc.sync.dma_start(out=xo[dc, :, :], in_=xfm[dc, :, :]), 16)
        nc.tensor.wait_ge(gp.h, gp.n)

        for tt in range(NTT):
            if tt > 0:
                nc.sync.wait_ge(gv.h, gv.n)
            ln.emit(sm, xfm, tt,
                    lambda dc, _t=tt: xf[:, dc * T + _t * TT: dc * T + (_t + 1) * TT],
                    ones)

        nc.tensor.wait_ge(gv.h, gv.n)

        def vch(ins):
            ve.inc(ins)
            nc.vector.wait_ge(ve.h, ve.n)
        for ts in range(16):
            nc.tensor.wait_ge(ve.h, ve.n)
            for dc in range(DC):
                ins = nc.tensor.matmul(psR[:, :],
                                       xf[:, dc * T + ts * 128: dc * T + (ts + 1) * 128],
                                       rtsb[:, dc * 128: dc * 128 + 8],
                                       start=(dc == 0), stop=(dc == DC - 1))
            pe.inc(ins)
            nc.vector.wait_ge(pe.h, pe.n)
            g0 = gsc[:, 0:8]
            nm = gsc[:, 8:9]
            ex = gsc[:, 9:17]
            s1 = gsc[:, 17:18]
            rs = gsc[:, 18:19]
            gg = gsc[:, 19:27]
            m1 = gsc[:, 27:28]
            vch(nc.vector.tensor_copy(g0, psR[:, :]))
            vch(nc.vector.reduce_max(nm, g0, axis=AX.X, negate=True))
            nc.scalar.wait_ge(ve.h, ve.n)
            ac.inc(nc.scalar.activation(ex, g0, AF.Exp, bias=nm, accum_out=s1))
            nc.vector.wait_ge(ac.h, ac.n)
            vch(nc.vector.reciprocal(rs, s1))
            vch(nc.vector.tensor_scalar(gg, ex, rs, None, ALU.mult))
            vch(nc.vector.reduce_max(m1, gg, axis=AX.X))
            eq = gsc[:, 28:36]
            vch(nc.vector.tensor_scalar(eq, gg, m1, None, ALU.is_ge))
            tt2 = gsc[:, 36:44]
            vch(nc.vector.scalar_tensor_tensor(tt2, eq, -1e30, gg, ALU.mult, ALU.add))
            m2 = gsc[:, 44:45]
            vch(nc.vector.reduce_max(m2, tt2, axis=AX.X))
            keep = gsc[:, 45:53]
            vch(nc.vector.tensor_scalar(keep, gg, m2, None, ALU.is_ge))
            den = gsc[:, 53:54]
            vch(nc.vector.tensor_add(den, m1, m2))
            vch(nc.vector.tensor_scalar(den, den, 1e-20, None, ALU.add))
            rd = gsc[:, 54:55]
            vch(nc.vector.reciprocal(rd, den))
            vch(nc.vector.scalar_tensor_tensor(gw[:, ts * 8:(ts + 1) * 8],
                                               gg, rd, keep, ALU.mult, ALU.mult))
        nc.tensor.wait_ge(ve.h, ve.n)
        for ts in range(16):
            nc.tensor.wait_ge(ac.h, ac.n)
            pe.inc(nc.tensor.transpose(psT[:, :], gw[:, ts * 8:(ts + 1) * 8], idf[:, :]))
            nc.scalar.wait_ge(pe.h, pe.n)
            ac.inc(nc.scalar.copy(grow[:, ts * 128:(ts + 1) * 128], psT[:, 0:128]))

        nc.gpsimd.wait_ge(sy.h, sy.n)
        for e in range(NE + 2):
            shared = e >= NE
            hh = e - NE
            if not shared:
                nc.tensor.wait_ge(ac.h, ac.n)
                nc.tensor.wait_ge(gv.h, gv.n)
                for q in range(4):
                    nc.tensor.wait_ge(ac.h, ac.n)
                    pe.inc(nc.tensor.matmul(psB[:, :], sel[:, e * 128:(e + 1) * 128],
                                            grow[:, q * TT:(q + 1) * TT],
                                            start=True, stop=True))
                    nc.scalar.wait_ge(pe.h, pe.n)
                    ac.inc(nc.scalar.copy(geb[:, q * TT:(q + 1) * TT], psB[:, :]))
            pe_gu = pe.n
            wsl_b = [wsl[0].n, wsl[1].n]

            def gu_load(hc):
                src_g = sgs[hh * HC_E + hc, :, :] if shared else wgs[e, hc, :, :]
                src_u = sus[hh * HC_E + hc, :, :] if shared else wus[e, hc, :, :]
                if hc >= 2:
                    nc.gpsimd.wait_ge(pe.h, pe_gu + 4 * hc - 4)
                wsl[hc % 2].inc(nc.gpsimd.dma_start(out=wgb[hc % 2][:, :], in_=src_g), 16)
                wsl[hc % 2].inc(nc.gpsimd.dma_start(out=wub[hc % 2][:, :], in_=src_u), 16)
            gu_load(0)
            gu_load(1)
            ac_gu = ac.n
            ve_gu = ve.n
            gv_gu = gv.n
            for hc in range(HC_E):
                nc.tensor.wait_ge(wsl[hc % 2].h, wsl_b[hc % 2] + 32 * (hc // 2 + 1))
                for th in range(2):
                    k = 2 * hc + th
                    if k >= 1:
                        nc.tensor.wait_ge(ac.h, ac_gu + k)
                        nc.tensor.wait_ge(ve.h, ve_gu + k)
                    for dc in range(DC):
                        for tq in range(2):
                            ins = nc.tensor.matmul(
                                psG[:, tq * TT:(tq + 1) * TT],
                                wgb[hc % 2][:, dc * 128:(dc + 1) * 128],
                                xf[:, dc * T + th * 1024 + tq * TT: dc * T + th * 1024 + (tq + 1) * TT],
                                start=(dc == 0), stop=(dc == DC - 1))
                    pe.inc(ins)
                    for dc in range(DC):
                        for tq in range(2):
                            ins = nc.tensor.matmul(
                                psU[:, tq * TT:(tq + 1) * TT],
                                wub[hc % 2][:, dc * 128:(dc + 1) * 128],
                                xf[:, dc * T + th * 1024 + tq * TT: dc * T + th * 1024 + (tq + 1) * TT],
                                start=(dc == 0), stop=(dc == DC - 1))
                    pe.inc(ins)
                    nc.scalar.wait_ge(pe.h, pe_gu + 2 * k + 1)
                    if k >= 1:
                        nc.scalar.wait_ge(gv.h, gv_gu + k)
                    ac.inc(nc.scalar.activation(sgt[:, :], psG[:, :], AF.Silu))
                    nc.vector.wait_ge(pe.h, pe_gu + 2 * k + 2)
                    if k >= 1:
                        nc.vector.wait_ge(gv.h, gv_gu + k)
                    if shared:
                        ve.inc(nc.vector.tensor_copy(ut[:, :], psU[:, :]))
                    else:
                        ve.inc(nc.vector.tensor_tensor(
                            ut[:, :], psU[:, :],
                            geb[:, th * 1024:(th + 1) * 1024], ALU.mult))
                    nc.gpsimd.wait_ge(ac.h, ac_gu + k + 1)
                    nc.gpsimd.wait_ge(ve.h, ve_gu + k + 1)
                    gv.inc(nc.gpsimd.tensor_mul(
                        hid[:, hc * T + th * 1024: hc * T + (th + 1) * 1024],
                        sgt[:, :], ut[:, :]))
                if hc + 2 < HC_E:
                    gu_load(hc + 2)
            gv_hid = gv.n
            dsl_b = [dsl[0].n, dsl[1].n]
            pe_wd = pe.n
            ac_wd = ac.n
            osl_b = [osl[0].n, osl[1].n]

            def wd_load(dc):
                src_d = sds[dc, :, hh * 1408:(hh + 1) * 1408] if shared else wds[e, dc, :, :]
                if dc >= 2:
                    nc.gpsimd.wait_ge(pe.h, pe_wd + 4 * (dc - 1))
                dsl[dc % 2].inc(nc.gpsimd.dma_start(out=wdb[dc % 2][:, :], in_=src_d), 16)
            wd_load(0)
            wd_load(1)
            for dc in range(DC):
                nc.tensor.wait_ge(dsl[dc % 2].h, dsl_b[dc % 2] + 16 * (dc // 2 + 1))
                if dc == 0:
                    nc.tensor.wait_ge(gv.h, gv_hid)
                for tq in range(4):
                    k = 4 * dc + tq
                    if k >= 1:
                        nc.tensor.wait_ge(ac.h, ac_wd + k)
                    for hc in range(HC_E):
                        ins = nc.tensor.matmul(
                            psO[:, :], wdb[dc % 2][:, hc * 128:(hc + 1) * 128],
                            hid[:, hc * T + tq * TT: hc * T + (tq + 1) * TT],
                            start=(hc == 0), stop=(hc == HC_E - 1))
                    pe.inc(ins)
                    nc.scalar.wait_ge(pe.h, pe_wd + k + 1)
                    if k >= 2:
                        nc.scalar.wait_ge(osl[k % 2].h, osl_b[k % 2] + 16 * (k // 2))
                    ac.inc(nc.scalar.activation(ot[k % 2][:, :], psO[:, :], AF.Copy,
                                                scale=0.5))
                    nc.gpsimd.wait_ge(ac.h, ac.n)
                    osl[k % 2].inc(nc.gpsimd.dma_start(
                        out=xo[dc, :, tq * TT:(tq + 1) * TT], in_=ot[k % 2][:, :],
                        accum_op=ALU.add), 16)
                if dc + 2 < DC:
                    wd_load(dc + 2)
            nc.gpsimd.wait_ge(osl[0].h, osl[0].n)
            nc.gpsimd.wait_ge(osl[1].h, osl[1].n)
        nc.sync.wait_ge(osl[0].h, osl[0].n)
        nc.sync.wait_ge(osl[1].h, osl[1].n)
        nc.vector.wait_ge(osl[0].h, osl[0].n)
        nc.vector.wait_ge(osl[1].h, osl[1].n)


def build_moe():
    nc = bass.Bass("TRN2", target_bir_lowering=False, debug=False)
    xfm = nc.dram_tensor("xfm", [DC, 128, T], F32, kind="ExternalInput").ap()
    rts = nc.dram_tensor("rts", [1, 128, 2048], F32, kind="ExternalInput").ap()
    wgs = nc.dram_tensor("wgs", [NE, HC_E, 128, 2048], F32, kind="ExternalInput").ap()
    wus = nc.dram_tensor("wus", [NE, HC_E, 128, 2048], F32, kind="ExternalInput").ap()
    wds = nc.dram_tensor("wds", [NE, DC, 128, 1408], F32, kind="ExternalInput").ap()
    sgs = nc.dram_tensor("sgs", [HC_S, 128, 2048], F32, kind="ExternalInput").ap()
    sus = nc.dram_tensor("sus", [HC_S, 128, 2048], F32, kind="ExternalInput").ap()
    sds = nc.dram_tensor("sds", [DC, 128, 2816], F32, kind="ExternalInput").ap()
    onesc = nc.dram_tensor("onesc", [128, 128], BF16, kind="ExternalInput").ap()
    identf = nc.dram_tensor("identf", [128, 128], F32, kind="ExternalInput").ap()
    selc = nc.dram_tensor("selc", [8, 8 * 128], BF16, kind="ExternalInput").ap()
    xo = nc.dram_tensor("xo", [DC, 128, T], F32, kind="ExternalOutput").ap()
    es = contextlib.ExitStack()
    with es:
        sm = make_sems(nc, es)
        emit_moe(nc, es, sm, xfm, rts, wgs, wus, wds, sgs, sus, sds, onesc,
                 identf, selc, xo)
    return nc


def run_moe(h, xs):
    im = [{"xfm": xs[c], "rts": h["rts"], "wgs": h["wgs"], "wus": h["wus"],
           "wds": h["wds"], "sgs": h["sgs"], "sus": h["sus"], "sds": h["sds"],
           "onesc": h["onesc"], "identf": h["identf"], "selc": h["selc"]}
          for c in range(NCORES)]
    res = _run("moe", build_moe, im)
    return [res.results[c]["xo"] for c in range(NCORES)]


NKT = 32
QTN = 4


def emit_attn(nc, es, sm, xloc, xrem, wdqs, wdkvs, wuqr, wukv, wos, gq, gkv,
              qA, qB, kA, kB, dmaskc, droprc, onesc, xo, qd, vd):
    ones = es.enter_context(nc.sbuf_tensor(uq("ones"), [128, 128], BF16))
    wdqb = es.enter_context(nc.sbuf_tensor(uq("wdqb"), [128, 2048], BF16))
    wdkvb = es.enter_context(nc.sbuf_tensor(uq("wdkvb"), [128, 2 * 2048], BF16))
    wuqb = es.enter_context(nc.sbuf_tensor(uq("wuqb"), [128, 2048], BF16))
    wukvb = es.enter_context(nc.sbuf_tensor(uq("wukvb"), [128, 1280], BF16))
    gqb = es.enter_context(nc.sbuf_tensor(uq("gqb"), [128, 1], F32))
    gkvb = es.enter_context(nc.sbuf_tensor(uq("gkvb"), [128, 1], F32))
    qAb = es.enter_context(nc.sbuf_tensor(uq("qAb"), [128, T], BF16))
    qBb = es.enter_context(nc.sbuf_tensor(uq("qBb"), [128, T], BF16))
    kAb = es.enter_context(nc.sbuf_tensor(uq("kAb"), [64, 2 * T], BF16))
    kBb = es.enter_context(nc.sbuf_tensor(uq("kBb"), [64, 2 * T], BF16))
    dmask = es.enter_context(nc.sbuf_tensor(uq("dmask"), [128, 4 * TT], BF16))
    dropr = es.enter_context(nc.sbuf_tensor(uq("dropr"), [1, NKT], F32))
    cqnall = es.enter_context(nc.sbuf_tensor(uq("cqnall"), [128, T + TT], BF16))
    ckvnall = es.enter_context(nc.sbuf_tensor(uq("ckvnall"), [128, 2 * T], BF16))
    krall = es.enter_context(nc.sbuf_tensor(uq("krall"), [64, TT], BF16))
    kall = [es.enter_context(nc.sbuf_tensor(uq(f"kall{g}"), [128, 2 * T], BF16))
            for g in range(4)]
    sc1 = es.enter_context(nc.sbuf_tensor(uq("sc1"), [128, 1024], BF16))
    sc2 = es.enter_context(nc.sbuf_tensor(uq("sc2"), [128, 1024], BF16))
    sf1 = es.enter_context(nc.sbuf_tensor(uq("sf1"), [128, TT], F32))
    sf2 = es.enter_context(nc.sbuf_tensor(uq("sf2"), [128, TT], F32))
    rows = es.enter_context(nc.sbuf_tensor(uq("rows"), [1, 5 * TT + 64], F32))
    rowsb = es.enter_context(nc.sbuf_tensor(uq("rowsb"), [1, 1024], BF16))
    lnrb = es.enter_context(nc.sbuf_tensor(uq("lnrb"), [128, TT], F32))
    kmax = es.enter_context(nc.sbuf_tensor(uq("kmax"), [1, 8], F32))
    qpr = es.enter_context(nc.sbuf_tensor(uq("qpr"), [128, 2 * TT], BF16))
    vtb = [es.enter_context(nc.sbuf_tensor(uq(f"vtb{i}"), [128, 1024], BF16)) for i in range(2)]
    sc3 = vtb[0]
    pexp = [es.enter_context(nc.sbuf_tensor(uq(f"pexp{i}"), [128, TT], BF16)) for i in range(2)]
    biasT = es.enter_context(nc.sbuf_tensor(uq("biasT"), [128, 2 * NKT], F32))
    oa = es.enter_context(nc.sbuf_tensor(uq("oa"), [128, 32 * TT], BF16))
    rlb = es.enter_context(nc.sbuf_tensor(uq("rlb"), [128, TT], F32))
    wob = es.enter_context(nc.sbuf_tensor(uq("wob"), [128, 4096], BF16))
    xot = [es.enter_context(nc.sbuf_tensor(uq(f"xot{i}"), [128, TT], F32)) for i in range(2)]

    psS = es.enter_context(nc.psum_tensor(uq("psS"), [128, TT], F32))
    psS2 = es.enter_context(nc.psum_tensor(uq("psS2"), [128, TT], F32))
    psB = es.enter_context(nc.psum_tensor(uq("psB"), [128, TT], F32))
    psL = es.enter_context(nc.psum_tensor(uq("psL"), [1, TT], F32))
    psO = [es.enter_context(nc.psum_tensor(uq(f"psO{i}"), [128, TT], F32)) for i in range(4)]

    if True:
        sy, gp, pe, ac, ve, gv = (sm["sy"], sm["gp"], sm["pe"], sm["ac"],
                                  sm["ve"], sm["gv"])
        vsl, wsl, stsl, xsl = sm["vsl"], sm["wsl"], sm["stsl"], sm["xsl"]
        vst, qst, qld = sm["vst"], sm["qst"], sm["qld"]
        ln = LNEmitter(nc, es, psO[0], psO[1], psr1=psS[0:1, :], psr2=psB[0:1, :],
                       xsq_buf=oa[:, 0:DC * TT], rows_buf=rows, rowsb_buf=rowsb,
                       mb_buf=rlb, rb_buf=lnrb, t32_bufs=[xot[0], xot[1]])
        xlnb = oa[:, DC * TT:2 * DC * TT]
        sm = {"sy": sy, "pe": pe, "ac": ac, "ve": ve, "gv": gv}

        def vch(ins):
            ve.inc(ins)
            nc.vector.wait_ge(ve.h, ve.n)

        for t, d in [(ones, onesc), (wdqb, wdqs[0]), (wuqb, wuqr), (wukvb, wukv),
                     (gqb, gq), (gkvb, gkv), (qAb, qA), (qBb, qB), (kAb, kA),
                     (kBb, kB), (dropr, droprc)]:
            gp.inc(nc.gpsimd.dma_start(out=t[:, :] if t is not dmask else t, in_=d), 16)
        gp.inc(nc.gpsimd.dma_start(out=wdkvb[:, :].rearrange("p (m q) -> p m q", m=2), in_=wdkvs[:, :, :].transpose([1, 0, 2])), 16)
        gp.inc(nc.gpsimd.dma_start(out=dmask[:, :].rearrange("p (m q) -> p m q", m=4), in_=dmaskc[:, :, :].transpose([1, 0, 2])), 16)
        nc.tensor.wait_ge(gp.h, gp.n)
        nc.vector.wait_ge(gp.h, gp.n)
        nc.scalar.wait_ge(gp.h, gp.n)

        for tt in range(2 * NTT):
            xsrc = xloc if tt < NTT else xrem
            tl = tt if tt < NTT else tt - NTT
            if tt > 0:
                nc.sync.wait_ge(gv.h, gv.n)
            ln.emit(sm, xsrc, tl,
                    lambda dc: xlnb[:, dc * TT:(dc + 1) * TT], ones)
            nc.tensor.wait_ge(gv.h, gv.n)
            col = slice(tt * TT, (tt + 1) * TT)

            def t5ln(ps_in, gcol, dest, nrows=128):
                nc.scalar.wait_ge(pe.h, pe.n)
                nc.scalar.wait_ge(ve.h, ve.n)
                nc.scalar.wait_ge(gv.h, gv.n)
                ac.inc(nc.scalar.copy(sc1[:, 0:TT], ps_in))
                ac.inc(nc.scalar.activation(sc2[:, 0:TT], ps_in, AF.Square))
                nc.tensor.wait_ge(ac.h, ac.n)
                pe.inc(nc.tensor.matmul(psS[0:1, :], ones[:, 0:1], sc1[:, 0:TT],
                                        start=True, stop=True))
                pe.inc(nc.tensor.matmul(psB[0:1, :], ones[:, 0:1], sc2[:, 0:TT],
                                        start=True, stop=True))
                nc.vector.wait_ge(pe.h, pe.n)
                r_m, r_q, r_v, r_s, r_r = (rows[:, 0:TT], rows[:, TT:2 * TT],
                                           rows[:, 2 * TT:3 * TT], rows[:, 3 * TT:4 * TT],
                                           rows[:, 4 * TT:5 * TT])
                vch(nc.vector.tensor_scalar_mul(r_m, psS[0:1, :], 1.0 / nrows))
                vch(nc.vector.tensor_scalar_mul(r_q, psB[0:1, :], 1.0 / nrows))
                vch(nc.vector.tensor_mul(r_v, r_m, r_m))
                vch(nc.vector.tensor_sub(r_v, r_q, r_v))
                ve.inc(nc.vector.tensor_scalar_add(r_v, r_v, 1e-5))
                nc.scalar.wait_ge(ve.h, ve.n)
                ac.inc(nc.scalar.activation(r_s, r_v, AF.Sqrt))
                nc.vector.wait_ge(ac.h, ac.n)
                vch(nc.vector.reciprocal(r_r, r_s))
                vch(nc.vector.tensor_copy(rowsb[:, 0:TT], r_r))
                vch(nc.vector.tensor_mul(r_s, r_m, r_r))
                ve.inc(nc.vector.tensor_copy(rowsb[:, TT:2 * TT], r_s))
                nc.tensor.wait_ge(ve.h, ve.n)
                p_s = pe.n
                pe.inc(nc.tensor.matmul(psO[0][:, :], ones[0:1, :], rowsb[:, 0:TT],
                                        start=True, stop=True))
                pe.inc(nc.tensor.matmul(psO[1][:, :], ones[0:1, :], rowsb[:, TT:2 * TT],
                                        start=True, stop=True))
                nc.scalar.wait_ge(pe.h, p_s + 1)
                ac.inc(nc.scalar.copy(sf1[:, :], psO[0][:, :]))
                nc.scalar.wait_ge(pe.h, p_s + 2)
                ac.inc(nc.scalar.copy(sf2[:, :], psO[1][:, :]))
                nc.vector.wait_ge(ac.h, ac.n)
                ve.inc(nc.vector.tensor_mul(sf1[:, :], sc1[:, 0:TT], sf1[:, :]))
                nc.gpsimd.wait_ge(ve.h, ve.n)
                gv.inc(nc.gpsimd.tensor_sub(sc3[:, 0:TT], sf1[:, :], sf2[:, :]))
                nc.vector.wait_ge(gv.h, gv.n)
                ve.inc(nc.vector.tensor_scalar(dest, sc3[:, 0:TT], gcol, None, ALU.mult))
                nc.vector.wait_ge(ve.h, ve.n)

            nc.tensor.wait_ge(ac.h, ac.n)
            for dc in range(DC):
                ins = nc.tensor.matmul(psO[2][:, :], wdqb[:, dc * 128:(dc + 1) * 128],
                                       xlnb[:, dc * TT:(dc + 1) * TT],
                                       start=(dc == 0), stop=(dc == DC - 1))
            pe.inc(ins)
            cq_col = slice(tt * TT, (tt + 1) * TT) if tt < NTT else slice(T, T + TT)
            t5ln(psO[2][:, :], gqb[:, :], cqnall[:, cq_col])
            for dc in range(DC):
                ins = nc.tensor.matmul(psO[2][:, :], wdkvb[:, dc * 128:(dc + 1) * 128],
                                       xlnb[:, dc * TT:(dc + 1) * TT],
                                       start=(dc == 0), stop=(dc == DC - 1))
            pe.inc(ins)
            for dc in range(DC):
                ins = nc.tensor.matmul(psO[3][:, :], wdkvb[:, 2048 + dc * 128:2048 + (dc + 1) * 128],
                                       xlnb[:, dc * TT:(dc + 1) * TT],
                                       start=(dc == 0), stop=(dc == DC - 1))
            pe.inc(ins)
            t5ln(psO[2][:, :], gkvb[:, :], ckvnall[:, col])
            nc.scalar.wait_ge(pe.h, pe.n)
            nc.scalar.wait_ge(ve.h, ve.n)
            nc.scalar.wait_ge(gv.h, gv.n)
            ac.inc(nc.scalar.copy(sc1[0:64, 0:TT], psO[3][0:64, :]))
            nc.vector.wait_ge(ac.h, ac.n)
            mask32 = [(i // 2 * 2) + (1 - i % 2) for i in range(32)]
            vch(nc.vector.stream_shuffle(sc2[0:64, 0:TT], sc1[0:64, 0:TT], mask32))
            vch(nc.vector.tensor_mul(sf1[0:64, :], sc1[0:64, 0:TT], kAb[:, col]))
            nc.gpsimd.wait_ge(ve.h, ve.n)
            gv.inc(nc.gpsimd.tensor_mul(sf2[0:64, :], sc2[0:64, 0:TT], kBb[:, col]))
            nc.vector.wait_ge(gv.h, gv.n)
            vch(nc.vector.tensor_add(krall[:, :], sf1[0:64, :], sf2[0:64, :]))
            nc.tensor.wait_ge(ve.h, ve.n)
            pe_k = pe.n
            pe.inc(nc.tensor.matmul(psO[2][:, :], wukvb[:, 0:128], ckvnall[:, col],
                                    start=True, stop=True))
            pe.inc(nc.tensor.matmul(psO[3][:, :], wukvb[:, 128:256], ckvnall[:, col],
                                    start=True, stop=True))
            nc.scalar.wait_ge(pe.h, pe.n)
            ac.inc(nc.scalar.copy(kall[0][0:64, tt * TT:(tt + 1) * TT], psO[2][0:64, :]))
            ac.inc(nc.scalar.copy(kall[1][0:64, tt * TT:(tt + 1) * TT], psO[2][64:128, :]))
            ac.inc(nc.scalar.copy(kall[2][0:64, tt * TT:(tt + 1) * TT], psO[3][0:64, :]))
            ac.inc(nc.scalar.copy(kall[3][0:64, tt * TT:(tt + 1) * TT], psO[3][64:128, :]))
            nc.vector.wait_ge(ve.h, ve.n)
            for g in range(4):
                ins = nc.vector.tensor_copy(kall[g][64:128, tt * TT:(tt + 1) * TT],
                                            krall[:, :])
            ve.inc(ins)
            for sub in range(4):
                nc.tensor.wait_ge(ac.h, ac.n)
                lh = ckvnall[:, tt * TT + sub * 128: tt * TT + (sub + 1) * 128]
                pe.inc(nc.tensor.matmul(psO[2][:, :], lh, wukvb[:, 256:768],
                                        start=True, stop=True))
                pe.inc(nc.tensor.matmul(psO[3][:, :], lh, wukvb[:, 768:1280],
                                        start=True, stop=True))
                nc.scalar.wait_ge(pe.h, pe.n)
                ac.inc(nc.scalar.copy(sc3[:, 0:512], psO[2][:, :]))
                ac.inc(nc.scalar.copy(sc3[:, 512:1024], psO[3][:, :]))
                nc.sync.wait_ge(ac.h, ac.n)
                vst.inc(nc.sync.dma_start(out=vd[tt * 4 + sub, :, :], in_=sc3[:, 0:1024]), 16)
                nc.scalar.wait_ge(vst.h, vst.n)

        for qt in range(QTN):
            for blk in range(16):
                nc.tensor.wait_ge(ac.h, ac.n)
                for _ in [0]:
                    ins = nc.tensor.matmul(psO[2][:, :], wuqb[:, blk * 128:(blk + 1) * 128],
                                           cqnall[:, qt * TT:(qt + 1) * TT],
                                           start=True, stop=True)
                pe.inc(ins)
                nc.scalar.wait_ge(pe.h, pe.n)
                nc.scalar.wait_ge(ve.h, ve.n)
                nc.scalar.wait_ge(gv.h, gv.n)
                ac.inc(nc.scalar.copy(sc1[:, 0:TT], psO[2][:, :]))
                nc.vector.wait_ge(ac.h, ac.n)
                qcol = slice(qt * TT, (qt + 1) * TT)
                mask32 = [(i // 2 * 2) + (1 - i % 2) for i in range(32)]
                vch(nc.vector.stream_shuffle(sc2[:, 0:TT], sc1[:, 0:TT], mask32))
                vch(nc.vector.tensor_mul(sf1[:, :], sc1[:, 0:TT], qAb[:, qcol]))
                nc.gpsimd.wait_ge(ve.h, ve.n)
                gv.inc(nc.gpsimd.tensor_mul(sf2[:, :], sc2[:, 0:TT], qBb[:, qcol]))
                nc.vector.wait_ge(gv.h, gv.n)
                vch(nc.vector.tensor_add(sc3[:, 0:TT], sf1[:, :], sf2[:, :]))
                nc.sync.wait_ge(ve.h, ve.n)
                qst.inc(nc.sync.dma_start(out=qd[blk, :, qcol], in_=sc3[:, 0:TT]), 16)
                nc.vector.wait_ge(qst.h, qst.n)

        nc.vector.wait_ge(qst.h, qst.n)
        ve.inc(nc.vector.memset(kmax[:, :], 0.0))
        for g in range(4):
            for piece in range(8):
                nc.scalar.wait_ge(pe.h, pe.n)
                nc.scalar.wait_ge(ve.h, ve.n)
                ac.inc(nc.scalar.activation(sc1[:, 0:TT],
                                            kall[g][:, piece * TT:(piece + 1) * TT],
                                            AF.Square))
                nc.tensor.wait_ge(ac.h, ac.n)
                nc.tensor.wait_ge(ve.h, ve.n)
                pe.inc(nc.tensor.matmul(psS[0:1, :], ones[:, 0:1], sc1[:, 0:TT],
                                        start=True, stop=True))
                nc.scalar.wait_ge(pe.h, pe.n)
                ac.inc(nc.scalar.activation(rows[:, 0:TT], psS[0:1, :], AF.Sqrt))
                nc.vector.wait_ge(ac.h, ac.n)
                vch(nc.vector.reduce_max(rows[:, TT:TT + 1], rows[:, 0:TT], axis=AX.X))
                vch(nc.vector.tensor_max(kmax[:, g:g + 1], kmax[:, g:g + 1],
                                         rows[:, TT:TT + 1]))

        scale = 1.0 / math.sqrt(128.0)
        mask32 = [(i // 2 * 2) + (1 - i % 2) for i in range(32)]
        for qt in range(QTN):
            kts = list(range(4 * qt + 4)) + list(range(16, NKT))
            nkt = len(kts)
            for hp in range(8):
                nc.sync.wait_ge(pe.h, pe.n)
                for hi in range(2):
                    qld.inc(nc.sync.dma_start(out=qpr[:, hi * TT:(hi + 1) * TT],
                                              in_=qd[2 * hp + hi, :, qt * TT:(qt + 1) * TT]), 16)
                for hi, h in enumerate((2 * hp, 2 * hp + 1)):
                    g = h // 4
                    nc.scalar.wait_ge(qld.h, qld.n)
                    nc.scalar.wait_ge(pe.h, pe.n)
                    nc.scalar.wait_ge(ve.h, ve.n)
                    ac.inc(nc.scalar.activation(sc1[:, 0:TT], qpr[:, hi * TT:(hi + 1) * TT],
                                                AF.Square))
                    nc.tensor.wait_ge(ac.h, ac.n)
                    nc.tensor.wait_ge(ve.h, ve.n)
                    pe.inc(nc.tensor.matmul(psS[0:1, :], ones[:, 0:1], sc1[:, 0:TT],
                                            start=True, stop=True))
                    nc.scalar.wait_ge(pe.h, pe.n)
                    ac.inc(nc.scalar.activation(rows[:, 0:TT], psS[0:1, :], AF.Sqrt))
                    nc.vector.wait_ge(ac.h, ac.n)
                    vch(nc.vector.reduce_max(rows[:, TT:TT + 1], rows[:, 0:TT], axis=AX.X))
                    vch(nc.vector.tensor_mul(rows[:, TT:TT + 1], rows[:, TT:TT + 1],
                                             kmax[:, g:g + 1]))
                    vch(nc.vector.tensor_scalar_mul(rows[:, TT:TT + 1],
                                                    rows[:, TT:TT + 1], scale))
                    vch(nc.vector.tensor_scalar(rows[:, 2 * TT:2 * TT + NKT], dropr[:, :],
                                                -1e30, rows[:, TT:TT + 1],
                                                ALU.mult, ALU.subtract))
                    ve.inc(nc.vector.tensor_copy(rowsb[:, 0:NKT], rows[:, 2 * TT:2 * TT + NKT]))
                    nc.tensor.wait_ge(ve.h, ve.n)
                    pe.inc(nc.tensor.matmul(psB[:, 0:NKT], ones[0:1, :], rowsb[:, 0:NKT],
                                            start=True, stop=True))
                    nc.scalar.wait_ge(pe.h, pe.n)
                    ac.inc(nc.scalar.copy(biasT[:, hi * NKT:(hi + 1) * NKT], psB[:, 0:NKT]))

                vs_b = [vsl[0].n, vsl[1].n]
                pe_kt = pe.n
                steps = [(ki, kt, hi) for ki, kt in enumerate(kts) for hi in (0, 1)]
                SN = len(steps)
                qk_n = [0] * SN
                pv_n = [0] * SN
                exp_n = [0] * SN
                psSx = [psS, psS2]

                def v_load(ki):
                    if ki >= 2:
                        nc.sync.wait_ge(pe.h, pv_n[2 * (ki - 2) + 1])
                    else:
                        nc.sync.wait_ge(pe.h, pe_kt)
                    vsl[ki % 2].inc(nc.sync.dma_start(out=vtb[ki % 2][:, :],
                                                      in_=vd[kts[ki], :, :]), 16)
                v_load(0)
                if nkt > 1:
                    v_load(1)
                ac_kt = ac.n

                def emit_qk(s):
                    ki2, kt2, hi2 = steps[s]
                    g2 = (2 * hp + hi2) // 4
                    if s >= 2:
                        nc.tensor.wait_ge(ac.h, exp_n[s - 2])
                    else:
                        nc.tensor.wait_ge(ac.h, ac_kt)
                    pe.inc(nc.tensor.matmul(psSx[s % 2][:, :],
                                            kall[g2][:, kt2 * 128:(kt2 + 1) * 128],
                                            qpr[:, hi2 * TT:(hi2 + 1) * TT],
                                            start=True, stop=True))
                    qk_n[s] = pe.n

                emit_qk(0)
                for s in range(SN):
                    ki, kt, hi = steps[s]
                    g = (2 * hp + hi) // 4
                    diag = kt < 16 and kt >= 4 * qt
                    nc.scalar.wait_ge(pe.h, qk_n[s])
                    if s >= 2:
                        nc.scalar.wait_ge(pe.h, pv_n[s - 2])
                    ac.inc(nc.scalar.activation(
                        pexp[hi][:, :], psSx[s % 2][:, :], AF.Exp, scale=scale,
                        bias=biasT[:, hi * NKT + kt: hi * NKT + kt + 1]))
                    exp_n[s] = ac.n
                    use_sem_h, use_n = ac.h, exp_n[s]
                    if diag:
                        nc.vector.wait_ge(ac.h, exp_n[s])
                        ve.inc(nc.vector.tensor_tensor(
                            pexp[hi][:, :], pexp[hi][:, :],
                            dmask[:, (kt - 4 * qt) * TT:(kt - 4 * qt + 1) * TT], ALU.mult))
                        use_sem_h, use_n = ve.h, ve.n
                    if s + 1 < SN:
                        emit_qk(s + 1)
                    first, last = (ki == 0), (ki == nkt - 1)
                    nc.tensor.wait_ge(use_sem_h, use_n)
                    nc.tensor.wait_ge(vsl[ki % 2].h, vs_b[ki % 2] + 16 * (ki // 2 + 1))
                    nc.tensor.matmul(psO[2 * hi][:, :],
                                     vtb[ki % 2][:, g * 256:g * 256 + 128],
                                     pexp[hi][:, :], start=first, stop=last)
                    nc.tensor.matmul(psO[2 * hi + 1][:, :],
                                     vtb[ki % 2][:, g * 256 + 128:g * 256 + 256],
                                     pexp[hi][:, :], start=first, stop=last)
                    ldst = psL[0:1, :] if hi == 0 else psB[0:1, 0:TT]
                    pe.inc(nc.tensor.matmul(ldst, ones[:, 0:1], pexp[hi][:, :],
                                            start=first, stop=last))
                    pv_n[s] = pe.n
                    if hi == 1 and ki + 2 < nkt:
                        v_load(ki + 2)
                for hi, h in enumerate((2 * hp, 2 * hp + 1)):
                    nc.vector.wait_ge(pe.h, pe.n)
                    lsrc = psL[0:1, :] if hi == 0 else psB[0:1, 0:TT]
                    vch(nc.vector.reciprocal(rows[:, 0:TT], lsrc))
                    ve.inc(nc.vector.tensor_copy(rowsb[:, 0:TT], rows[:, 0:TT]))
                    nc.tensor.wait_ge(ve.h, ve.n)
                    nc.tensor.wait_ge(ac.h, ac.n)
                    pe.inc(nc.tensor.matmul(psS[:, :], ones[0:1, :], rowsb[:, 0:TT],
                                            start=True, stop=True))
                    nc.scalar.wait_ge(pe.h, pe.n)
                    ac.inc(nc.scalar.copy(rlb[:, :], psS[:, :]))
                    nc.vector.wait_ge(ac.h, ac.n)
                    nc.gpsimd.wait_ge(ac.h, ac.n)
                    ve.inc(nc.vector.tensor_tensor(
                        oa[:, (h * 2) * TT:(h * 2 + 1) * TT],
                        psO[2 * hi][:, :], rlb[:, :], ALU.mult))
                    ve.inc(nc.vector.tensor_tensor(
                        oa[:, (h * 2 + 1) * TT:(h * 2 + 2) * TT],
                        psO[2 * hi + 1][:, :], rlb[:, :], ALU.mult))
                nc.tensor.wait_ge(ve.h, ve.n)

            ws_b = wsl[0].n
            pe_wo = pe.n

            def wo_load(dc):
                nc.gpsimd.wait_ge(pe.h, pe_wo + dc)
                wsl[0].inc(nc.gpsimd.dma_start(out=wob[:, :], in_=wos[dc, :, :]), 16)
            st_b = [stsl[0].n, stsl[1].n]
            xs_b = [xsl[0].n, xsl[1].n]
            for dc in range(DC):
                wo_load(dc)
                nc.tensor.wait_ge(wsl[0].h, ws_b + 16 * (dc + 1))
                if dc >= 1:
                    nc.tensor.wait_ge(ve.h, ve.n)
                if dc >= 2:
                    nc.sync.wait_ge(stsl[dc % 2].h, st_b[dc % 2] + 16 * (dc // 2))
                xsl[dc % 2].inc(nc.sync.dma_start(out=sf1[:, :] if dc % 2 == 0 else sf2[:, :],
                                                  in_=xloc[dc, :, qt * TT:(qt + 1) * TT]), 16)
                for hv in range(32):
                    ins = nc.tensor.matmul(psS[:, :], wob[:, hv * 128:(hv + 1) * 128],
                                           oa[:, hv * TT:(hv + 1) * TT],
                                           start=(hv == 0), stop=(hv == 31))
                pe.inc(ins)
                nc.vector.wait_ge(pe.h, pe.n)
                nc.vector.wait_ge(xsl[dc % 2].h, xs_b[dc % 2] + 16 * (dc // 2 + 1))
                if dc >= 2:
                    nc.vector.wait_ge(stsl[dc % 2].h, st_b[dc % 2] + 16 * (dc // 2))
                else:
                    nc.vector.wait_ge(stsl[dc % 2].h, st_b[dc % 2])
                ve.inc(nc.vector.tensor_add(xot[dc % 2][:, :], psS[:, :],
                                            sf1[:, :] if dc % 2 == 0 else sf2[:, :]))
                nc.scalar.wait_ge(ve.h, ve.n)
                stsl[dc % 2].inc(nc.scalar.dma_start(
                    out=xo[dc, :, qt * TT:(qt + 1) * TT], in_=xot[dc % 2][:, :]), 16)

        nc.sync.wait_ge(stsl[0].h, stsl[0].n)
        nc.sync.wait_ge(stsl[1].h, stsl[1].n)
        nc.vector.wait_ge(stsl[0].h, stsl[0].n)
        nc.vector.wait_ge(stsl[1].h, stsl[1].n)


ATTN_DRAM = dict(
    xloc=([DC, 128, T], "F32", "ExternalInput"),
    xrem=([DC, 128, T], "F32", "ExternalInput"),
    wdqs=([1, 128, 2048], "F32", "ExternalInput"),
    wdkvs=([2, 128, 2048], "F32", "ExternalInput"),
    wuqr=([128, 2048], "F32", "ExternalInput"),
    wukv=([128, 1280], "F32", "ExternalInput"),
    wos=([16, 128, 4096], "F32", "ExternalInput"),
    gq=([128, 1], "F32", "ExternalInput"),
    gkv=([128, 1], "F32", "ExternalInput"),
    qA=([128, T], "BF16", "ExternalInput"),
    qB=([128, T], "BF16", "ExternalInput"),
    kA=([64, 2 * T], "BF16", "ExternalInput"),
    kB=([64, 2 * T], "BF16", "ExternalInput"),
    dmaskc=([4, 128, TT], "BF16", "ExternalInput"),
    droprc=([1, 32], "F32", "ExternalInput"),
    onesc=([128, 128], "BF16", "ExternalInput"),
    xo=([DC, 128, T], "F32", "ExternalOutput"),
    qd=([16, 128, T], "BF16", None),
    vd=([32, 128, 1024], "BF16", None),
)


def declare_drams(nc, spec, prefix=""):
    aps = {}
    for nm, (shape, dtname, kind) in spec.items():
        dt_ = F32 if dtname == "F32" else BF16
        full = prefix + nm
        aps[nm] = (nc.dram_tensor(full, shape, dt_, kind=kind) if kind
                   else nc.dram_tensor(full, shape, dt_)).ap()
    return aps


def build_attn():
    nc = bass.Bass("TRN2", target_bir_lowering=False, debug=False)
    aps = declare_drams(nc, ATTN_DRAM)
    es = contextlib.ExitStack()
    with es:
        sm = make_sems(nc, es)
        emit_attn(nc, es, sm, **aps)
    return nc


def run_attn(h, xs):
    im = []
    for c in range(NCORES):
        half = c & 1
        im.append({"xloc": xs[c], "xrem": xs[c ^ 1], "wdqs": h["wdqs"],
                   "wdkvs": h["wdkvs"], "wuqr": h["wuqr"], "wukv": h["wukv"],
                   "wos": h["wos"], "gq": h["gq"], "gkv": h["gkv"],
                   "qA": h[f"qA{half}"], "qB": h[f"qB{half}"],
                   "kA": h[f"kA{half}"], "kB": h[f"kB{half}"],
                   "dmaskc": h["dmaskc"], "droprc": h[f"drop{half}"],
                   "onesc": h["onesc"]})
    res = _run("attn", build_attn, im)
    return [res.results[c]["xo"] for c in range(NCORES)]


PROJC = 67
CONVC = 34


def mamba_dram_spec(TM):
    return dict(
        xfm=([DC, 128, TM], "F32", "ExternalInput"),
        wins=([PROJC, 128, 2048], "F32", "ExternalInput"),
        wouts=([DC, 128, 4096], "F32", "ExternalInput"),
        convw=([128, CONVC * 4], "F32", "ExternalInput"),
        convbc=([128, CONVC], "F32", "ExternalInput"),
        nrmw=([128, 32], "F32", "ExternalInput"),
        dexp=([128, 32], "F32", "ExternalInput"),
        abias=([64, 2], "F32", "ExternalInput"),
        identb=([128, 128], "BF16", "ExternalInput"),
        identf=([128, 128], "F32", "ExternalInput"),
        onesc=([128, 128], "BF16", "ExternalInput"),
        onesf=([128, 128], "F32", "ExternalInput"),
        maskc=([128, 128], "BF16", "ExternalInput"),
        xo=([DC, 128, TM], "F32", "ExternalOutput"),
        projD=([PROJC, 128, TM], "BF16", None),
        convD=([CONVC, 128, TM], "BF16", None),
        ynD=([32, 128, TM], "BF16", None),
        crowD=([64, 128], "F32", None),
    )


def emit_mamba(nc, es, sm, TM, xfm, wins, wouts, convw, convbc, nrmw, dexp,
               abias, identb, identf, onesc, onesf, maskc, xo, projD, convD,
               ynD, crowD):
    NCH = TM // 128
    NT = TM // TT
    ar1 = es.enter_context(nc.sbuf_tensor(uq("ar1"), [128, 16384], BF16))
    ones = es.enter_context(nc.sbuf_tensor(uq("ones"), [128, 128], BF16))
    onf = es.enter_context(nc.sbuf_tensor(uq("onf"), [128, 128], F32))
    idb = es.enter_context(nc.sbuf_tensor(uq("idb"), [128, 128], BF16))
    idf = es.enter_context(nc.sbuf_tensor(uq("idf"), [128, 128], F32))
    mskb = es.enter_context(nc.sbuf_tensor(uq("mskb"), [128, 128], BF16))
    cwb = es.enter_context(nc.sbuf_tensor(uq("cwb"), [128, CONVC * 4], F32))
    cbb = es.enter_context(nc.sbuf_tensor(uq("cbb"), [128, CONVC], F32))
    nwb = es.enter_context(nc.sbuf_tensor(uq("nwb"), [128, 32], F32))
    dxb = es.enter_context(nc.sbuf_tensor(uq("dxb"), [128, 32], F32))
    abb = es.enter_context(nc.sbuf_tensor(uq("abb"), [64, 2], F32))
    winb = [es.enter_context(nc.sbuf_tensor(uq(f"winb{i}"), [128, 2048], BF16)) for i in range(2)]
    ptile = [es.enter_context(nc.sbuf_tensor(uq(f"ptile{i}"), [128, TT], BF16)) for i in range(2)]
    dgt2 = [es.enter_context(nc.sbuf_tensor(uq(f"dgt{i}"), [128, 4 * 128], BF16)) for i in range(2)]
    cvt2 = [es.enter_context(nc.sbuf_tensor(uq(f"cvt{i}"), [128, TT + 3], BF16)) for i in range(2)]
    S = es.enter_context(nc.sbuf_tensor(uq("Sst"), [128, 4096], F32))
    Sb = es.enter_context(nc.sbuf_tensor(uq("Sbst"), [128, 4096], BF16))
    d64 = es.enter_context(nc.sbuf_tensor(uq("d64"), [64, 128 * 8], F32))
    negA = es.enter_context(nc.sbuf_tensor(uq("negA"), [64, 1], F32))
    dtraw = [es.enter_context(nc.sbuf_tensor(uq(f"dtraw{i}"), [64, 128], BF16)) for i in range(2)]
    dtpT = es.enter_context(nc.sbuf_tensor(uq("dtpT"), [128, 64], BF16))
    fT = es.enter_context(nc.sbuf_tensor(uq("fT"), [128, 64], BF16))
    cT = es.enter_context(nc.sbuf_tensor(uq("cT"), [128, 64], F32))
    crow2 = [es.enter_context(nc.sbuf_tensor(uq(f"crow2_{i}"), [1, 512], F32)) for i in range(2)]
    crowE2 = [es.enter_context(nc.sbuf_tensor(uq(f"crowE2_{i}"), [1, 1024], F32)) for i in range(2)]
    grow = es.enter_context(nc.sbuf_tensor(uq("grow"), [1, 64], F32))
    epsT = es.enter_context(nc.sbuf_tensor(uq("epsT"), [1, 1], F32))
    Btile = [es.enter_context(nc.sbuf_tensor(uq(f"Btile{i}"), [128, 128], BF16)) for i in range(2)]
    Ctile = [es.enter_context(nc.sbuf_tensor(uq(f"Ctile{i}"), [128, 128], BF16)) for i in range(2)]
    Btb = es.enter_context(nc.sbuf_tensor(uq("Btb"), [128, 128], BF16))
    CBm = es.enter_context(nc.sbuf_tensor(uq("CBm"), [128, 128], BF16))
    dsb = es.enter_context(nc.sbuf_tensor(uq("dsb"), [128, 512], F32))
    cpbs = [es.enter_context(nc.sbuf_tensor(uq(f"cpbs{i}"), [128, 512], F32)) for i in range(2)]
    tf1 = [es.enter_context(nc.sbuf_tensor(uq(f"tf1_{i}"), [128, 512], F32)) for i in range(2)]
    tf2 = [es.enter_context(nc.sbuf_tensor(uq(f"tf2_{i}"), [128, 512], F32)) for i in range(2)]
    gtl = [es.enter_context(nc.sbuf_tensor(uq(f"gtl{i}"), [128, 512], BF16)) for i in range(2)]
    sgt = [es.enter_context(nc.sbuf_tensor(uq(f"sgt{i}"), [128, 512], BF16)) for i in range(2)]
    rrow = es.enter_context(nc.sbuf_tensor(uq("rrow"), [1, 128], F32))
    rrow4 = es.enter_context(nc.sbuf_tensor(uq("rrow4"), [1, 512], F32))
    woutb = [es.enter_context(nc.sbuf_tensor(uq(f"woutb{i}"), [128, 4096], BF16)) for i in range(2)]
    xrt = [es.enter_context(nc.sbuf_tensor(uq(f"xrt{i}"), [128, TT], F32)) for i in range(2)]
    xot = [es.enter_context(nc.sbuf_tensor(uq(f"xot{i}"), [128, TT], F32)) for i in range(2)]

    pA = es.enter_context(nc.psum_tensor(uq("pA"), [128, TT], F32))
    pB = es.enter_context(nc.psum_tensor(uq("pB"), [128, TT], F32))
    pC = es.enter_context(nc.psum_tensor(uq("pC"), [128, TT], F32))
    pD = es.enter_context(nc.psum_tensor(uq("pD"), [128, TT], F32))
    pE = es.enter_context(nc.psum_tensor(uq("pE"), [128, TT], F32))
    pF = es.enter_context(nc.psum_tensor(uq("pF"), [128, TT], F32))
    pG = es.enter_context(nc.psum_tensor(uq("pG"), [128, TT], F32))
    pTb = es.enter_context(nc.psum_tensor(uq("pTb"), [128, TT], BF16))

    if True:
        sy, gp, pe, ac, ve, gv = (sm["sy"], sm["gp"], sm["pe"], sm["ac"],
                                  sm["ve"], sm["gv"])
        wsl, psl, gsl, xsl, stsl = (sm["wsl"], sm["psl"], sm["gsl"], sm["xsl"],
                                    sm["stsl"])
        cld, bnc = sm["cld"], sm["bnc"]
        ln = LNEmitter(nc, es, pA, pB, psr1=pC[0:1, :], psr2=pD[0:1, :],
                       xsq_buf=ar1[:, 0:DC * TT])
        xlnb = ar1[:, DC * TT:2 * DC * TT]
        sm = {"sy": sy, "pe": pe, "ac": ac, "ve": ve, "gv": gv}

        def vch(ins):
            ve.inc(ins)
            nc.vector.wait_ge(ve.h, ve.n)

        for t, d in [(ones, onesc), (onf, onesf), (idb, identb), (idf, identf),
                     (mskb, maskc), (cwb, convw), (cbb, convbc), (nwb, nrmw),
                     (dxb, dexp), (abb, abias)]:
            gp.inc(nc.gpsimd.dma_start(out=t[:, :], in_=d), 16)
        nc.tensor.wait_ge(gp.h, gp.n)
        nc.vector.wait_ge(gp.h, gp.n)
        nc.scalar.wait_ge(gp.h, gp.n)
        gv.inc(nc.gpsimd.memset(epsT[:, :], 1e-6))
        ac.inc(nc.scalar.activation(negA[:, :], abb[:, 1:2], AF.Exp))
        nc.vector.wait_ge(ac.h, ac.n)
        vch(nc.vector.tensor_scalar_mul(negA[:, :], negA[:, :], -1.0))

        for tt in range(NT):
            if tt > 0:
                nc.sync.wait_ge(gv.h, gv.n)
            ln.emit(sm, xfm, tt, lambda dc: xlnb[:, dc * TT:(dc + 1) * TT], ones)
            nc.tensor.wait_ge(gv.h, gv.n)
            ws_b = [wsl[0].n, wsl[1].n]
            ps_b = [psl[0].n, psl[1].n]
            pe_p = pe.n
            ac_p = ac.n

            def win_load(oc):
                nc.gpsimd.wait_ge(pe.h, pe_p + max(0, oc - 1))
                wsl[oc % 2].inc(nc.gpsimd.dma_start(out=winb[oc % 2][:, :],
                                                    in_=wins[oc, :, :]), 16)
            win_load(0)
            win_load(1)
            for oc in range(PROJC):
                pps = pA if oc % 2 == 0 else pB
                nc.tensor.wait_ge(wsl[oc % 2].h, ws_b[oc % 2] + 16 * (oc // 2 + 1))
                if oc >= 2:
                    nc.tensor.wait_ge(ac.h, ac_p + oc - 1)
                for dc in range(DC):
                    ins = nc.tensor.matmul(pps[:, :], winb[oc % 2][:, dc * 128:(dc + 1) * 128],
                                           xlnb[:, dc * TT:(dc + 1) * TT],
                                           start=(dc == 0), stop=(dc == DC - 1))
                pe.inc(ins)
                nc.scalar.wait_ge(pe.h, pe_p + oc + 1)
                nc.scalar.wait_ge(psl[oc % 2].h, ps_b[oc % 2] + 16 * (oc // 2))
                ac.inc(nc.scalar.copy(ptile[oc % 2][:, :], pps[:, :]))
                nc.sync.wait_ge(ac.h, ac.n)
                psl[oc % 2].inc(nc.sync.dma_start(
                    out=projD[oc, :, tt * TT:(tt + 1) * TT], in_=ptile[oc % 2][:, :]), 16)
                if oc + 2 < PROJC:
                    win_load(oc + 2)
        nc.gpsimd.wait_ge(psl[0].h, psl[0].n)
        nc.gpsimd.wait_ge(psl[1].h, psl[1].n)

        NIT = CONVC * NT
        pe_cv0 = pe.n
        ac_cv0 = ac.n
        ps_cb = [psl[0].n, psl[1].n]
        ld_snap = [0] * NIT
        ms_snap = [0] * NIT
        dgt_snap = [0] * CONVC

        def conv_load(it):
            c2, t2 = divmod(it, NT)
            b2 = it % 2
            if it >= 2:
                nc.sync.wait_ge(pe.h, pe_cv0 + it - 1)
            if t2 == 0:
                sy.inc(nc.sync.dma_start(out=cvt2[b2][:, 3:TT + 3],
                                         in_=projD[32 + c2, :, 0:TT]), 16)
                nc.gpsimd.wait_ge(pe.h, pe_cv0 + max(0, it - 1))
                gv.inc(nc.gpsimd.memset(cvt2[b2][:, 0:3], 0.0))
            else:
                sy.inc(nc.sync.dma_start(
                    out=cvt2[b2][:, :],
                    in_=projD[32 + c2, :, t2 * TT - 3:(t2 + 1) * TT]), 16)
            ld_snap[it] = sy.n
            ms_snap[it] = gv.n

        def dgt_build(c2):
            if c2 >= 2:
                nc.vector.wait_ge(pe.h, pe_cv0 + NT * (c2 - 1))
            for k in range(4):
                ins = nc.vector.tensor_scalar(dgt2[c2 % 2][:, k * 128:(k + 1) * 128],
                                              idb[:, :],
                                              cwb[:, c2 * 4 + k:c2 * 4 + k + 1], None,
                                              ALU.mult)
            ve.inc(ins)
            dgt_snap[c2] = ve.n

        nc.vector.wait_ge(pe.h, pe.n)
        dgt_build(0)
        dgt_build(1)
        conv_load(0)
        conv_load(1)
        for it in range(NIT):
            c, tt = divmod(it, NT)
            b = it % 2
            bank = pA if b == 0 else pB
            nc.tensor.wait_ge(sy.h, ld_snap[it])
            nc.tensor.wait_ge(gv.h, ms_snap[it])
            nc.tensor.wait_ge(ve.h, dgt_snap[c])
            if it >= 2:
                nc.tensor.wait_ge(ac.h, ac_cv0 + it - 1)
            for k in range(4):
                ins = nc.tensor.matmul(bank[:, :], dgt2[c % 2][:, k * 128:(k + 1) * 128],
                                       cvt2[b][:, k:k + TT],
                                       start=(k == 0), stop=(k == 3))
            pe.inc(ins)
            if it + 2 < NIT:
                conv_load(it + 2)
            if tt == 0 and c + 2 < CONVC:
                dgt_build(c + 2)
            nc.scalar.wait_ge(pe.h, pe_cv0 + it + 1)
            nc.scalar.wait_ge(psl[b].h, ps_cb[b] + 16 * (it // 2))
            ac.inc(nc.scalar.activation(ptile[b][:, :], bank[:, :], AF.Silu,
                                        bias=cbb[:, c:c + 1]))
            nc.sync.wait_ge(ac.h, ac_cv0 + it + 1)
            psl[b].inc(nc.sync.dma_start(
                out=convD[c, :, tt * TT:(tt + 1) * TT], in_=ptile[b][:, :]), 16)
        nc.gpsimd.wait_ge(psl[0].h, psl[0].n)
        nc.gpsimd.wait_ge(psl[1].h, psl[1].n)

        xa_bufs = [ar1[:, 12288:16384], ar1[:, 8192:12288]]
        sqsb = Sb[:, :]
        sT = ar1[:, 0:8192]
        yg = ln.xt[:, 0:4096]
        Xp = ln.xt[:, 4096:6144].bitcast(BF16)
        Xpp = ln.xt[:, 6144:8192].bitcast(BF16)
        nc.vector.wait_ge(gv.h, gv.n)
        ve.inc(nc.vector.memset(S[:, :], 0.0))
        nc.sync.wait_ge(psl[0].h, psl[0].n)
        nc.sync.wait_ge(psl[1].h, psl[1].n)

        def chunk_loads(ch2, eng, sem):
            b2 = ch2 % 2
            cc2 = slice(ch2 * 128, (ch2 + 1) * 128)
            for ic in range(32):
                sem.inc(eng.dma_start(out=xa_bufs[b2][:, ic * 128:(ic + 1) * 128],
                                      in_=convD[ic, :, cc2]), 16)
            sem.inc(eng.dma_start(out=Btile[b2][:, :], in_=convD[32, :, cc2]), 16)
            sem.inc(eng.dma_start(out=Ctile[b2][:, :], in_=convD[33, :, cc2]), 16)
            sem.inc(eng.dma_start(out=dtraw[b2][:, :], in_=projD[66, 0:64, cc2]), 16)

        chunk_loads(0, nc.sync, cld)
        cld_snap = [cld.n]
        prev = {}
        for ch in range(NCH):
            buf = ch % 2
            xa = xa_bufs[buf]
            Bt_c, Ct_c, dtr = Btile[buf], Ctile[buf], dtraw[buf]
            ccol = slice(ch * 128, (ch + 1) * 128)
            zt = d64[:, 0:128]
            a1 = d64[:, 128:256]
            a2 = d64[:, 256:384]
            dtp = d64[:, 384:512]
            s_t = d64[:, 512:640]
            chm = d64[:, 640:768]
            fF = d64[:, 768:896]
            nc.vector.wait_ge(cld.h, cld_snap[ch])
            if ch > 0:
                nc.vector.wait_ge(bnc.h, prev["bnc"])
                nc.vector.wait_ge(pe.h, prev["pe_tr"])
            vch(nc.vector.tensor_scalar(zt, dtr[:, :], abb[:, 0:1], None, ALU.add))
            nc.scalar.wait_ge(ve.h, ve.n)
            ac.inc(nc.scalar.activation(a1, zt, AF.Abs))
            nc.scalar.wait_ge(ac.h, ac.n)
            ac.inc(nc.scalar.activation(a1, a1, AF.Exp, scale=-1.0))
            nc.scalar.wait_ge(ac.h, ac.n)
            ac.inc(nc.scalar.activation(a1, a1, AF.Ln, bias=1.0))
            ac.inc(nc.scalar.activation(a2, zt, AF.Relu))
            nc.vector.wait_ge(ac.h, ac.n)
            vch(nc.vector.tensor_add(dtp, a1, a2))
            vch(nc.vector.tensor_scalar_min(dtp, dtp, 1e6))
            vch(nc.vector.tensor_scalar(s_t, dtp, negA[:, :], None, ALU.mult))
            vch(nc.vector.tensor_tensor_scan(chm, s_t, s_t, 0.0, ALU.add, ALU.bypass))
            vch(nc.vector.tensor_scalar(fF, chm, chm[:, 127:128], None, ALU.subtract))
            nc.scalar.wait_ge(ve.h, ve.n)
            ac.inc(nc.scalar.activation(fF, fF, AF.Exp, scale=-1.0))
            nc.sync.wait_ge(ve.h, ve.n)
            bnc.inc(nc.sync.dma_start(out=crowD[:, :], in_=chm), 16)
            bnc_chm = bnc.n
            nc.sync.wait_ge(bnc.h, bnc_chm)
            if ch > 0:
                nc.sync.wait_ge(pe.h, prev["pe_state"])
            bnc.inc(nc.sync.dma_start(out=grow[:, :],
                                      in_=crowD[:, 127:128].transpose([1, 0]),
                                      allow_slow_non_contiguous=True), 16)
            bnc_grow = bnc.n
            nc.scalar.wait_ge(bnc.h, bnc_grow)
            ac.inc(nc.scalar.activation(grow[:, :], grow[:, :], AF.Exp))
            ac_grow = ac.n
            nc.tensor.wait_ge(ve.h, ve.n)
            nc.tensor.wait_ge(ac.h, ac.n)
            pe.inc(nc.tensor.transpose(pC[:, 0:64], chm, idf[0:64, 0:64]))
            nc.scalar.wait_ge(pe.h, pe.n)
            ac.inc(nc.scalar.copy(cT[:, :], pC[:, 0:64]))
            nc.tensor.wait_ge(ac.h, ac.n)
            pe.inc(nc.tensor.transpose(pC[:, 0:64], dtp, idf[0:64, 0:64]))
            nc.scalar.wait_ge(pe.h, pe.n)
            ac.inc(nc.scalar.copy(dtpT[:, :], pC[:, 0:64]))
            nc.tensor.wait_ge(ac.h, ac.n)
            pe.inc(nc.tensor.transpose(pC[:, 0:64], fF, idf[0:64, 0:64]))
            nc.scalar.wait_ge(pe.h, pe.n)
            ac.inc(nc.scalar.copy(fT[:, :], pC[:, 0:64]))
            nc.tensor.wait_ge(ac.h, ac.n)
            pe.inc(nc.tensor.matmul(pC[:, 0:128], Bt_c[:, :], Ct_c[:, :],
                                    start=True, stop=True))
            nc.vector.wait_ge(pe.h, pe.n)
            vch(nc.vector.tensor_tensor(CBm[:, :], pC[:, 0:128], mskb[:, :], ALU.mult))
            pe.inc(nc.tensor.transpose(pTb[:, 0:128], Bt_c[:, :], idb[:, :]))
            nc.scalar.wait_ge(pe.h, pe.n)
            ac.inc(nc.scalar.copy(Btb[:, :], pTb[:, 0:128]))
            nc.tensor.wait_ge(ac.h, ac.n)
            ve_xt = ve.n
            for grp in range(8):
                if grp > 0:
                    nc.tensor.wait_ge(ve.h, ve_xt + grp)
                for j in range(4):
                    ic = grp * 4 + j
                    ins = nc.tensor.transpose(pTb[:, j * 128:(j + 1) * 128],
                                              xa[:, ic * 128:(ic + 1) * 128], idb[:, :])
                pe.inc(ins)
                nc.vector.wait_ge(pe.h, pe.n)
                ve.inc(nc.vector.tensor_tensor(
                    Xp[:, grp * 512:(grp + 1) * 512].rearrange("p (h q) -> p h q", h=8),
                    pTb[:, 0:512].rearrange("p (h q) -> p h q", h=8),
                    dtpT[:, grp * 8:(grp + 1) * 8].unsqueeze(2).broadcast_to([128, 8, 64]), ALU.mult))
            pe_tr = pe.n
            nc.gpsimd.wait_ge(ve.h, ve.n)
            if ch > 0:
                nc.gpsimd.wait_ge(pe.h, prev["pe_state"])
            gv.inc(nc.gpsimd.tensor_tensor(
                Xpp[:, :].rearrange("p (h q) -> p h q", h=64),
                Xp[:, :].rearrange("p (h q) -> p h q", h=64),
                fT[:, :].unsqueeze(2).broadcast_to([128, 64, 64]),
                ALU.mult))
            gv_xpp = gv.n
            nc.vector.wait_ge(pe.h, pe.n)
            vch(nc.vector.tensor_copy(Sb[:, :], S[:, :]))
            ve_mk = ve.n
            ac_mk = ac.n
            pe_mk = pe.n
            for pc_i in range(16):
                hb = pc_i * 4
                if pc_i == 0:
                    nc.sync.wait_ge(bnc.h, bnc_chm)
                nc.sync.wait_ge(pe.h, pe_mk + max(0, pc_i - 1))
                bnc.inc(nc.sync.dma_start(
                    out=crow2[pc_i % 2][:, :],
                    in_=crowD[hb:hb + 4, :].rearrange("a b -> (a b)").unsqueeze(0)), 16)
                bnc_mk = bnc.n
                nc.tensor.wait_ge(bnc.h, bnc_mk)
                if pc_i == 0:
                    nc.tensor.wait_ge(ve.h, ve.n)
                else:
                    nc.tensor.wait_ge(ve.h, ve_mk + 3 * (pc_i - 1) + 1)
                pe.inc(nc.tensor.matmul(pA[:, :], onf[0:1, :], crow2[pc_i % 2][:, :],
                                        start=True, stop=True))
                nc.vector.wait_ge(pe.h, pe_mk + pc_i + 1)
                ve.inc(nc.vector.tensor_tensor(
                    dsb[:, :].rearrange("p (h t) -> p h t", h=4),
                    pA[:, :].rearrange("p (h t) -> p h t", h=4),
                    cT[:, hb:hb + 4].unsqueeze(2).broadcast_to([128, 4, 128]),
                    ALU.subtract))
                ve.inc(nc.vector.tensor_scalar_min(dsb[:, :], dsb[:, :], 0.0))
                nc.scalar.wait_ge(ve.h, ve_mk + 3 * pc_i + 2)
                ac.inc(nc.scalar.activation(dsb[:, :], dsb[:, :], AF.Exp))
                nc.vector.wait_ge(ac.h, ac_mk + pc_i + 1)
                ve.inc(nc.vector.tensor_tensor(
                    sT[:, pc_i * 512:(pc_i + 1) * 512].rearrange("p (h t) -> p h t", h=4),
                    dsb[:, :].rearrange("p (h t) -> p h t", h=4),
                    CBm[:, :].unsqueeze(1).broadcast_to([128, 4, 128]),
                    ALU.mult))
            pe_mask_end = pe.n
            ysets = [(pB, pC, pD), (pE, pF, pG)]
            pe_y0 = pe.n
            ac_y0 = ac.n
            ve_y0 = ve.n
            gv_y0 = gv.n
            gl_b = [gsl[0].n, gsl[1].n]
            ce_snap = [0] * 8

            def gate_load(g2):
                if g2 >= 2:
                    nc.sync.wait_ge(ac.h, ac_y0 + 3 * (g2 - 2) + 3)
                else:
                    nc.sync.wait_ge(ac.h, ac_y0)
                for j2 in range(4):
                    gsl[g2 % 2].inc(nc.sync.dma_start(
                        out=gtl[g2 % 2][:, j2 * 128:(j2 + 1) * 128],
                        in_=projD[4 * g2 + j2, :, ccol]), 16)

            def crowe_load(g2):
                if g2 >= 2:
                    nc.sync.wait_ge(pe.h, pe_y0 + 3 * (g2 - 2) + 1)
                elif ch > 0:
                    nc.sync.wait_ge(pe.h, prev["pe_y_end"])
                bnc.inc(nc.sync.dma_start(
                    out=crowE2[g2 % 2][:, :],
                    in_=crowD[8 * g2:8 * g2 + 8, :].rearrange("a b -> (a b)").unsqueeze(0)), 16)
                ce_snap[g2] = bnc.n

            gate_load(0)
            crowe_load(0)
            gate_load(1)
            crowe_load(1)
            for g in range(8):
                pBg, pCg, pDg = ysets[g % 2]
                nc.scalar.wait_ge(bnc.h, ce_snap[g])
                if g >= 2:
                    nc.scalar.wait_ge(pe.h, pe_y0 + 3 * (g - 2) + 1)
                ac.inc(nc.scalar.activation(crowE2[g % 2][:, :], crowE2[g % 2][:, :],
                                            AF.Exp))
                nc.tensor.wait_ge(ac.h, ac_y0 + 3 * g + 1)
                if g >= 2:
                    nc.tensor.wait_ge(ac.h, ac_y0 + 3 * (g - 2) + 2)
                    nc.tensor.wait_ge(ve.h, ve_y0 + 4 * (g - 2) + 2)
                nc.tensor.wait_ge(ve.h, ve_mk + 3 * (2 * g + 2))
                for j in range(4):
                    nc.tensor.matmul(pDg[0:64, j * 128:(j + 1) * 128], onf[0:1, 0:64],
                                     crowE2[g % 2][:, (2 * j) * 128:(2 * j + 1) * 128],
                                     start=True, stop=True)
                    ins = nc.tensor.matmul(
                        pDg[64:128, j * 128:(j + 1) * 128], onf[0:1, 0:64],
                        crowE2[g % 2][:, (2 * j + 1) * 128:(2 * j + 2) * 128],
                        start=True, stop=True)
                pe.inc(ins)
                for j in range(4):
                    pr = 4 * g + j
                    ins = nc.tensor.matmul(pBg[:, j * 128:(j + 1) * 128],
                                           Sb[:, pr * 128:(pr + 1) * 128], Ct_c[:, :],
                                           start=True, stop=True)
                pe.inc(ins)
                for j in range(4):
                    pr = 4 * g + j
                    nc.tensor.matmul(pCg[0:64, j * 128:(j + 1) * 128],
                                     Xp[:, pr * 128:pr * 128 + 64],
                                     sT[:, (2 * pr) * 128:(2 * pr + 1) * 128],
                                     start=True, stop=True)
                    ins = nc.tensor.matmul(pCg[64:128, j * 128:(j + 1) * 128],
                                           Xp[:, pr * 128 + 64:(pr + 1) * 128],
                                           sT[:, (2 * pr + 1) * 128:(2 * pr + 2) * 128],
                                           start=True, stop=True)
                pe.inc(ins)
                nc.scalar.wait_ge(pe.h, pe_y0 + 3 * g + 1)
                if g >= 2:
                    nc.scalar.wait_ge(ve.h, ve_y0 + 4 * (g - 2) + 1)
                else:
                    nc.scalar.wait_ge(ve.h, ve_y0)
                ac.inc(nc.scalar.copy(cpbs[g % 2][:, :], pDg[:, 0:512]))
                nc.scalar.wait_ge(gsl[g % 2].h, gl_b[g % 2] + 64 * (g // 2 + 1))
                if g >= 2:
                    nc.scalar.wait_ge(gv.h, gv_y0 + (g - 2) + 1)
                ac.inc(nc.scalar.activation(sgt[g % 2][:, :], gtl[g % 2][:, :],
                                            AF.Silu))
                nc.vector.wait_ge(pe.h, pe_y0 + 3 * g + 2)
                nc.vector.wait_ge(ac.h, ac_y0 + 3 * g + 2)
                if g >= 2:
                    nc.vector.wait_ge(gv.h, gv_y0 + (g - 2) + 1)
                ve.inc(nc.vector.tensor_tensor(tf1[g % 2][:, :], pBg[:, 0:512],
                                               cpbs[g % 2][:, :], ALU.mult))
                nc.vector.wait_ge(pe.h, pe_y0 + 3 * g + 3)
                ve.inc(nc.vector.tensor_tensor(tf1[g % 2][:, :], pCg[:, 0:512],
                                               tf1[g % 2][:, :], ALU.add))
                ve.inc(nc.vector.tensor_tensor(
                    tf2[g % 2][:, :].rearrange("p (i q) -> p i q", q=128),
                    xa[:, g * 512:(g + 1) * 512].rearrange("p (i q) -> p i q", q=128),
                    dxb[:, 4 * g:4 * g + 4].unsqueeze(2).broadcast_to([128, 4, 128]),
                    ALU.mult))
                ve.inc(nc.vector.tensor_tensor(tf1[g % 2][:, :], tf1[g % 2][:, :],
                                               tf2[g % 2][:, :], ALU.add))
                nc.gpsimd.wait_ge(ve.h, ve_y0 + 4 * g + 4)
                nc.gpsimd.wait_ge(ac.h, ac_y0 + 3 * g + 3)
                gv.inc(nc.gpsimd.tensor_mul(yg[:, g * 512:(g + 1) * 512],
                                            tf1[g % 2][:, :], sgt[g % 2][:, :]))
                if g + 2 < 8:
                    gate_load(g + 2)
                    crowe_load(g + 2)
            pe_y_end = pe.n
            if ch + 1 < NCH:
                chunk_loads(ch + 1, nc.gpsimd, cld)
                cld_snap.append(cld.n)
            pe_st0 = pe.n
            ve_st0 = ve.n
            for pc_i in range(8):
                cols = slice(pc_i * 512, (pc_i + 1) * 512)
                if pc_i == 0:
                    nc.tensor.wait_ge(ve.h, ve_y0 + 32)
                    nc.tensor.wait_ge(gv.h, gv_xpp)
                    nc.tensor.wait_ge(ac.h, ac_grow)
                else:
                    nc.tensor.wait_ge(ve.h, ve_st0 + 2 * (pc_i - 1) + 2)
                pe.inc(nc.tensor.matmul(pA[:, :], Btb[:, :], Xpp[:, cols],
                                        start=True, stop=True))
                pe.inc(nc.tensor.matmul(pB[:, :], onf[0:1, :],
                                        grow[:, :].unsqueeze(2).broadcast_to([1, 64, 64])[:, pc_i * 8:(pc_i + 1) * 8, :],
                                        start=True, stop=True))
                nc.vector.wait_ge(pe.h, pe_st0 + 2 * pc_i + 2)
                ve.inc(nc.vector.tensor_tensor(S[:, cols], S[:, cols], pB[:, :], ALU.mult))
                ve.inc(nc.vector.tensor_tensor(S[:, cols], S[:, cols], pA[:, :], ALU.add))
            pe_state = pe.n
            nc.scalar.wait_ge(gv.h, gv.n)
            nc.scalar.wait_ge(pe.h, pe_y_end)
            ac.inc(nc.scalar.activation(sqsb[:, :], yg[:, :], AF.Square))
            nc.tensor.wait_ge(ac.h, ac.n)
            nc.tensor.wait_ge(ve.h, ve.n)
            for ic in range(32):
                ins = nc.tensor.matmul(pD[0:1, 0:128], ones[:, 0:1],
                                       sqsb[:, ic * 128:(ic + 1) * 128],
                                       start=(ic == 0), stop=(ic == 31))
            pe.inc(ins)
            nc.scalar.wait_ge(pe.h, pe.n)
            ac.inc(nc.scalar.activation(rrow[:, :], pD[0:1, 0:128], AF.Ln,
                                        scale=1.0 / 4096.0, bias=epsT[:, :]))
            nc.scalar.wait_ge(ac.h, ac.n)
            ac.inc(nc.scalar.activation(rrow[:, :], rrow[:, :], AF.Exp, scale=-0.5))
            nc.scalar.wait_ge(ac.h, ac.n)
            for r4 in range(4):
                ac.inc(nc.scalar.copy(rrow4[:, r4 * 128:(r4 + 1) * 128], rrow[:, :]))
            nc.tensor.wait_ge(ac.h, ac.n)
            pe.inc(nc.tensor.matmul(pD[:, 0:512], onf[0:1, :], rrow4[:, :],
                                    start=True, stop=True))
            st_b = [stsl[0].n, stsl[1].n]
            ve_rms = ve.n
            nc.vector.wait_ge(pe.h, pe.n)
            nc.vector.wait_ge(gv.h, gv.n)
            for k in range(8):
                if k >= 2:
                    nc.vector.wait_ge(stsl[k % 2].h, st_b[k % 2] + 64 * (k // 2))
                ve.inc(nc.vector.tensor_tensor(tf1[k % 2][:, :],
                                               yg[:, k * 512:(k + 1) * 512],
                                               pD[:, 0:512], ALU.mult))
                ve.inc(nc.vector.tensor_tensor(
                    sgt[k % 2][:, :].rearrange("p (i q) -> p i q", q=128),
                    tf1[k % 2][:, :].rearrange("p (i q) -> p i q", q=128),
                    nwb[:, 4 * k:4 * k + 4].unsqueeze(2).broadcast_to([128, 4, 128]),
                    ALU.mult))
                nc.scalar.wait_ge(ve.h, ve_rms + 2 * k + 2)
                for j in range(4):
                    stsl[k % 2].inc(nc.scalar.dma_start(
                        out=ynD[4 * k + j, :, ccol],
                        in_=sgt[k % 2][:, j * 128:(j + 1) * 128]), 16)
            prev = {"bnc": bnc.n, "pe_tr": pe_tr, "pe_state": pe_state,
                    "pe_y_end": pe_y_end}

        nc.gpsimd.wait_ge(stsl[0].h, stsl[0].n)
        nc.gpsimd.wait_ge(stsl[1].h, stsl[1].n)
        nc.sync.wait_ge(stsl[0].h, stsl[0].n)
        nc.sync.wait_ge(stsl[1].h, stsl[1].n)
        ynt = ar1[:, 0:16384]
        for tt in range(NT):
            yl_b = cld.n
            nc.sync.wait_ge(pe.h, pe.n)
            for ic in range(32):
                cld.inc(nc.sync.dma_start(out=ynt[:, ic * TT:(ic + 1) * TT],
                                          in_=ynD[ic, :, tt * TT:(tt + 1) * TT]), 16)
            ws_b = [wsl[0].n, wsl[1].n]
            xs_b = [xsl[0].n, xsl[1].n]
            st_b = [stsl[0].n, stsl[1].n]
            pe_w = pe.n

            def wout_load(dc):
                nc.gpsimd.wait_ge(pe.h, pe_w + max(0, dc - 1))
                wsl[dc % 2].inc(nc.gpsimd.dma_start(out=woutb[dc % 2][:, :],
                                                    in_=wouts[dc, :, :]), 16)
            wout_load(0)
            wout_load(1)
            ve_w = ve.n
            for dc in range(DC):
                nc.tensor.wait_ge(wsl[dc % 2].h, ws_b[dc % 2] + 16 * (dc // 2 + 1))
                nc.tensor.wait_ge(cld.h, cld.n)
                nc.tensor.wait_ge(ve.h, ve_w + dc)
                nc.sync.wait_ge(stsl[dc % 2].h, st_b[dc % 2] + 16 * (dc // 2))
                xr_src = (xfm(dc, slice(tt * TT, (tt + 1) * TT)) if callable(xfm)
                          else xfm[dc, :, tt * TT:(tt + 1) * TT])
                xsl[dc % 2].inc(nc.sync.dma_start(
                    out=xrt[dc % 2][:, :], in_=xr_src), 16)
                for ic in range(32):
                    ins = nc.tensor.matmul(pA[:, :], woutb[dc % 2][:, ic * 128:(ic + 1) * 128],
                                           ynt[:, ic * TT:(ic + 1) * TT],
                                           start=(ic == 0), stop=(ic == 31))
                pe.inc(ins)
                nc.vector.wait_ge(pe.h, pe.n)
                nc.vector.wait_ge(xsl[dc % 2].h, xs_b[dc % 2] + 16 * (dc // 2 + 1))
                if dc >= 2:
                    nc.vector.wait_ge(stsl[dc % 2].h, st_b[dc % 2] + 16 * (dc // 2))
                ve.inc(nc.vector.tensor_add(xot[dc % 2][:, :], pA[:, :], xrt[dc % 2][:, :]))
                nc.scalar.wait_ge(ve.h, ve.n)
                stsl[dc % 2].inc(nc.scalar.dma_start(
                    out=xo[dc, :, tt * TT:(tt + 1) * TT], in_=xot[dc % 2][:, :]), 16)
                if dc + 2 < DC:
                    wout_load(dc + 2)
        nc.sync.wait_ge(stsl[0].h, stsl[0].n)
        nc.sync.wait_ge(stsl[1].h, stsl[1].n)
        nc.vector.wait_ge(stsl[0].h, stsl[0].n)
        nc.vector.wait_ge(stsl[1].h, stsl[1].n)


def build_mamba(TM=4096):
    nc = bass.Bass("TRN2", target_bir_lowering=False, debug=False)
    aps = declare_drams(nc, mamba_dram_spec(TM))
    es = contextlib.ExitStack()
    with es:
        sm = make_sems(nc, es)
        emit_mamba(nc, es, sm, TM, **aps)
    return nc


def run_mamba(h, xs):
    full = [np.concatenate([xs[2 * b], xs[2 * b + 1]], axis=2) for b in range(4)]
    im = []
    for c in range(NCORES):
        b = c % 4
        im.append({"xfm": full[b], "wins": h["wins"], "wouts": h["wouts"],
                   "convw": h["convw"], "convbc": h["convbc"], "nrmw": h["nrmw"],
                   "dexp": h["dexp"], "abias": h["abias"], "identb": h["identb"],
                   "identf": h["identf"], "onesc": h["onesc"], "onesf": h["onesf"],
                   "maskc": h["maskc"]})
    res = _run("mamba", build_mamba, im)
    out = []
    for c in range(NCORES):
        b, hh = c >> 1, c & 1
        out.append(np.ascontiguousarray(res.results[b]["xo"][:, :, hh * T:(hh + 1) * T]))
    return out


FFN_DRAM = dict(
    w1s=([64, 128, 2048], "F32", "ExternalInput"),
    w3s=([64, 128, 2048], "F32", "ExternalInput"),
    onesc=([128, 128], "BF16", "ExternalInput"),
)
MOE_DRAM = dict(
    rts=([1, 128, 2048], "F32", "ExternalInput"),
    wgs=([NE, HC_E, 128, 2048], "F32", "ExternalInput"),
    wus=([NE, HC_E, 128, 2048], "F32", "ExternalInput"),
    wds=([NE, DC, 128, 1408], "F32", "ExternalInput"),
    sgs=([HC_S, 128, 2048], "F32", "ExternalInput"),
    sus=([HC_S, 128, 2048], "F32", "ExternalInput"),
    sds=([DC, 128, 2816], "F32", "ExternalInput"),
    onesc=([128, 128], "BF16", "ExternalInput"),
    identf=([128, 128], "F32", "ExternalInput"),
    selc=([8, 8 * 128], "BF16", "ExternalInput"),
)
PREP_DRAM = dict(
    w2t=([8192, 2048], "F32", "ExternalInput"),
    h128=([128, 128], "F32", "ExternalInput"),
    h64=([64, 64], "F32", "ExternalInput"),
    w2ps=([16, 128, 8192], "BF16", None),
    t1d=([64, 128, 2048], "BF16", None),
)


def emit_halfsel(nc, es, sm, srcfn, dst, hs, flip=False):
    sy, ve, gv = sm["sy"], sm["ve"], sm["gv"]
    t0 = [es.enter_context(nc.sbuf_tensor(uq(f"hs_t0_{i}"), [128, TT], F32)) for i in range(2)]
    t1 = [es.enter_context(nc.sbuf_tensor(uq(f"hs_t1_{i}"), [128, TT], F32)) for i in range(2)]
    t2 = [es.enter_context(nc.sbuf_tensor(uq(f"hs_t2_{i}"), [128, TT], F32)) for i in range(2)]
    hsb = es.enter_context(nc.sbuf_tensor(uq("hs_m"), [128, 2], F32))
    st = sm["stsl"][0]
    sy.inc(nc.sync.dma_start(out=hsb[:, :], in_=hs), 16)
    c0, c1 = (1, 0) if flip else (0, 1)
    NITH = DC * NTT
    ve0 = ve.n
    st0 = st.n
    ld = [0] * NITH

    def hs_load(it):
        dc2, tt2 = divmod(it, NTT)
        b2 = it % 2
        if it >= 2:
            nc.sync.wait_ge(ve.h, ve0 + 2 * (it - 1))
        a_src, b_src = srcfn(dc2, tt2)
        sy.inc(nc.sync.dma_start(out=t0[b2][:, :], in_=a_src), 16)
        sy.inc(nc.sync.dma_start(out=t1[b2][:, :], in_=b_src), 16)
        ld[it] = sy.n

    hs_load(0)
    hs_load(1)
    for it in range(NITH):
        dc, tt = divmod(it, NTT)
        b = it % 2
        nc.vector.wait_ge(sy.h, ld[it])
        if it >= 2:
            nc.vector.wait_ge(st.h, st0 + 16 * (it - 1))
        ve.inc(nc.vector.tensor_scalar(t2[b][:, :], t0[b][:, :], hsb[:, c0:c0 + 1],
                                       None, ALU.mult))
        ve.inc(nc.vector.scalar_tensor_tensor(
            t2[b][:, :], t1[b][:, :], hsb[:, c1:c1 + 1], t2[b][:, :],
            ALU.mult, ALU.add))
        if it + 2 < NITH:
            hs_load(it + 2)
        nc.scalar.wait_ge(ve.h, ve0 + 2 * it + 2)
        st.inc(nc.scalar.dma_start(
            out=dst[dc, :, tt * TT:(tt + 1) * TT], in_=t2[b][:, :]), 16)
    nc.sync.wait_ge(st.h, st.n)
    nc.gpsimd.wait_ge(st.h, st.n)
    nc.vector.wait_ge(st.h, st.n)
    nc.scalar.wait_ge(st.h, st.n)


def build_fused():
    nc = bass.Bass("TRN2", target_bir_lowering=False, debug=False)
    xin = nc.dram_tensor("xin", [DC, 128, T], F32, kind="ExternalInput").ap()
    xfinal = nc.dram_tensor("xfinal", [DC, 128, T], F32, kind="ExternalOutput").ap()
    hs1 = nc.dram_tensor("hs1", [128, 2], F32, kind="ExternalInput").ap()
    hs2 = nc.dram_tensor("hs2", [128, 2], F32, kind="ExternalInput").ap()
    xa = nc.dram_tensor("xa", [DC, 128, T], F32).ap()
    xb = nc.dram_tensor("xb", [DC, 128, T], F32).ap()
    xc = nc.dram_tensor("xc", [DC, 128, T], F32).ap()
    xd = nc.dram_tensor("xd", [DC, 128, T], F32).ap()
    xe = nc.dram_tensor("xe", [DC, 128, T], F32).ap()
    g1out = nc.dram_tensor("g1out", [2 * D, T], F32).ap()
    g2out = nc.dram_tensor("g2out", [2 * D, T], F32).ap()
    mxfm = nc.dram_tensor("mxfm", [DC, 128, 2 * T], F32).ap()
    mxo = nc.dram_tensor("mxo", [DC, 128, 2 * T], F32).ap()
    axrem = nc.dram_tensor("axrem", [DC, 128, T], F32).ap()

    p_aps = declare_drams(nc, PREP_DRAM, "p_")
    f_aps = declare_drams(nc, FFN_DRAM, "f_")
    spec_m = mamba_dram_spec(2 * T)
    del spec_m["xfm"]
    del spec_m["xo"]
    spec_m = {k: (v[0], v[1], v[2]) for k, v in spec_m.items()}
    m_aps = declare_drams(nc, spec_m, "m_")
    e_aps = declare_drams(nc, MOE_DRAM, "e_")
    spec_a = dict(ATTN_DRAM)
    del spec_a["xloc"]
    del spec_a["xrem"]
    del spec_a["xo"]
    a_aps = declare_drams(nc, spec_a, "a_")
    CAST_LIST = [(f_aps, "w1s", FFN_DRAM), (f_aps, "w3s", FFN_DRAM),
                 (m_aps, "wins", spec_m), (m_aps, "wouts", spec_m),
                 (e_aps, "wgs", MOE_DRAM), (e_aps, "wus", MOE_DRAM),
                 (e_aps, "wds", MOE_DRAM), (e_aps, "sgs", MOE_DRAM),
                 (e_aps, "sus", MOE_DRAM), (e_aps, "sds", MOE_DRAM),
                 (a_aps, "wos", spec_a)]
    shadows = []
    for aps_d, nm, spec in CAST_LIST:
        shape = spec[nm][0]
        sh = nc.dram_tensor("bf_" + nm, shape, BF16).ap()
        shadows.append((aps_d, nm, sh))

    es0 = contextlib.ExitStack()
    with es0:
        sm = make_sems(nc, es0)
        cc = S_(nc, es0, "cc")

        def bar():
            nc.all_engine_barrier()

        def gather(src, out):
            cc.inc(nc.gpsimd.collective_compute(
                "AllGather", ALU.bypass,
                replica_groups=[[0, 1], [2, 3], [4, 5], [6, 7]],
                ins=[src.rearrange("c p t -> (c p) t")],
                outs=[out]), 1)
            nc.gpsimd.wait_ge(cc.h, cc.n)
            bar()

        for aps_d, nm, sh in shadows:
            fp = aps_d[nm]
            nflat = 1
            for s in fp.shape:
                nflat *= s
            rows = nflat // 4096
            sm["gp"].inc(nc.gpsimd.dma_start(
                out=sh.rearrange(" ".join(f"a{i}" for i in range(len(sh.shape)))
                                 + " -> (" + " ".join(f"a{i}" for i in range(len(sh.shape))) + ")")
                    .rearrange("(r c) -> r c", c=4096),
                in_=fp.rearrange(" ".join(f"a{i}" for i in range(len(fp.shape)))
                                 + " -> (" + " ".join(f"a{i}" for i in range(len(fp.shape))) + ")")
                    .rearrange("(r c) -> r c", c=4096)), 16)
            aps_d[nm] = sh
        with contextlib.ExitStack() as es:
            emit_prep(nc, es, sm, p_aps["w2t"], p_aps["h128"], p_aps["h64"],
                      p_aps["w2ps"], p_aps["t1d"])
        nc.gpsimd.wait_ge(sm["gp"].h, sm["gp"].n)
        bar()
        with contextlib.ExitStack() as es:
            emit_ffn(nc, es, sm, xin, f_aps["w1s"], f_aps["w3s"], p_aps["w2ps"],
                     f_aps["onesc"], xa)
        bar()
        gather(xa, g1out)
        g1v = g1out.rearrange("(r c p) t -> r c p t", r=2, p=128)

        def mam_src(dc, sl):
            r, lo = divmod(sl.start, T)
            return g1v[r, dc, :, lo:lo + (sl.stop - sl.start)]

        with contextlib.ExitStack() as es:
            emit_mamba(nc, es, sm, 2 * T, mam_src, **m_aps, xo=mxo)
        bar()
        with contextlib.ExitStack() as es:
            emit_halfsel(nc, es, sm,
                         lambda dc, tt: (mxo[dc, :, tt * TT:(tt + 1) * TT],
                                         mxo[dc, :, T + tt * TT:T + (tt + 1) * TT]),
                         xb, hs1)
        bar()
        with contextlib.ExitStack() as es:
            emit_ffn(nc, es, sm, xb, f_aps["w1s"], f_aps["w3s"], p_aps["w2ps"],
                     f_aps["onesc"], xc)
        bar()
        with contextlib.ExitStack() as es:
            emit_moe(nc, es, sm, xc, e_aps["rts"], e_aps["wgs"], e_aps["wus"],
                     e_aps["wds"], e_aps["sgs"], e_aps["sus"], e_aps["sds"],
                     e_aps["onesc"], e_aps["identf"], e_aps["selc"], xd)
        bar()
        gather(xd, g2out)
        g2v = g2out.rearrange("(r c p) t -> r c p t", r=2, p=128)
        with contextlib.ExitStack() as es:
            emit_halfsel(nc, es, sm,
                         lambda dc, tt: (g2v[0, dc, :, tt * TT:(tt + 1) * TT],
                                         g2v[1, dc, :, tt * TT:(tt + 1) * TT]),
                         axrem, hs2, flip=True)
        bar()
        with contextlib.ExitStack() as es:
            emit_attn(nc, es, sm, xd, axrem, a_aps["wdqs"], a_aps["wdkvs"],
                      a_aps["wuqr"], a_aps["wukv"], a_aps["wos"], a_aps["gq"],
                      a_aps["gkv"], a_aps["qA"], a_aps["qB"], a_aps["kA"],
                      a_aps["kB"], a_aps["dmaskc"], a_aps["droprc"],
                      a_aps["onesc"], xe, a_aps["qd"], a_aps["vd"])
        bar()
        with contextlib.ExitStack() as es:
            emit_moe(nc, es, sm, xe, e_aps["rts"], e_aps["wgs"], e_aps["wus"],
                     e_aps["wds"], e_aps["sgs"], e_aps["sus"], e_aps["sds"],
                     e_aps["onesc"], e_aps["identf"], e_aps["selc"], xfinal)
        bar()
    return nc


def run_fused(h, inputs):
    xs = shard_x(np.asarray(inputs["L0_attn_hidden_states"], np.float32))
    im = []
    for c in range(NCORES):
        half = c & 1
        hs = np.zeros((128, 2), np.float32)
        hs[:, 0] = 1.0 - half
        hs[:, 1] = half
        d = {"xin": xs[c], "hs1": hs, "hs2": hs,
             "p_w2t": h["w2t"], "p_h128": h["h128"], "p_h64": h["h64"],
             "f_w1s": h["w1s"], "f_w3s": h["w3s"], "f_onesc": h["onesc"],
             "m_wins": h["wins"], "m_wouts": h["wouts"], "m_convw": h["convw"],
             "m_convbc": h["convbc"], "m_nrmw": h["nrmw"], "m_dexp": h["dexp"],
             "m_abias": h["abias"], "m_identb": h["identb"],
             "m_identf": h["identf"], "m_onesc": h["onesc"],
             "m_onesf": h["onesf"], "m_maskc": h["maskc"],
             "e_rts": h["rts"], "e_wgs": h["wgs"], "e_wus": h["wus"],
             "e_wds": h["wds"], "e_sgs": h["sgs"], "e_sus": h["sus"],
             "e_sds": h["sds"], "e_onesc": h["onesc"], "e_identf": h["identf"],
             "e_selc": h["selc"],
             "a_wdqs": h["wdqs"], "a_wdkvs": h["wdkvs"], "a_wuqr": h["wuqr"],
             "a_wukv": h["wukv"], "a_wos": h["wos"], "a_gq": h["gq"],
             "a_gkv": h["gkv"], "a_qA": h[f"qA{half}"], "a_qB": h[f"qB{half}"],
             "a_kA": h[f"kA{half}"], "a_kB": h[f"kB{half}"],
             "a_dmaskc": h["dmaskc"], "a_droprc": h[f"drop{half}"],
             "a_onesc": h["onesc"]}
        im.append(d)
    res = _run("fused", build_fused, im)
    return unshard_x([res.results[c]["xfinal"] for c in range(NCORES)])


def kernel(**inputs):
    h = _prep_host(inputs)
    return run_fused(h, inputs)


def kernel_unfused(**inputs):
    return _kernel_unfused_body(**inputs)
